# version 10
# speedup vs baseline: 1.1947x; 1.1947x over previous
import numpy as np
import concourse.bass as bass
import concourse.mybir as mybir
from concourse.bass import IndirectOffsetOnAxis
from concourse.bass_utils import run_bass_kernel_spmd

F32 = mybir.dt.float32
BF16 = mybir.dt.bfloat16
I32 = mybir.dt.int32
U32 = mybir.dt.uint32
AF = mybir.ActivationFunctionType
ALU = mybir.AluOpType
AX = mybir.AxisListType

MM_BF16 = True

B, N, D = 4, 8192, 1024
NCORES = 8
TPC = (B * N) // NCORES
HEADS, DH = 8, 64
DI = HEADS * DH
SCALE = DH ** -0.5
PI = float(np.pi)



class Seq:

    def __init__(self, nc):
        self.nc = nc
        self.steps = []

    def step(self, eng, fn, n_dma=0, after=None, aux_after=None, aux=False):
        sid = len(self.steps)
        self.steps.append((eng, fn, n_dma, after, aux_after, aux))
        return sid

    def run(self, block, sem, entry=None, aux_sem=None):
        nst = len(self.steps)
        cum = [0] * (nst + 1)
        cumx = [0] * (nst + 1)
        for i, (_, _, n_dma, _, _, is_aux) in enumerate(self.steps):
            inc = 16 * n_dma if n_dma else 1
            cum[i + 1] = cum[i] + (0 if is_aux else inc)
            cumx[i + 1] = cumx[i] + (inc if is_aux else 0)
        acc = cum[nst]
        waits = [0] * nst
        waitsx = [0] * nst
        prev_on = {}
        for i, (eng, fn, n_dma, after, aux_after, is_aux) in \
                enumerate(self.steps):
            assert after is None, "after= on main sem is unsound; use aux_after"
            w = cum[i]
            if aux_after is None:
                wx = cumx[i]
            else:
                xdeps = aux_after if isinstance(aux_after, (list, tuple)) \
                    else [aux_after]
                wx = max((cumx[d + 1] for d in xdeps), default=0)
            waits[i] = w
            waitsx[i] = wx
            prev_on[eng] = i
        assert cumx[nst] == 0 or aux_sem is not None
        by_eng = {}
        for i, (eng, fn, n_dma, _, _, is_aux) in enumerate(self.steps):
            by_eng.setdefault(eng, []).append((i, fn, n_dma, is_aux))

        def make_body(items):
            def body(e):
                if entry is not None:
                    e.wait_ge(entry[0], entry[1])
                for (i, fn, n_dma, is_aux) in items:
                    if waits[i] > 0:
                        e.wait_ge(sem, waits[i])
                    if waitsx[i] > 0:
                        e.wait_ge(aux_sem, waitsx[i])
                    ret = fn(e)
                    tgt = aux_sem if is_aux else sem
                    if n_dma:
                        insts = ret if isinstance(ret, (list, tuple)) else [ret]
                        assert len(insts) == n_dma, f"step {i}: {len(insts)} != {n_dma}"
                        for ins in insts:
                            ins.then_inc(tgt, 16)
                    else:
                        assert ret is not None, f"step {i} returned None"
                        ret.then_inc(tgt, 1)
            return body

        for eng, items in by_eng.items():
            getattr(block, eng)(make_body(items))
        assert acc < 2 ** 30
        return acc



def build_stageA():
    MMDT = BF16 if MM_BF16 else F32
    nc = bass.Bass("TRN2", target_bir_lowering=False)
    x_in = nc.dram_tensor("x", [TPC, D], F32, kind="ExternalInput").ap()
    w_in = {}
    for nm in ("wq", "wk", "wv", "ware", "waim"):
        w_in[nm] = nc.dram_tensor(nm, [D, DI], F32, kind="ExternalInput").ap()
    bq = nc.dram_tensor("bq", [128, 4], F32, kind="ExternalInput").ap()
    bk = nc.dram_tensor("bk", [128, 4], F32, kind="ExternalInput").ap()
    bare = nc.dram_tensor("bare", [128, 4], F32, kind="ExternalInput").ap()
    baim = nc.dram_tensor("baim", [128, 4], F32, kind="ExternalInput").ap()
    bv = nc.dram_tensor("bv", [1, DI], F32, kind="ExternalInput").ap()
    gam = nc.dram_tensor("gam", [128, 8], F32, kind="ExternalInput").ap()
    ident_d = nc.dram_tensor("ident_d", [128, 128], F32, kind="ExternalInput").ap()

    qs_o = nc.dram_tensor("qs_fm", [DI, TPC], F32, kind="ExternalOutput").ap()
    k_o = nc.dram_tensor("k_fm", [DI, TPC], F32, kind="ExternalOutput").ap()
    v_o = nc.dram_tensor("v_tm", [TPC, DI], F32, kind="ExternalOutput").ap()
    ls_o = nc.dram_tensor("lsig_fm", [DI, TPC], F32, kind="ExternalOutput").ap()
    th_o = nc.dram_tensor("th_fm", [DI, TPC], F32, kind="ExternalOutput").ap()

    G = 512
    NG = TPC // G
    nsub = G // 128

    from contextlib import ExitStack
    with ExitStack() as ctx:
        E = ctx.enter_context
        wf = E(nc.sbuf_tensor("wf", [128, 8 * DI], F32))
        ws_q = E(nc.sbuf_tensor("ws_q", [128, 8 * DI], MMDT))
        ws_k = E(nc.sbuf_tensor("ws_k", [128, 8 * DI], MMDT))
        ws_v = E(nc.sbuf_tensor("ws_v", [128, 8 * DI], MMDT))
        ws_re = E(nc.sbuf_tensor("ws_re", [128, 8 * DI], F32))
        ws_im = E(nc.sbuf_tensor("ws_im", [128, 8 * DI], F32))
        bq_s = E(nc.sbuf_tensor("bq_s", [128, 4], F32))
        bk_s = E(nc.sbuf_tensor("bk_s", [128, 4], F32))
        bre_s = E(nc.sbuf_tensor("bre_s", [128, 4], F32))
        bim_s = E(nc.sbuf_tensor("bim_s", [128, 4], F32))
        gam_s = E(nc.sbuf_tensor("gam_s", [128, 8], F32))
        bvrow = E(nc.sbuf_tensor("bvrow", [1, DI], F32))
        bv_b = E(nc.sbuf_tensor("bv_b", [128, DI], F32))
        ones1 = E(nc.sbuf_tensor("ones1", [1, 128], F32))
        onescol = E(nc.sbuf_tensor("onescol", [128, 1], F32))
        ident = E(nc.sbuf_tensor("ident", [128, 128], F32))
        xtm = E(nc.sbuf_tensor("xtm", [128, nsub * D], F32))
        xfm = E(nc.sbuf_tensor("xfm", [128, 8 * G], F32))
        xnm = E(nc.sbuf_tensor("xnm", [128, 8 * G], MMDT))
        xnf = E(nc.sbuf_tensor("xnf", [128, 8 * G], F32))
        x2t = E(nc.sbuf_tensor("x2t", [128, G], F32))
        srow = E(nc.sbuf_tensor("srow", [1, 3 * G], F32))
        oq = E(nc.sbuf_tensor("oq", [128, G], F32))
        ok_ = E(nc.sbuf_tensor("ok_", [128, G], F32))
        ov = E(nc.sbuf_tensor("ov", [128, DI], F32))
        re_c = E(nc.sbuf_tensor("re_c", [128, G], F32))
        im_c = E(nc.sbuf_tensor("im_c", [128, G], F32))
        t1 = E(nc.sbuf_tensor("t1", [128, G], F32))
        t2 = E(nc.sbuf_tensor("t2", [128, G], F32))
        t3 = E(nc.sbuf_tensor("t3", [128, G], F32))
        lsg = E(nc.sbuf_tensor("lsg", [128, G], F32))
        tht = E(nc.sbuf_tensor("tht", [128, G], F32))
        pst = E(nc.psum_tensor("pst", [128, G], F32))
        ps1 = E(nc.psum_tensor("ps1", [128, G], F32))
        ps2 = E(nc.psum_tensor("ps2", [128, G], F32))
        psv = E(nc.psum_tensor("psv", [128, DI], F32))
        psrow = E(nc.psum_tensor("psrow", [1, G], F32))
        psb = E(nc.psum_tensor("psb", [128, G], F32))
        gs = E(nc.semaphore("gs"))
        block = E(nc.Block())

        seq = Seq(nc)

        seq.step("vector", lambda v: v.memset(ones1[:], 1.0))
        seq.step("vector", lambda v: v.memset(onescol[:], 1.0))
        seq.step("sync", lambda s: [
            s.dma_start(out=ident[:], in_=ident_d),
            s.dma_start(out=bq_s[:], in_=bq),
            s.dma_start(out=bk_s[:], in_=bk),
            s.dma_start(out=bre_s[:], in_=bare),
            s.dma_start(out=bim_s[:], in_=baim),
            s.dma_start(out=gam_s[:], in_=gam),
            s.dma_start(out=bvrow[:], in_=bv),
        ], n_dma=7)
        seq.step("tensor", lambda t: t.matmul(psv[:, 0:DI], ones1[:], bvrow[:],
                                              start=True, stop=True))
        seq.step("vector", lambda v: v.tensor_copy(out=bv_b[:], in_=psv[:, 0:DI]))

        for nm, ws in (("wq", ws_q), ("wk", ws_k), ("wv", ws_v),
                       ("ware", ws_re), ("waim", ws_im)):
            src = w_in[nm]
            seq.step("sync", lambda s, src=src: [
                s.dma_start(out=wf[:, c * DI:(c + 1) * DI],
                            in_=src[c * 128:(c + 1) * 128, :])
                for c in range(8)], n_dma=8)
            for c in range(8):
                seq.step("vector", lambda v, c=c, ws=ws: v.tensor_scalar(
                    out=ws[:, c * DI:(c + 1) * DI], in0=wf[:, c * DI:(c + 1) * DI],
                    scalar1=gam_s[:, c:c + 1], scalar2=None, op0=ALU.mult))

        for gi in range(NG):
            def ldx(s, gi=gi):
                return [s.dma_start(
                    out=xtm[:, sub * D:(sub + 1) * D],
                    in_=x_in[gi * G + sub * 128: gi * G + (sub + 1) * 128, :])
                    for sub in range(nsub)]
            seq.step("sync", ldx, n_dma=nsub)

            for c in range(8):
                def tr(t, c=c):
                    last = None
                    for sub in range(nsub):
                        last = t.transpose(
                            pst[:, sub * 128:(sub + 1) * 128],
                            xtm[:, sub * D + c * 128: sub * D + (c + 1) * 128],
                            ident[:])
                    return last
                seq.step("tensor", tr)
                seq.step("vector", lambda v, c=c: v.tensor_copy(
                    out=xfm[:, c * G:(c + 1) * G], in_=pst[:]))
                seq.step("scalar", lambda s, c=c: s.activation(
                    out=x2t[:], in_=pst[:], func=AF.Square))
                seq.step("tensor", lambda t, c=c: t.matmul(
                    psrow[:], onescol[:], x2t[:], start=(c == 0), stop=(c == 7)))
            seq.step("scalar", lambda s: s.activation(
                out=srow[0:1, 0:G], in_=psrow[:], func=AF.Sqrt))
            seq.step("vector", lambda v: v.tensor_scalar(
                out=srow[0:1, G:2 * G], in0=srow[0:1, 0:G],
                scalar1=1e-12, scalar2=None, op0=ALU.max))
            seq.step("vector", lambda v: v.reciprocal(
                out=srow[0:1, 2 * G:3 * G], in_=srow[0:1, G:2 * G]))
            seq.step("vector", lambda v: v.tensor_scalar(
                out=srow[0:1, 2 * G:3 * G], in0=srow[0:1, 2 * G:3 * G],
                scalar1=float(np.sqrt(D)), scalar2=None, op0=ALU.mult))
            seq.step("tensor", lambda t: t.matmul(
                psb[:], ones1[:], srow[0:1, 2 * G:3 * G], start=True, stop=True))
            for c in range(8):
                seq.step("vector", lambda v, c=c: v.tensor_tensor(
                    out=xnf[:, c * G:(c + 1) * G], in0=xfm[:, c * G:(c + 1) * G],
                    in1=psb[:], op=ALU.mult))
            for c in range(8):
                seq.step("vector", lambda v, c=c: v.tensor_copy(
                    out=xnm[:, c * G:(c + 1) * G], in_=xnf[:, c * G:(c + 1) * G]))

            for rc in range(4):
                def mmq(t, rc=rc):
                    last = None
                    for c in range(8):
                        last = t.matmul(
                            ps1[:], ws_q[:, c * DI + rc * 128: c * DI + rc * 128 + 128],
                            xnm[:, c * G:(c + 1) * G], start=(c == 0), stop=(c == 7))
                    return last
                seq.step("tensor", mmq)
                seq.step("vector", lambda v, rc=rc: v.tensor_scalar(
                    out=oq[:], in0=ps1[:], scalar1=bq_s[:, rc:rc + 1],
                    scalar2=SCALE, op0=ALU.add, op1=ALU.mult))
                seq.step("sync", lambda s, rc=rc, gi=gi: s.dma_start(
                    out=qs_o[rc * 128:(rc + 1) * 128, gi * G:(gi + 1) * G],
                    in_=oq[:]), n_dma=1)

                def mmk(t, rc=rc):
                    last = None
                    for c in range(8):
                        last = t.matmul(
                            ps2[:], ws_k[:, c * DI + rc * 128: c * DI + rc * 128 + 128],
                            xnm[:, c * G:(c + 1) * G], start=(c == 0), stop=(c == 7))
                    return last
                seq.step("tensor", mmk)
                seq.step("vector", lambda v, rc=rc: v.tensor_scalar(
                    out=ok_[:], in0=ps2[:], scalar1=bk_s[:, rc:rc + 1],
                    scalar2=None, op0=ALU.add))
                seq.step("sync", lambda s, rc=rc, gi=gi: s.dma_start(
                    out=k_o[rc * 128:(rc + 1) * 128, gi * G:(gi + 1) * G],
                    in_=ok_[:]), n_dma=1)

            for sub in range(nsub):
                def mmv(t, sub=sub):
                    last = None
                    for c in range(8):
                        last = t.matmul(
                            psv[:], xnm[:, c * G + sub * 128: c * G + (sub + 1) * 128],
                            ws_v[:, c * DI:(c + 1) * DI], start=(c == 0), stop=(c == 7))
                    return last
                seq.step("tensor", mmv)
                seq.step("vector", lambda v: v.tensor_tensor(
                    out=ov[:], in0=psv[:], in1=bv_b[:], op=ALU.add))
                seq.step("sync", lambda s, sub=sub, gi=gi: s.dma_start(
                    out=v_o[gi * G + sub * 128: gi * G + (sub + 1) * 128, :],
                    in_=ov[:]), n_dma=1)

            for rc in range(4):
                def mmre(t, rc=rc):
                    last = None
                    for c in range(8):
                        last = t.matmul(
                            ps1[:], ws_re[:, c * DI + rc * 128: c * DI + rc * 128 + 128],
                            xnf[:, c * G:(c + 1) * G], start=(c == 0), stop=(c == 7))
                    return last
                seq.step("tensor", mmre)
                seq.step("vector", lambda v, rc=rc: v.tensor_scalar(
                    out=re_c[:], in0=ps1[:], scalar1=bre_s[:, rc:rc + 1],
                    scalar2=None, op0=ALU.add))

                def mmim(t, rc=rc):
                    last = None
                    for c in range(8):
                        last = t.matmul(
                            ps2[:], ws_im[:, c * DI + rc * 128: c * DI + rc * 128 + 128],
                            xnf[:, c * G:(c + 1) * G], start=(c == 0), stop=(c == 7))
                    return last
                seq.step("tensor", mmim)
                seq.step("vector", lambda v, rc=rc: v.tensor_scalar(
                    out=im_c[:], in0=ps2[:], scalar1=bim_s[:, rc:rc + 1],
                    scalar2=None, op0=ALU.add))

                seq.step("scalar", lambda s: s.activation(out=t1[:], in_=re_c[:],
                                                          func=AF.Square))
                seq.step("scalar", lambda s: s.activation(out=t2[:], in_=im_c[:],
                                                          func=AF.Square))
                seq.step("vector", lambda v: v.tensor_add(out=t1[:], in0=t1[:],
                                                          in1=t2[:]))
                seq.step("scalar", lambda s: s.activation(out=t2[:], in_=t1[:],
                                                          func=AF.Sqrt))
                seq.step("scalar", lambda s: s.activation(out=t3[:], in_=t2[:],
                                                          func=AF.Sigmoid))
                seq.step("scalar", lambda s: s.activation(out=lsg[:], in_=t3[:],
                                                          func=AF.Ln))
                seq.step("sync", lambda s, rc=rc, gi=gi: s.dma_start(
                    out=ls_o[rc * 128:(rc + 1) * 128, gi * G:(gi + 1) * G],
                    in_=lsg[:]), n_dma=1)
                seq.step("vector", lambda v: v.reciprocal(out=t1[:], in_=re_c[:]))
                seq.step("vector", lambda v: v.tensor_tensor(
                    out=t1[:], in0=t1[:], in1=im_c[:], op=ALU.mult))
                seq.step("scalar", lambda s: s.activation(out=t2[:], in_=t1[:],
                                                          func=AF.Arctan))
                seq.step("vector", lambda v: v.tensor_scalar(
                    out=t1[:], in0=re_c[:], scalar1=0.0, scalar2=None,
                    op0=ALU.is_lt))
                seq.step("vector", lambda v: v.tensor_scalar(
                    out=t3[:], in0=im_c[:], scalar1=0.0, scalar2=None,
                    op0=ALU.is_lt))
                seq.step("vector", lambda v: v.tensor_scalar(
                    out=t3[:], in0=t3[:], scalar1=-2.0, scalar2=1.0,
                    op0=ALU.mult, op1=ALU.add))
                seq.step("vector", lambda v: v.tensor_tensor(
                    out=t3[:], in0=t3[:], in1=t1[:], op=ALU.mult))
                seq.step("vector", lambda v: v.scalar_tensor_tensor(
                    out=tht[:], in0=t3[:], scalar=PI, in1=t2[:],
                    op0=ALU.mult, op1=ALU.add))
                seq.step("sync", lambda s, rc=rc, gi=gi: s.dma_start(
                    out=th_o[rc * 128:(rc + 1) * 128, gi * G:(gi + 1) * G],
                    in_=tht[:]), n_dma=1)

        total = seq.run(block, gs)

        @block.sync
        def _(s):
            s.wait_ge(gs, total)
    return nc


def stageA_inmaps(x, W_qkv, b_qkv, W_a, b_a, gamma):
    xt = np.ascontiguousarray(x.reshape(B * N, D))
    com = dict(
        wq=np.ascontiguousarray(W_qkv[:, 0:DI]),
        wk=np.ascontiguousarray(W_qkv[:, DI:2 * DI]),
        wv=np.ascontiguousarray(W_qkv[:, 2 * DI:3 * DI]),
        ware=np.ascontiguousarray(W_a[:, 0::2]),
        waim=np.ascontiguousarray(W_a[:, 1::2]),
        bq=np.ascontiguousarray(b_qkv[0:DI].reshape(4, 128).T),
        bk=np.ascontiguousarray(b_qkv[DI:2 * DI].reshape(4, 128).T),
        bv=np.ascontiguousarray(b_qkv[2 * DI:3 * DI].reshape(1, DI)),
        bare=np.ascontiguousarray(b_a[0::2].reshape(4, 128).T),
        baim=np.ascontiguousarray(b_a[1::2].reshape(4, 128).T),
        gam=np.ascontiguousarray(gamma.reshape(8, 128).T),
        ident_d=np.eye(128, dtype=np.float32),
    )
    maps = []
    for c in range(NCORES):
        m = dict(com)
        m["x"] = np.ascontiguousarray(xt[c * TPC:(c + 1) * TPC])
        maps.append(m)
    return maps



C1RR = 6.28125
C2RR = 1.9353071228e-03
C3RR = float(2 * np.pi - np.float64(np.float32(C1RR)) - np.float64(np.float32(C2RR)))
MAGIC = 1.5 * 2.0 ** 23
INV2PI = float(1.0 / (2 * np.pi))


def build_stageB():
    MMDT = BF16 if MM_BF16 else F32
    nc = bass.Bass("TRN2", target_bir_lowering=False)
    qs_i = nc.dram_tensor("qs_u", [4, 64, N], F32, kind="ExternalInput").ap()
    k_i = nc.dram_tensor("k_u", [4, 64, N], F32, kind="ExternalInput").ap()
    ls_i = nc.dram_tensor("ls_u", [4, 64, N], F32, kind="ExternalInput").ap()
    th_i = nc.dram_tensor("th_u", [4, 64, N], F32, kind="ExternalInput").ap()
    v_i = nc.dram_tensor("v_u", [4, N, 64], F32, kind="ExternalInput").ap()
    masku_d = nc.dram_tensor("masku_d", [128, 128], F32, kind="ExternalInput").ap()
    ident_d = nc.dram_tensor("ident_d", [128, 128], F32, kind="ExternalInput").ap()
    at_o = nc.dram_tensor("attn_u", [4, N, 64], F32, kind="ExternalOutput").ap()

    NCH = N // 128

    from contextlib import ExitStack
    with ExitStack() as ctx:
        E = ctx.enter_context
        B1 = E(nc.sbuf_tensor("B1", [128, N], F32))
        B2 = E(nc.sbuf_tensor("B2", [128, N], F32))
        B3 = E(nc.sbuf_tensor("B3", [128, N], F32))
        B4 = E(nc.sbuf_tensor("B4", [128, N], F32))
        qtm = E(nc.sbuf_tensor("qtm", [128, N], MMDT))
        ktm = E(nc.sbuf_tensor("ktm", [128, N], MMDT))
        vm = E(nc.sbuf_tensor("vm", [128, NCH * 64], MMDT))
        zrow = E(nc.sbuf_tensor("zrow", [128, 1], F32))
        masku = E(nc.sbuf_tensor("masku", [128, 128], F32))
        identf = E(nc.sbuf_tensor("identf", [128, 128], F32))
        identm = E(nc.sbuf_tensor("identm", [128, 128], MMDT))
        mscore = E(nc.sbuf_tensor("mscore", [128, 128], MMDT))
        ktc = E(nc.sbuf_tensor("ktc", [128, 64], MMDT))
        S_sb = E(nc.sbuf_tensor("S_sb", [128, 64], MMDT))
        outsb = E(nc.sbuf_tensor("outsb", [128, 64], F32))
        ps_s = E(nc.psum_tensor("ps_s", [128, 128], F32))
        out_ps = E(nc.psum_tensor("out_ps", [128, 64], F32))
        S_ps = E(nc.psum_tensor("S_ps", [64, 64], F32))
        kt_ps = E(nc.psum_tensor("kt_ps", [128, 64], MMDT))
        gs = E(nc.semaphore("gs"))
        block = E(nc.Block())

        seq = Seq(nc)
        seq.step("vector", lambda v: v.memset(zrow[:], 0.0))
        seq.step("sync", lambda s: [s.dma_start(out=masku[:], in_=masku_d),
                                    s.dma_start(out=identf[:], in_=ident_d)],
                 n_dma=2)
        seq.step("vector", lambda v: v.tensor_copy(out=identm[:], in_=identf[:]))

        for pair in range(2):
            u0, u1 = 2 * pair, 2 * pair + 1

            def ld2(s, src, dst, u0=u0, u1=u1):
                return [s.dma_start(out=dst[0:64, :], in_=src[u0]),
                        s.dma_start(out=dst[64:128, :], in_=src[u1])]
            seq.step("sync", lambda s, f=ls_i, ld2=ld2: ld2(s, f, B1), n_dma=2)
            seq.step("vector", lambda v: v.tensor_tensor_scan(
                out=B2[:], data0=B1[:], data1=zrow[:].to_broadcast([128, N]),
                initial=0.0, op0=ALU.add, op1=ALU.add))
            seq.step("scalar", lambda s: s.activation(out=B1[:], in_=B2[:],
                                                      func=AF.Exp))
            seq.step("sync", lambda s, f=th_i, ld2=ld2: ld2(s, f, B2), n_dma=2)
            seq.step("vector", lambda v: v.tensor_tensor_scan(
                out=B3[:], data0=B2[:], data1=zrow[:].to_broadcast([128, N]),
                initial=0.0, op0=ALU.add, op1=ALU.add))
            seq.step("vector", lambda v: v.tensor_scalar(
                out=B2[:], in0=B3[:], scalar1=PI / 2, scalar2=None, op0=ALU.add))
            seq.step("vector", lambda v: v.tensor_scalar(
                out=B4[:], in0=B2[:], scalar1=INV2PI, scalar2=MAGIC,
                op0=ALU.mult, op1=ALU.add))
            seq.step("vector", lambda v: v.tensor_scalar(
                out=B4[:], in0=B4[:], scalar1=MAGIC, scalar2=None, op0=ALU.subtract))
            for cc in (C1RR, C2RR, C3RR):
                seq.step("vector", lambda v, cc=cc: v.scalar_tensor_tensor(
                    out=B2[:], in0=B4[:], scalar=-cc, in1=B2[:],
                    op0=ALU.mult, op1=ALU.add))
            seq.step("scalar", lambda s: s.activation(out=B2[:], in_=B2[:],
                                                      func=AF.Sin))
            seq.step("vector", lambda v: v.tensor_tensor(
                out=B2[:], in0=B1[:], in1=B2[:], op=ALU.mult))
            seq.step("vector", lambda v: v.tensor_scalar(
                out=B4[:], in0=B2[:], scalar1=1e-10, scalar2=None, op0=ALU.max))
            seq.step("vector", lambda v: v.reciprocal(out=B4[:], in_=B4[:]))
            seq.step("sync", lambda s, f=qs_i, ld2=ld2: ld2(s, f, B1), n_dma=2)
            seq.step("sync", lambda s, f=k_i, ld2=ld2: ld2(s, f, B3), n_dma=2)
            seq.step("vector", lambda v: v.tensor_tensor(
                out=qtm[:], in0=B1[:], in1=B2[:], op=ALU.mult))
            seq.step("vector", lambda v: v.tensor_tensor(
                out=ktm[:], in0=B3[:], in1=B4[:], op=ALU.mult))

            for (u, pb) in ((u0, 0), (u1, 64)):
                def ldv_all(g, u=u):
                    return [g.dma_start(out=vm[:, c * 64:(c + 1) * 64],
                                        in_=v_i[u, c * 128:(c + 1) * 128, :])
                            for c in range(NCH)]
                seq.step("gpsimd", ldv_all, n_dma=NCH)
                for c in range(NCH):
                    qc = qtm[pb:pb + 64, c * 128:(c + 1) * 128]
                    kc = ktm[pb:pb + 64, c * 128:(c + 1) * 128]
                    seq.step("tensor", lambda t, kc=kc, qc=qc: t.matmul(
                        ps_s[:], kc, qc, start=True, stop=True))
                    seq.step("vector", lambda v: v.tensor_tensor(
                        out=mscore[:], in0=ps_s[:], in1=masku[:], op=ALU.mult))

                    def mmout(t, c=c, qc=qc, pb=pb):
                        i1 = t.matmul(out_ps[:], mscore[:],
                                      vm[:, c * 64:(c + 1) * 64],
                                      start=True, stop=(c == 0))
                        if c > 0:
                            return t.matmul(out_ps[:], qc, S_sb[pb:pb + 64, :],
                                            start=False, stop=True)
                        return i1
                    seq.step("tensor", mmout)
                    seq.step("vector", lambda v: v.tensor_copy(out=outsb[:],
                                                               in_=out_ps[:]))
                    seq.step("sync", lambda s, u=u, c=c: s.dma_start(
                        out=at_o[u, c * 128:(c + 1) * 128, :], in_=outsb[:]),
                        n_dma=1)
                    seq.step("tensor", lambda t, kc=kc, pb=pb: t.transpose(
                        kt_ps[:, 0:64], kc, identm[pb:pb + 64, pb:pb + 64]))
                    seq.step("vector", lambda v: v.tensor_copy(out=ktc[:],
                                                               in_=kt_ps[:, 0:64]))
                    seq.step("tensor", lambda t, c=c: t.matmul(
                        S_ps[:], ktc[:], vm[:, c * 64:(c + 1) * 64],
                        start=(c == 0), stop=(c == NCH - 1)))
                    seq.step("vector", lambda v, pb=pb: v.tensor_copy(
                        out=S_sb[pb:pb + 64, :], in_=S_ps[:]))

        total = seq.run(block, gs)

        @block.sync
        def _(s):
            s.wait_ge(gs, total)
    return nc


def stageB_inmaps(qs_fm, k_fm, lsig_fm, th_fm, v_tm):
    masku = (np.arange(128)[:, None] <= np.arange(128)[None, :]).astype(np.float32)
    ident = np.eye(128, dtype=np.float32)
    maps = []
    units = [(b, h) for b in range(B) for h in range(HEADS)]
    for c in range(NCORES):
        us = units[c * 4:(c + 1) * 4]
        m = dict(masku_d=masku, ident_d=ident)
        m["qs_u"] = np.ascontiguousarray(
            np.stack([qs_fm[b, h * 64:(h + 1) * 64] for (b, h) in us]))
        m["k_u"] = np.ascontiguousarray(
            np.stack([k_fm[b, h * 64:(h + 1) * 64] for (b, h) in us]))
        m["ls_u"] = np.ascontiguousarray(
            np.stack([lsig_fm[b, h * 64:(h + 1) * 64] for (b, h) in us]))
        m["th_u"] = np.ascontiguousarray(
            np.stack([th_fm[b, h * 64:(h + 1) * 64] for (b, h) in us]))
        m["v_u"] = np.ascontiguousarray(
            np.stack([v_tm[b, :, h * 64:(h + 1) * 64] for (b, h) in us]))
        maps.append(m)
    return maps



LNEPS = 1e-5
QEPS = 1e-5


def build_stageC(ng_limit=None, skip_quant=False):
    MMDT = BF16 if MM_BF16 else F32
    nc = bass.Bass("TRN2", target_bir_lowering=False)
    x_in = nc.dram_tensor("x", [TPC, D], F32, kind="ExternalInput").ap()
    at_in = nc.dram_tensor("attn", [TPC, DI], F32, kind="ExternalInput").ap()
    wout_d = nc.dram_tensor("wout", [DI, D], F32, kind="ExternalInput").ap()
    bout_d = nc.dram_tensor("bout", [128, 8], F32, kind="ExternalInput").ap()
    w1t_d = nc.dram_tensor("w1t", [D, 4096], F32, kind="ExternalInput").ap()
    b1_d = nc.dram_tensor("b1", [128, 32], F32, kind="ExternalInput").ap()
    w2t_d = nc.dram_tensor("w2t", [4096, D], F32, kind="ExternalInput").ap()
    b2_d = nc.dram_tensor("b2", [128, 8], F32, kind="ExternalInput").ap()
    ident_d = nc.dram_tensor("ident_d", [128, 128], F32, kind="ExternalInput").ap()
    x2_o = nc.dram_tensor("x2_fm", [D, TPC], F32, kind="ExternalOutput").ap()

    G = 128
    NG = ng_limit or (TPC // G)
    FC = 32

    from contextlib import ExitStack
    with ExitStack() as ctx:
        E = ctx.enter_context
        wf = E(nc.sbuf_tensor("wf", [128, 1024], F32))
        w1q = E(nc.sbuf_tensor("w1q", [128, 8 * 4096], MMDT))
        w2q = E(nc.sbuf_tensor("w2q", [128, 32 * 1024], MMDT))
        wout_s = E(nc.sbuf_tensor("wout_s", [128, 4 * 1024], MMDT))
        bout_s = E(nc.sbuf_tensor("bout_s", [128, 8], F32))
        b1_s = E(nc.sbuf_tensor("b1_s", [128, 32], F32))
        b2_s = E(nc.sbuf_tensor("b2_s", [128, 8], F32))
        ident = E(nc.sbuf_tensor("ident", [128, 128], F32))
        ones1 = E(nc.sbuf_tensor("ones1", [1, 128], F32))
        onescol = E(nc.sbuf_tensor("onescol", [128, 1], F32))
        wabs = E(nc.sbuf_tensor("wabs", [128, 72], F32))
        thr_b = E(nc.sbuf_tensor("thr_b", [128, 1], F32))
        qtmp = E(nc.sbuf_tensor("qtmp", [128, 1024], F32))
        qsgn = E(nc.sbuf_tensor("qsgn", [128, 1024], F32))
        xtm = E(nc.sbuf_tensor("xtm", [128, D], F32))
        attm = E(nc.sbuf_tensor("attm", [128, DI], F32))
        atfm = E(nc.sbuf_tensor("atfm", [128, 4 * G], MMDT))
        xfm = E(nc.sbuf_tensor("xfm", [128, 8 * G], F32))
        z1 = E(nc.sbuf_tensor("z1", [128, 8 * G], F32))
        x1m = E(nc.sbuf_tensor("x1m", [128, 8 * G], MMDT))
        hm = E(nc.sbuf_tensor("hm", [128, FC * G], MMDT))
        ysb = E(nc.sbuf_tensor("ysb", [128, 8 * G], F32))
        sqt = E(nc.sbuf_tensor("sqt", [128, G], F32))
        rowt = E(nc.sbuf_tensor("rowt", [1, 6 * G], F32))
        mub = E(nc.sbuf_tensor("mub", [128, G], F32))
        rsb = E(nc.sbuf_tensor("rsb", [128, G], F32))
        pst = E(nc.psum_tensor("pst", [128, G], F32))
        pso = E(nc.psum_tensor("pso", [128, G], F32))
        psh = E(nc.psum_tensor("psh", [128, G], F32))
        psrow = E(nc.psum_tensor("psrow", [1, G], F32))
        psb = E(nc.psum_tensor("psb", [128, G], F32))
        psq = E(nc.psum_tensor("psq", [1, 1], F32))
        gs = E(nc.semaphore("gs"))
        block = E(nc.Block())

        seq = Seq(nc)
        seq.step("vector", lambda v: v.memset(ones1[:], 1.0))
        seq.step("vector", lambda v: v.memset(onescol[:], 1.0))
        seq.step("sync", lambda s: [
            s.dma_start(out=ident[:], in_=ident_d),
            s.dma_start(out=bout_s[:], in_=bout_d),
            s.dma_start(out=b1_s[:], in_=b1_d),
            s.dma_start(out=b2_s[:], in_=b2_d)], n_dma=4)
        seq.step("gpsimd", lambda g: [
            g.dma_start(out=wout_s[:, c * 1024:(c + 1) * 1024],
                        in_=wout_d[c * 128:(c + 1) * 128, :]) for c in range(4)],
            n_dma=4)

        def w1_chunk(i):
            c, q = divmod(i, 4)
            return w1t_d[c * 128:(c + 1) * 128, q * 1024:(q + 1) * 1024]

        def w2_chunk(i):
            return w2t_d[i * 128:(i + 1) * 128, :]

        for i in range(32):
            seq.step("sync", lambda s, i=i: s.dma_start(out=wf[:], in_=w1_chunk(i)),
                     n_dma=1)
            seq.step("vector", lambda v, i=i: v.tensor_reduce(
                out=wabs[:, i:i + 1], in_=wf[:], axis=AX.X, op=ALU.add,
                apply_absolute_value=True))
        for i in range(32):
            seq.step("sync", lambda s, i=i: s.dma_start(out=wf[:], in_=w2_chunk(i)),
                     n_dma=1)
            seq.step("vector", lambda v, i=i: v.tensor_reduce(
                out=wabs[:, 32 + i:33 + i], in_=wf[:], axis=AX.X, op=ALU.add,
                apply_absolute_value=True))
        seq.step("vector", lambda v: v.tensor_reduce(
            out=wabs[:, 64:65], in_=wabs[:, 0:32], axis=AX.X, op=ALU.add))
        seq.step("vector", lambda v: v.tensor_reduce(
            out=wabs[:, 65:66], in_=wabs[:, 32:64], axis=AX.X, op=ALU.add))
        for (src_col, dst_col) in ((64, 66), (65, 67)):
            seq.step("tensor", lambda t, src_col=src_col: t.matmul(
                psq[:], wabs[:, src_col:src_col + 1], onescol[:],
                start=True, stop=True))
            seq.step("vector", lambda v, dst_col=dst_col: v.tensor_scalar(
                out=wabs[0:1, dst_col:dst_col + 1], in0=psq[:],
                scalar1=1.0 / (4096.0 * 1024.0), scalar2=QEPS,
                op0=ALU.mult, op1=ALU.add))
            seq.step("vector", lambda v, dst_col=dst_col: v.tensor_scalar(
                out=wabs[0:1, dst_col:dst_col + 1],
                in0=wabs[0:1, dst_col:dst_col + 1], scalar1=0.5,
                scalar2=None, op0=ALU.mult))

        def quant_pass(chunk_fn, thr_col, dst):
            seq.step("tensor", lambda t, thr_col=thr_col: t.matmul(
                psb[:, 0:1], ones1[:], wabs[0:1, thr_col:thr_col + 1],
                start=True, stop=True))
            seq.step("vector", lambda v: v.tensor_copy(out=thr_b[:],
                                                       in_=psb[:, 0:1]))
            for i in range(32):
                seq.step("sync", lambda s, i=i, chunk_fn=chunk_fn: s.dma_start(
                    out=wf[:], in_=chunk_fn(i)), n_dma=1)
                seq.step("scalar", lambda sc: sc.sign(out=qsgn[:], in_=wf[:]))
                seq.step("scalar", lambda sc: sc.activation(out=qtmp[:], in_=wf[:],
                                                            func=AF.Abs))
                seq.step("vector", lambda v: v.tensor_scalar(
                    out=qtmp[:], in0=qtmp[:], scalar1=thr_b[:, 0:1], scalar2=None,
                    op0=ALU.is_gt))
                seq.step("vector", lambda v, i=i, dst=dst: v.tensor_tensor(
                    out=dst[:, i * 1024:(i + 1) * 1024],
                    in0=qsgn[:], in1=qtmp[:], op=ALU.mult))

        if not skip_quant:
            quant_pass(w1_chunk, 66, w1q)
            quant_pass(w2_chunk, 67, w2q)

        def ln_block(src_sb, dst_f32, dst_m):
            for c in range(8):
                seq.step("tensor", lambda t, c=c, src_sb=src_sb: t.matmul(
                    psrow[:], onescol[:], src_sb[:, c * G:(c + 1) * G],
                    start=(c == 0), stop=(c == 7)))
            seq.step("vector", lambda v: v.tensor_scalar(
                out=rowt[0:1, 0:G], in0=psrow[:], scalar1=1.0 / D, scalar2=None,
                op0=ALU.mult))
            for c in range(8):
                seq.step("scalar", lambda s, c=c, src_sb=src_sb: s.activation(
                    out=sqt[:], in_=src_sb[:, c * G:(c + 1) * G], func=AF.Square))
                seq.step("tensor", lambda t, c=c: t.matmul(
                    psrow[:], onescol[:], sqt[:], start=(c == 0), stop=(c == 7)))
            seq.step("vector", lambda v: v.tensor_scalar(
                out=rowt[0:1, G:2 * G], in0=psrow[:], scalar1=1.0 / D,
                scalar2=None, op0=ALU.mult))
            seq.step("scalar", lambda s: s.activation(
                out=rowt[0:1, 2 * G:3 * G], in_=rowt[0:1, 0:G], func=AF.Square))
            seq.step("vector", lambda v: v.tensor_sub(
                out=rowt[0:1, G:2 * G], in0=rowt[0:1, G:2 * G],
                in1=rowt[0:1, 2 * G:3 * G]))
            seq.step("vector", lambda v: v.tensor_scalar(
                out=rowt[0:1, G:2 * G], in0=rowt[0:1, G:2 * G], scalar1=LNEPS,
                scalar2=None, op0=ALU.add))
            seq.step("scalar", lambda s: s.activation(
                out=rowt[0:1, 3 * G:4 * G], in_=rowt[0:1, G:2 * G], func=AF.Sqrt))
            seq.step("vector", lambda v: v.reciprocal(
                out=rowt[0:1, 4 * G:5 * G], in_=rowt[0:1, 3 * G:4 * G]))
            seq.step("tensor", lambda t: t.matmul(
                psb[:], ones1[:], rowt[0:1, 0:G], start=True, stop=True))
            seq.step("vector", lambda v: v.tensor_copy(out=mub[:], in_=psb[:]))
            seq.step("tensor", lambda t: t.matmul(
                psb[:], ones1[:], rowt[0:1, 4 * G:5 * G], start=True, stop=True))
            seq.step("vector", lambda v: v.tensor_copy(out=rsb[:], in_=psb[:]))
            for c in range(8):
                seq.step("vector", lambda v, c=c, src_sb=src_sb, dst_f32=dst_f32:
                         v.tensor_sub(
                             out=dst_f32[:, c * G:(c + 1) * G],
                             in0=src_sb[:, c * G:(c + 1) * G], in1=mub[:]))
                seq.step("vector", lambda v, c=c, dst_f32=dst_f32: v.tensor_tensor(
                    out=dst_f32[:, c * G:(c + 1) * G],
                    in0=dst_f32[:, c * G:(c + 1) * G], in1=rsb[:], op=ALU.mult))
                if dst_m is not None:
                    seq.step("vector", lambda v, c=c, dst_f32=dst_f32, dst_m=dst_m:
                             v.tensor_copy(
                                 out=dst_m[:, c * G:(c + 1) * G],
                                 in_=dst_f32[:, c * G:(c + 1) * G]))

        for gi in range(NG):
            seq.step("sync", lambda s, gi=gi: [
                s.dma_start(out=xtm[:], in_=x_in[gi * G:(gi + 1) * G, :]),
                s.dma_start(out=attm[:], in_=at_in[gi * G:(gi + 1) * G, :])],
                n_dma=2)
            for c in range(4):
                seq.step("tensor", lambda t, c=c: t.transpose(
                    pst[:], attm[:, c * 128:(c + 1) * 128], ident[:]))
                seq.step("vector", lambda v, c=c: v.tensor_copy(
                    out=atfm[:, c * G:(c + 1) * G], in_=pst[:]))
            for c in range(8):
                seq.step("tensor", lambda t, c=c: t.transpose(
                    pst[:], xtm[:, c * 128:(c + 1) * 128], ident[:]))
                seq.step("vector", lambda v, c=c: v.tensor_copy(
                    out=xfm[:, c * G:(c + 1) * G], in_=pst[:]))
            for dc in range(8):
                def mmo(t, dc=dc):
                    last = None
                    for ec in range(4):
                        last = t.matmul(
                            pso[:],
                            wout_s[:, ec * 1024 + dc * 128: ec * 1024 + dc * 128 + 128],
                            atfm[:, ec * G:(ec + 1) * G],
                            start=(ec == 0), stop=(ec == 3))
                    return last
                seq.step("tensor", mmo)
                seq.step("vector", lambda v, dc=dc: v.tensor_scalar(
                    out=z1[:, dc * G:(dc + 1) * G], in0=pso[:],
                    scalar1=bout_s[:, dc:dc + 1], scalar2=None, op0=ALU.add))
                seq.step("vector", lambda v, dc=dc: v.tensor_tensor(
                    out=z1[:, dc * G:(dc + 1) * G], in0=z1[:, dc * G:(dc + 1) * G],
                    in1=xfm[:, dc * G:(dc + 1) * G], op=ALU.add))
            ln_block(z1, xfm, x1m)
            for fc in range(FC):
                def mmh(t, fc=fc):
                    last = None
                    for c in range(8):
                        last = t.matmul(
                            psh[:],
                            w1q[:, c * 4096 + fc * 128: c * 4096 + fc * 128 + 128],
                            x1m[:, c * G:(c + 1) * G], start=(c == 0), stop=(c == 7))
                    return last
                seq.step("tensor", mmh)
                seq.step("scalar", lambda s, fc=fc: s.activation(
                    out=hm[:, fc * G:(fc + 1) * G], in_=psh[:], func=AF.Relu,
                    bias=b1_s[:, fc:fc + 1]))
            for dc in range(8):
                def mmy(t, dc=dc):
                    last = None
                    for fc in range(FC):
                        last = t.matmul(
                            pso[:],
                            w2q[:, fc * 1024 + dc * 128: fc * 1024 + dc * 128 + 128],
                            hm[:, fc * G:(fc + 1) * G],
                            start=(fc == 0), stop=(fc == FC - 1))
                    return last
                seq.step("tensor", mmy)
                seq.step("vector", lambda v, dc=dc: v.tensor_scalar(
                    out=ysb[:, dc * G:(dc + 1) * G], in0=pso[:],
                    scalar1=b2_s[:, dc:dc + 1], scalar2=None, op0=ALU.add))
                seq.step("vector", lambda v, dc=dc: v.tensor_tensor(
                    out=z1[:, dc * G:(dc + 1) * G], in0=xfm[:, dc * G:(dc + 1) * G],
                    in1=ysb[:, dc * G:(dc + 1) * G], op=ALU.add))
            ln_block(z1, ysb, None)
            seq.step("sync", lambda s, gi=gi: [
                s.dma_start(out=x2_o[c * 128:(c + 1) * 128, gi * G:(gi + 1) * G],
                            in_=ysb[:, c * G:(c + 1) * G])
                for c in range(8)], n_dma=8)

        total = seq.run(block, gs)

        @block.sync
        def _(s):
            s.wait_ge(gs, total)
    return nc


def stageC_inmaps(x, attn_tok, W_out, b_out, W1, b1, W2, b2):
    xt = np.ascontiguousarray(x.reshape(B * N, D))
    com = dict(
        wout=np.ascontiguousarray(W_out),
        bout=np.ascontiguousarray(b_out.reshape(8, 128).T),
        w1t=np.ascontiguousarray(W1.T),
        b1=np.ascontiguousarray(b1.reshape(32, 128).T),
        w2t=np.ascontiguousarray(W2.T),
        b2=np.ascontiguousarray(b2.reshape(8, 128).T),
        ident_d=np.eye(128, dtype=np.float32),
    )
    maps = []
    for c in range(NCORES):
        m = dict(com)
        m["x"] = np.ascontiguousarray(xt[c * TPC:(c + 1) * TPC])
        m["attn"] = np.ascontiguousarray(attn_tok[c * TPC:(c + 1) * TPC])
        maps.append(m)
    return maps




D_CKV, D_CQ, D_ROPE = 256, 512, 32
RHALF = 16
THETA = (1.0 / (10000.0 ** (2.0 * np.arange(RHALF) / D_ROPE))).astype(np.float64)


def build_stageD(core_id_token_base):
    MMDT = BF16 if MM_BF16 else F32
    nc = bass.Bass("TRN2", target_bir_lowering=False)
    x2_in = nc.dram_tensor("x2_fm", [D, TPC], F32, kind="ExternalInput").ap()
    wdkv = nc.dram_tensor("wdkv", [D, D_CKV], F32, kind="ExternalInput").ap()
    wuk = nc.dram_tensor("wuk", [D_CKV, 512], F32, kind="ExternalInput").ap()
    wuv = nc.dram_tensor("wuv", [D_CKV, 512], F32, kind="ExternalInput").ap()
    wdq = nc.dram_tensor("wdq", [D, D_CQ], F32, kind="ExternalInput").ap()
    wuq = nc.dram_tensor("wuq", [D_CQ, 512], F32, kind="ExternalInput").ap()
    wqr = nc.dram_tensor("wqr", [D_CQ, 256], F32, kind="ExternalInput").ap()
    wkr = nc.dram_tensor("wkr", [D, D_ROPE], F32, kind="ExternalInput").ap()
    wiq = nc.dram_tensor("wiq", [D, 256], F32, kind="ExternalInput").ap()
    wik = nc.dram_tensor("wik", [D, 64], F32, kind="ExternalInput").ap()
    cosj = nc.dram_tensor("cosj", [TPC, RHALF], F32, kind="ExternalInput").ap()
    sinj = nc.dram_tensor("sinj", [TPC, RHALF], F32, kind="ExternalInput").ap()
    rdel = nc.dram_tensor("rdel", [1, RHALF], F32, kind="ExternalInput").ap()

    tab_o = nc.dram_tensor("tab", [TPC, 1088], BF16, kind="ExternalOutput").ap()
    ki_o = nc.dram_tensor("ki_fm", [64, TPC], F32, kind="ExternalOutput").ap()
    qi_o = nc.dram_tensor("qi_fm", [256, TPC], F32, kind="ExternalOutput").ap()
    qc_o = nc.dram_tensor("qc_tm", [TPC, 512], F32, kind="ExternalOutput").ap()
    qr_o = nc.dram_tensor("qr_tm", [TPC, 256], F32, kind="ExternalOutput").ap()

    G = 128
    NG = TPC // G

    from contextlib import ExitStack
    with ExitStack() as ctx:
        E = ctx.enter_context
        x2f = E(nc.sbuf_tensor("x2f", [128, 8 * G], F32))
        x2m = E(nc.sbuf_tensor("x2m", [128, 8 * G], MMDT))
        wdkv_s = E(nc.sbuf_tensor("wdkv_s", [128, 8 * D_CKV], MMDT))
        wuk_s = E(nc.sbuf_tensor("wuk_s", [128, 2 * 512], MMDT))
        wuv_s = E(nc.sbuf_tensor("wuv_s", [128, 2 * 512], MMDT))
        wdq_s = E(nc.sbuf_tensor("wdq_s", [128, 8 * D_CQ], MMDT))
        wuq_s = E(nc.sbuf_tensor("wuq_s", [128, 4 * 512], MMDT))
        wqr_s = E(nc.sbuf_tensor("wqr_s", [128, 4 * 256], MMDT))
        wkr_s = E(nc.sbuf_tensor("wkr_s", [128, 8 * D_ROPE], MMDT))
        wiq_s = E(nc.sbuf_tensor("wiq_s", [128, 8 * 256], F32))
        wik_s = E(nc.sbuf_tensor("wik_s", [128, 8 * 64], F32))
        ckvf = E(nc.sbuf_tensor("ckvf", [128, 2 * G], MMDT))
        cqf = E(nc.sbuf_tensor("cqf", [128, 4 * G], MMDT))
        keyc = E(nc.sbuf_tensor("keyc", [128, 512], F32))
        tabm = E(nc.sbuf_tensor("tabm", [128, 1088], BF16))
        krf = E(nc.sbuf_tensor("krf", [128, D_ROPE], F32))
        cosj_s = E(nc.sbuf_tensor("cosj_s", [128, RHALF], F32))
        sinj_s = E(nc.sbuf_tensor("sinj_s", [128, RHALF], F32))
        drow = E(nc.sbuf_tensor("drow", [1, 4 * RHALF], F32))
        cdb = E(nc.sbuf_tensor("cdb", [128, RHALF], F32))
        sdb = E(nc.sbuf_tensor("sdb", [128, RHALF], F32))
        ones1D = E(nc.sbuf_tensor("ones1D", [1, 128], F32))
        cj2 = E(nc.sbuf_tensor("cj2", [128, RHALF], F32))
        sj2 = E(nc.sbuf_tensor("sj2", [128, RHALF], F32))
        sp1 = E(nc.sbuf_tensor("sp1", [128, D_ROPE], F32))
        sp2 = E(nc.sbuf_tensor("sp2", [128, D_ROPE], F32))
        sp3 = E(nc.sbuf_tensor("sp3", [128, D_ROPE], F32))
        outf = E(nc.sbuf_tensor("outf", [128, 512], F32))
        pK = E(nc.psum_tensor("pK", [128, 512], F32))
        pV = E(nc.psum_tensor("pV", [128, 512], F32))
        pA = E(nc.psum_tensor("pA", [128, 512], F32))
        pB_ = E(nc.psum_tensor("pB_", [128, 512], F32))
        gs = E(nc.semaphore("gs"))
        block = E(nc.Block())

        seq = Seq(nc)

        def ldw(dst, src, chunks, cols):
            seq.step("gpsimd", lambda g, dst=dst, src=src, chunks=chunks, cols=cols: [
                g.dma_start(out=dst[:, c * cols:(c + 1) * cols],
                            in_=src[c * 128:(c + 1) * 128, :]) for c in range(chunks)],
                n_dma=chunks)

        ldw(wdkv_s, wdkv, 8, D_CKV)
        ldw(wuk_s, wuk, 2, 512)
        ldw(wuv_s, wuv, 2, 512)
        ldw(wdq_s, wdq, 8, D_CQ)
        ldw(wuq_s, wuq, 4, 512)
        ldw(wqr_s, wqr, 4, 256)
        ldw(wkr_s, wkr, 8, D_ROPE)
        seq.step("sync", lambda s: [
            s.dma_start(out=wiq_s[:, c * 256:(c + 1) * 256],
                        in_=wiq[c * 128:(c + 1) * 128, :]) for c in range(8)],
            n_dma=8)
        seq.step("sync", lambda s: [
            s.dma_start(out=wik_s[:, c * 64:(c + 1) * 64],
                        in_=wik[c * 128:(c + 1) * 128, :]) for c in range(8)],
            n_dma=8)
        PSBC = pA

        seq.step("vector", lambda v: v.memset(ones1D[:], 1.0))
        seq.step("sync", lambda s: s.dma_start(out=drow[0:1, 0:RHALF], in_=rdel),
                 n_dma=1)
        seq.step("scalar", lambda s: s.activation(out=drow[0:1, 0:RHALF],
                                                  in_=drow[0:1, 0:RHALF],
                                                  func=AF.Sigmoid))
        seq.step("vector", lambda v: v.tensor_scalar(
            out=drow[0:1, 0:RHALF], in0=drow[0:1, 0:RHALF],
            scalar1=-2.0 * PI, scalar2=None, op0=ALU.mult))
        seq.step("vector", lambda v: v.tensor_scalar(
            out=drow[0:1, RHALF:2 * RHALF], in0=drow[0:1, 0:RHALF],
            scalar1=-PI, scalar2=None, op0=ALU.is_lt))
        seq.step("vector", lambda v: v.scalar_tensor_tensor(
            out=drow[0:1, 0:RHALF], in0=drow[0:1, RHALF:2 * RHALF],
            scalar=2.0 * PI, in1=drow[0:1, 0:RHALF], op0=ALU.mult, op1=ALU.add))
        seq.step("scalar", lambda s: s.activation(
            out=drow[0:1, RHALF:2 * RHALF], in_=drow[0:1, 0:RHALF], func=AF.Sin))
        seq.step("scalar", lambda s: s.activation(
            out=drow[0:1, 2 * RHALF:3 * RHALF], in_=drow[0:1, 0:RHALF],
            func=AF.Abs))
        seq.step("vector", lambda v: v.tensor_scalar(
            out=drow[0:1, 2 * RHALF:3 * RHALF], in0=drow[0:1, 2 * RHALF:3 * RHALF],
            scalar1=-1.0, scalar2=PI / 2, op0=ALU.mult, op1=ALU.add))
        seq.step("scalar", lambda s: s.activation(
            out=drow[0:1, 2 * RHALF:3 * RHALF], in_=drow[0:1, 2 * RHALF:3 * RHALF],
            func=AF.Sin))
        seq.step("tensor", lambda t: t.matmul(
            PSBC[:, 0:RHALF], ones1D[:], drow[0:1, 2 * RHALF:3 * RHALF],
            start=True, stop=True))
        seq.step("vector", lambda v: v.tensor_copy(out=cdb[:],
                                                   in_=PSBC[:, 0:RHALF]))
        seq.step("tensor", lambda t: t.matmul(
            PSBC[:, 0:RHALF], ones1D[:], drow[0:1, RHALF:2 * RHALF],
            start=True, stop=True))
        seq.step("vector", lambda v: v.tensor_copy(out=sdb[:],
                                                   in_=PSBC[:, 0:RHALF]))

        for gi in range(NG):
            seq.step("sync", lambda s, gi=gi: [
                s.dma_start(out=x2f[:, c * G:(c + 1) * G],
                            in_=x2_in[c * 128:(c + 1) * 128, gi * G:(gi + 1) * G])
                for c in range(8)], n_dma=8)
            for c in range(8):
                seq.step("vector", lambda v, c=c: v.tensor_copy(
                    out=x2m[:, c * G:(c + 1) * G], in_=x2f[:, c * G:(c + 1) * G]))
            seq.step("sync", lambda s, gi=gi: [
                s.dma_start(out=cj2[:], in_=cosj[gi * G:(gi + 1) * G, :]),
                s.dma_start(out=sj2[:], in_=sinj[gi * G:(gi + 1) * G, :])],
                n_dma=2)
            seq.step("vector", lambda v: v.tensor_tensor(
                out=cosj_s[:], in0=cj2[:], in1=cdb[:], op=ALU.mult))
            seq.step("vector", lambda v: v.tensor_tensor(
                out=sp3[:, 0:RHALF], in0=sj2[:], in1=sdb[:], op=ALU.mult))
            seq.step("vector", lambda v: v.tensor_sub(
                out=cosj_s[:], in0=cosj_s[:], in1=sp3[:, 0:RHALF]))
            seq.step("vector", lambda v: v.tensor_tensor(
                out=sinj_s[:], in0=sj2[:], in1=cdb[:], op=ALU.mult))
            seq.step("vector", lambda v: v.tensor_tensor(
                out=sp3[:, 0:RHALF], in0=cj2[:], in1=sdb[:], op=ALU.mult))
            seq.step("vector", lambda v: v.tensor_add(
                out=sinj_s[:], in0=sinj_s[:], in1=sp3[:, 0:RHALF]))

            for rc in range(2):
                def mm1(t, rc=rc):
                    last = None
                    for c in range(8):
                        last = t.matmul(
                            pA[:, 0:G],
                            wdkv_s[:, c * D_CKV + rc * 128: c * D_CKV + rc * 128 + 128],
                            x2m[:, c * G:(c + 1) * G], start=(c == 0), stop=(c == 7))
                    return last
                seq.step("tensor", mm1)
                seq.step("vector", lambda v, rc=rc: v.tensor_copy(
                    out=ckvf[:, rc * G:(rc + 1) * G], in_=pA[:, 0:G]))
            for rc in range(4):
                def mm2(t, rc=rc):
                    last = None
                    for c in range(8):
                        last = t.matmul(
                            pA[:, 0:G],
                            wdq_s[:, c * D_CQ + rc * 128: c * D_CQ + rc * 128 + 128],
                            x2m[:, c * G:(c + 1) * G], start=(c == 0), stop=(c == 7))
                    return last
                seq.step("tensor", mm2)
                seq.step("vector", lambda v, rc=rc: v.tensor_copy(
                    out=cqf[:, rc * G:(rc + 1) * G], in_=pA[:, 0:G]))

            def mmK(t):
                last = None
                for j in range(2):
                    last = t.matmul(pK[:], ckvf[:, j * G:(j + 1) * G],
                                    wuk_s[:, j * 512:(j + 1) * 512],
                                    start=(j == 0), stop=(j == 1))
                return last
            seq.step("tensor", mmK)
            seq.step("vector", lambda v: v.tensor_copy(out=tabm[:, 0:512], in_=pK[:]))

            def mmV(t):
                last = None
                for j in range(2):
                    last = t.matmul(pV[:], ckvf[:, j * G:(j + 1) * G],
                                    wuv_s[:, j * 512:(j + 1) * 512],
                                    start=(j == 0), stop=(j == 1))
                return last
            seq.step("tensor", mmV)
            seq.step("vector", lambda v: v.tensor_copy(out=tabm[:, 512:1024],
                                                       in_=pV[:]))

            def mmR(t):
                last = None
                for c in range(8):
                    last = t.matmul(pA[:, 0:D_ROPE],
                                    x2m[:, c * G:(c + 1) * G],
                                    wkr_s[:, c * D_ROPE:(c + 1) * D_ROPE],
                                    start=(c == 0), stop=(c == 7))
                return last
            seq.step("tensor", mmR)
            seq.step("vector", lambda v: v.tensor_copy(out=krf[:], in_=pA[:, 0:D_ROPE]))
            seq.step("vector", lambda v: v.tensor_copy(out=tabm[:, 1024:1056],
                                                       in_=krf[:]))
            seq.step("scalar", lambda s: s.activation(out=sp1[:], in_=krf[:],
                                                      func=AF.Abs))
            seq.step("scalar", lambda s: s.activation(out=sp1[:], in_=sp1[:],
                                                      func=AF.Exp, scale=-1.0))
            seq.step("vector", lambda v: v.tensor_scalar(
                out=sp1[:], in0=sp1[:], scalar1=1.0, scalar2=None, op0=ALU.add))
            seq.step("scalar", lambda s: s.activation(out=sp1[:], in_=sp1[:],
                                                      func=AF.Ln))
            seq.step("vector", lambda v: v.tensor_scalar(
                out=sp2[:], in0=krf[:], scalar1=0.0, scalar2=None, op0=ALU.max))
            seq.step("vector", lambda v: v.tensor_add(out=sp1[:], in0=sp1[:],
                                                      in1=sp2[:]))
            seq.step("vector", lambda v: v.tensor_tensor(
                out=sp2[:, 0:RHALF], in0=sp1[:, 0:RHALF], in1=cosj_s[:],
                op=ALU.mult))
            seq.step("vector", lambda v: v.tensor_tensor(
                out=sp3[:, 0:RHALF], in0=sp1[:, RHALF:D_ROPE], in1=sinj_s[:],
                op=ALU.mult))
            seq.step("vector", lambda v: v.tensor_sub(
                out=sp2[:, 0:RHALF], in0=sp2[:, 0:RHALF], in1=sp3[:, 0:RHALF]))
            seq.step("vector", lambda v: v.tensor_copy(
                out=tabm[:, 1056:1056 + RHALF], in_=sp2[:, 0:RHALF]))
            seq.step("vector", lambda v: v.tensor_tensor(
                out=sp2[:, 0:RHALF], in0=sp1[:, 0:RHALF], in1=sinj_s[:],
                op=ALU.mult))
            seq.step("vector", lambda v: v.tensor_tensor(
                out=sp3[:, 0:RHALF], in0=sp1[:, RHALF:D_ROPE], in1=cosj_s[:],
                op=ALU.mult))
            seq.step("vector", lambda v: v.tensor_add(
                out=sp2[:, 0:RHALF], in0=sp2[:, 0:RHALF], in1=sp3[:, 0:RHALF]))
            seq.step("vector", lambda v: v.tensor_copy(
                out=tabm[:, 1056 + RHALF:1088], in_=sp2[:, 0:RHALF]))
            seq.step("gpsimd", lambda g, gi=gi: g.dma_start(
                out=tab_o[gi * G:(gi + 1) * G, :], in_=tabm[:]), n_dma=1)

            def mmQ(t):
                last = None
                for j in range(4):
                    last = t.matmul(pB_[:], cqf[:, j * G:(j + 1) * G],
                                    wuq_s[:, j * 512:(j + 1) * 512],
                                    start=(j == 0), stop=(j == 3))
                return last
            seq.step("tensor", mmQ)
            seq.step("vector", lambda v: v.tensor_copy(out=outf[:], in_=pB_[:]))
            seq.step("sync", lambda s, gi=gi: s.dma_start(
                out=qc_o[gi * G:(gi + 1) * G, :], in_=outf[:]), n_dma=1)

            def mmQR(t):
                last = None
                for j in range(4):
                    last = t.matmul(pB_[:, 0:256], cqf[:, j * G:(j + 1) * G],
                                    wqr_s[:, j * 256:(j + 1) * 256],
                                    start=(j == 0), stop=(j == 3))
                return last
            seq.step("tensor", mmQR)
            seq.step("vector", lambda v: v.tensor_copy(out=outf[:, 0:256],
                                                       in_=pB_[:, 0:256]))
            seq.step("sync", lambda s, gi=gi: s.dma_start(
                out=qr_o[gi * G:(gi + 1) * G, :], in_=outf[:, 0:256]), n_dma=1)

            for rc in range(2):
                def mmI(t, rc=rc):
                    last = None
                    for c in range(8):
                        last = t.matmul(
                            pA[:, 0:G],
                            wiq_s[:, c * 256 + rc * 128: c * 256 + rc * 128 + 128],
                            x2f[:, c * G:(c + 1) * G], start=(c == 0), stop=(c == 7))
                    return last
                seq.step("tensor", mmI)
                seq.step("vector", lambda v: v.tensor_copy(out=outf[:, 0:G],
                                                           in_=pA[:, 0:G]))
                seq.step("sync", lambda s, rc=rc, gi=gi: s.dma_start(
                    out=qi_o[rc * 128:(rc + 1) * 128, gi * G:(gi + 1) * G],
                    in_=outf[:, 0:G]), n_dma=1)

            def mmIK(t):
                last = None
                for c in range(8):
                    last = t.matmul(
                        pA[0:64, 0:G],
                        wik_s[:, c * 64:(c + 1) * 64],
                        x2f[:, c * G:(c + 1) * G], start=(c == 0), stop=(c == 7))
                return last
            seq.step("tensor", mmIK)
            seq.step("vector", lambda v: v.tensor_copy(out=outf[0:64, 0:G],
                                                       in_=pA[0:64, 0:G]))
            seq.step("sync", lambda s, gi=gi: s.dma_start(
                out=ki_o[:, gi * G:(gi + 1) * G], in_=outf[0:64, 0:G]), n_dma=1)

        total = seq.run(block, gs)

        @block.sync
        def _(s):
            s.wait_ge(gs, total)
    return nc


def stageD_inmaps(x2_fm_cores, W_dkv, W_uk, W_uv, W_dq, W_uq, W_qr, W_kr,
                  idx_wq, idx_wk, raw_delta):
    com = dict(
        wdkv=np.ascontiguousarray(W_dkv), wuk=np.ascontiguousarray(W_uk),
        wuv=np.ascontiguousarray(W_uv), wdq=np.ascontiguousarray(W_dq),
        wuq=np.ascontiguousarray(W_uq), wqr=np.ascontiguousarray(W_qr),
        wkr=np.ascontiguousarray(W_kr), wiq=np.ascontiguousarray(idx_wq),
        wik=np.ascontiguousarray(idx_wk),
        rdel=np.ascontiguousarray(raw_delta.reshape(1, RHALF)),
    )
    maps = []
    for c in range(NCORES):
        m = dict(com)
        m["x2_fm"] = np.ascontiguousarray(x2_fm_cores[c])
        j = (np.arange(TPC) + (c % 2) * TPC) % N
        ang = (j[:, None] + 63.0) * THETA[None, :]
        m["cosj"] = np.cos(ang).astype(np.float32)
        m["sinj"] = np.sin(ang).astype(np.float32)
        maps.append(m)
    return maps




NEG38 = -3.3e38
ZAPV = -3.39e38
FILLA = -1.0e30
FILLB = -1.0e26


def build_stageE(half, ti_list=None, dbg=False):
    nc = bass.Bass("TRN2", target_bir_lowering=False)
    qi_in = nc.dram_tensor("qi_sl", [256, TPC], F32, kind="ExternalInput").ap()
    ki_in = nc.dram_tensor("ki_b", [64, N], F32, kind="ExternalInput").ap()
    idxw_in = nc.dram_tensor("idxw", [1, 4], F32, kind="ExternalInput").ap()
    sel_o = nc.dram_tensor("sel", [TPC, 32], F32, kind="ExternalOutput").ap()
    idram = nc.dram_tensor("idram", [128, N], F32, kind="Internal").ap()

    NT = TPC // 128
    tis = list(range(NT)) if ti_list is None else list(ti_list)
    if dbg:
        dbg_sc = nc.dram_tensor("dbg_sc", [128, 8 * 288], F32, kind="ExternalOutput").ap()
        dbg_out = nc.dram_tensor("dbg_out", [128, 512], F32, kind="ExternalOutput").ap()
        dbg_vout = nc.dram_tensor("dbg_vout", [128, 512], F32, kind="ExternalOutput").ap()
        dbg_prob = nc.dram_tensor("dbg_prob", [128, 8 * 288], F32, kind="ExternalOutput").ap()
        dbg_qcfm = nc.dram_tensor("dbg_qcfm", [64, 8 * 128], F32, kind="ExternalOutput").ap()
        dbg_kwfm = nc.dram_tensor("dbg_kwfm", [64, 8 * 256], F32, kind="ExternalOutput").ap()
        dbg_qhfm = nc.dram_tensor("dbg_qhfm", [32, 8 * 128], F32, kind="ExternalOutput").ap()
        dbg_khfm = nc.dram_tensor("dbg_khfm", [32, 256], F32, kind="ExternalOutput").ap()
    from contextlib import ExitStack
    with ExitStack() as ctx:
        E = ctx.enter_context
        ki_s = E(nc.sbuf_tensor("ki_s", [64, N], F32))
        qi_s = E(nc.sbuf_tensor("qi_s", [64, 512], F32))
        itile = E(nc.sbuf_tensor("itile", [128, N], F32))
        tmpc = E(nc.sbuf_tensor("tmpc", [128, 512], F32))
        dsc = E(nc.sbuf_tensor("dsc", [128, 512], F32))
        winU = E(nc.sbuf_tensor("winU", [128, 512], U32))
        futU = E(nc.sbuf_tensor("futU", [128, 512], U32))
        m1 = E(nc.sbuf_tensor("m1", [128, 128], F32))
        m8 = E(nc.sbuf_tensor("m8", [128, 8], F32))
        m8i = E(nc.sbuf_tensor("m8i", [128, 8], U32))
        cidu = E(nc.sbuf_tensor("cidu", [128, 32], U32))
        cidf = E(nc.sbuf_tensor("cidf", [128, 32], F32))
        cneg = E(nc.sbuf_tensor("cneg", [128, 32], F32))
        sortid = E(nc.sbuf_tensor("sortid", [128, 32], F32))
        gidx = E(nc.sbuf_tensor("gidx", [128, 32], I32))
        gtmp = E(nc.sbuf_tensor("gtmp", [128, 32], F32))
        prow = E(nc.sbuf_tensor("prow", [128, 1], F32))
        cand = E(nc.sbuf_tensor("cand", [128, 2048], F32))
        pos = E(nc.sbuf_tensor("pos", [128, 32], U32))
        posf = E(nc.sbuf_tensor("posf", [128, 32], F32))
        lanef = E(nc.sbuf_tensor("lanef", [128, 32], F32))
        shfu = E(nc.sbuf_tensor("shfu", [128, 32], U32))
        eqt = E(nc.sbuf_tensor("eqt", [128, 1024], F32))
        selb = E(nc.sbuf_tensor("selb", [128, 32], F32))
        iotab = E(nc.sbuf_tensor("iotab", [128, 512], F32))
        iota32 = E(nc.sbuf_tensor("iota32", [128, 32], F32))
        tcol = E(nc.sbuf_tensor("tcol", [128, 1], F32))
        idxwb = E(nc.sbuf_tensor("idxwb", [128, 4], F32))
        idxwr = E(nc.sbuf_tensor("idxwr", [1, 4], F32))
        ones1 = E(nc.sbuf_tensor("ones1", [1, 128], F32))
        ngt = E(nc.sbuf_tensor("ngt", [128, 512], F32))
        psI0 = E(nc.psum_tensor("psI0", [128, 512], F32))
        psI1 = E(nc.psum_tensor("psI1", [128, 512], F32))
        psI2 = E(nc.psum_tensor("psI2", [128, 512], F32))
        psI3 = E(nc.psum_tensor("psI3", [128, 512], F32))
        psb = E(nc.psum_tensor("psb", [128, 4], F32))
        gs = E(nc.semaphore("gs"))
        block = E(nc.Block())

        seq = Seq(nc)
        seq.step("vector", lambda v: v.memset(ones1[:], 1.0))
        seq.step("vector", lambda v: v.memset(ngt[:], NEG38))
        seq.step("gpsimd", lambda g: g.iota(
            iotab[:], pattern=[[1, 512]], base=0, channel_multiplier=0,
            allow_small_or_imprecise_dtypes=True))
        seq.step("gpsimd", lambda g: g.iota(
            iota32[:], pattern=[[1, 32]], base=0, channel_multiplier=0,
            allow_small_or_imprecise_dtypes=True))
        seq.step("gpsimd", lambda g: g.iota(
            prow[:], pattern=[[0, 1]], base=0, channel_multiplier=128,
            allow_small_or_imprecise_dtypes=True))
        seq.step("sync", lambda s: [
            s.dma_start(out=ki_s[:], in_=ki_in),
            s.dma_start(out=idxwr[:], in_=idxw_in)], n_dma=2)
        seq.step("tensor", lambda t: t.matmul(psb[:], ones1[:], idxwr[:],
                                              start=True, stop=True))
        seq.step("vector", lambda v: v.tensor_copy(out=idxwb[:], in_=psb[:]))

        for ti in range(NT):
            T = half * TPC + ti * 128
            Lraw = ((T + 160 + 511) // 512) * 512
            L = max(2048, min(N, Lraw))
            NSC = L // 512
            seq.step("gpsimd", lambda g, T=T: g.iota(
                tcol[:], pattern=[[0, 1]], base=T, channel_multiplier=1,
                allow_small_or_imprecise_dtypes=True))
            seq.step("sync", lambda s, ti=ti: [
                s.dma_start(out=qi_s[0:64, h * 128:(h + 1) * 128],
                            in_=qi_in[h * 64:(h + 1) * 64, ti * 128:(ti + 1) * 128])
                for h in range(4)], n_dma=4)
            for sc in range(NSC):
                lo = sc * 512
                full_band = (lo + 512 <= max(T - 63, 0))
                pure_fill = (lo > T + 127)
                if pure_fill:
                    seq.step("vector", lambda v, lo=lo: v.tensor_scalar(
                        out=itile[:, lo:lo + 512], in0=iotab[:],
                        scalar1=FILLB, scalar2=FILLA + lo * FILLB,
                        op0=ALU.mult, op1=ALU.add))
                    continue
                def mmI(t, sc=sc):
                    last = None
                    for h in range(4):
                        ps = (psI0, psI1, psI2, psI3)[h]
                        last = t.matmul(
                            ps[:],
                            qi_s[0:64, h * 128:(h + 1) * 128],
                            ki_s[:, sc * 512:(sc + 1) * 512],
                            start=True, stop=True)
                    return last
                seq.step("tensor", mmI)
                for h in range(4):
                    ps = (psI0, psI1, psI2, psI3)[h]
                    if h == 0:
                        seq.step("vector", lambda v, ps=ps, h=h, lo=lo: v.tensor_scalar(
                            out=itile[:, lo:lo + 512], in0=ps[:], scalar1=0.0,
                            scalar2=idxwb[:, h:h + 1], op0=ALU.max, op1=ALU.mult))
                    else:
                        seq.step("vector", lambda v, ps=ps, h=h: v.tensor_scalar(
                            out=tmpc[:], in0=ps[:], scalar1=0.0,
                            scalar2=idxwb[:, h:h + 1], op0=ALU.max, op1=ALU.mult))
                        seq.step("vector", lambda v, lo=lo: v.tensor_add(
                            out=itile[:, lo:lo + 512], in0=itile[:, lo:lo + 512],
                            in1=tmpc[:]))
                if not full_band:
                    seq.step("vector", lambda v, lo=lo: v.tensor_scalar(
                        out=dsc[:], in0=iotab[:], scalar1=tcol[:, 0:1],
                        scalar2=float(lo), op0=ALU.subtract, op1=ALU.add))
                    seq.step("vector", lambda v: v.tensor_scalar(
                        out=winU[:], in0=dsc[:], scalar1=-63.0, scalar2=None,
                        op0=ALU.is_ge))
                    seq.step("vector", lambda v: v.tensor_scalar(
                        out=futU[:], in0=dsc[:], scalar1=0.0, scalar2=None,
                        op0=ALU.is_gt))
                    seq.step("vector", lambda v: v.tensor_tensor(
                        out=winU[:], in0=winU[:], in1=futU[:], op=ALU.subtract))
                    seq.step("vector", lambda v, lo=lo: v.select(
                        out=itile[:, lo:lo + 512], mask=winU[:], on_true=ngt[:],
                        on_false=itile[:, lo:lo + 512]))
                    seq.step("vector", lambda v, lo=lo: v.tensor_scalar(
                        out=tmpc[:], in0=iotab[:], scalar1=FILLB,
                        scalar2=FILLA + lo * FILLB, op0=ALU.mult, op1=ALU.add))
                    seq.step("vector", lambda v, lo=lo: v.select(
                        out=itile[:, lo:lo + 512], mask=futU[:], on_true=tmpc[:],
                        on_false=itile[:, lo:lo + 512]))
            NCH1 = L // 64
            seq.step("vector", lambda v, L=L, NCH1=NCH1: v.tensor_reduce(
                out=m1[:, 0:NCH1],
                in_=itile[:, 0:L].rearrange("p (a b) -> p a b", b=64),
                axis=AX.X, op=ALU.max))
            for r in range(4):
                seq.step("vector", lambda v, NCH1=NCH1: v.max(
                    out=m8[:], in_=m1[:, 0:NCH1]))
                seq.step("vector", lambda v, NCH1=NCH1: v.max_index(
                    out=m8i[:], in_max=m8[:], in_values=m1[:, 0:NCH1]))
                seq.step("vector", lambda v, r=r: v.tensor_copy(
                    out=cidu[:, r * 8:(r + 1) * 8], in_=m8i[:]))
                if r < 3:
                    seq.step("vector", lambda v, NCH1=NCH1: v.match_replace(
                        out=m1[:, 0:NCH1], in_to_replace=m8[:],
                        in_values=m1[:, 0:NCH1], imm_value=ZAPV))
            seq.step("vector", lambda v: v.tensor_copy(out=cidf[:], in_=cidu[:]))
            seq.step("vector", lambda v: v.tensor_scalar(
                out=cneg[:], in0=cidf[:], scalar1=-1.0, scalar2=None, op0=ALU.mult))
            for r in range(4):
                seq.step("vector", lambda v: v.max(out=m8[:], in_=cneg[:]))
                seq.step("vector", lambda v, r=r: v.tensor_scalar(
                    out=sortid[:, r * 8:(r + 1) * 8], in0=m8[:], scalar1=-1.0,
                    scalar2=None, op0=ALU.mult))
                if r < 3:
                    seq.step("vector", lambda v: v.match_replace(
                        out=cneg[:], in_to_replace=m8[:], in_values=cneg[:],
                        imm_value=ZAPV))
            seq.step("sync", lambda s, L=L: s.dma_start(
                out=idram[:, 0:L], in_=itile[:, 0:L]), n_dma=1)
            seq.step("vector", lambda v: v.tensor_scalar(
                out=gtmp[:], in0=sortid[:], scalar1=prow[:, 0:1], scalar2=None,
                op0=ALU.add))
            seq.step("vector", lambda v: v.tensor_copy(out=gidx[:], in_=gtmp[:]))
            seq.step("gpsimd", lambda g: [
                g.indirect_dma_start(
                    out=cand[:, gg * 64:(gg + 1) * 64],
                    out_offset=None,
                    in_=idram.rearrange("p (r c) -> (p r) c", c=64),
                    in_offset=IndirectOffsetOnAxis(gidx[:, gg:gg + 1], axis=0))
                for gg in range(32)], n_dma=32)
            for r in range(4):
                seq.step("vector", lambda v: v.max(out=m8[:], in_=cand[:]))
                seq.step("vector", lambda v: v.max_index(
                    out=m8i[:], in_max=m8[:], in_values=cand[:]))
                seq.step("vector", lambda v, r=r: v.tensor_copy(
                    out=pos[:, r * 8:(r + 1) * 8], in_=m8i[:]))
                if r < 3:
                    seq.step("vector", lambda v: v.match_replace(
                        out=cand[:], in_to_replace=m8[:], in_values=cand[:],
                        imm_value=ZAPV))
            seq.step("vector", lambda v: v.tensor_scalar(
                out=shfu[:], in0=pos[:], scalar1=6, scalar2=None,
                op0=ALU.logical_shift_right))
            seq.step("vector", lambda v: v.tensor_scalar(
                out=pos[:], in0=pos[:], scalar1=63, scalar2=None,
                op0=ALU.bitwise_and))
            seq.step("vector", lambda v: v.tensor_copy(out=posf[:], in_=shfu[:]))
            seq.step("vector", lambda v: v.tensor_copy(out=lanef[:], in_=pos[:]))
            seq.step("vector", lambda v: v.tensor_tensor(
                out=eqt[:].rearrange("p (j c) -> p j c", c=32),
                in0=posf[:].rearrange("p (j o) -> p j o", o=1).to_broadcast([128, 32, 32]),
                in1=iota32[:].rearrange("p (o c) -> p o c", o=1).to_broadcast([128, 32, 32]),
                op=ALU.is_equal))
            seq.step("vector", lambda v: v.tensor_tensor(
                out=eqt[:].rearrange("p (j c) -> p j c", c=32),
                in0=eqt[:].rearrange("p (j c) -> p j c", c=32),
                in1=sortid[:].rearrange("p (o c) -> p o c", o=1).to_broadcast([128, 32, 32]),
                op=ALU.mult))
            seq.step("vector", lambda v: v.tensor_reduce(
                out=selb[:], in_=eqt[:].rearrange("p (j c) -> p j c", c=32),
                axis=AX.X, op=ALU.add))
            seq.step("vector", lambda v: v.tensor_scalar(
                out=selb[:], in0=selb[:], scalar1=64.0, scalar2=None,
                op0=ALU.mult))
            seq.step("vector", lambda v: v.tensor_add(
                out=selb[:], in0=selb[:], in1=lanef[:]))
            seq.step("sync", lambda s, ti=ti: s.dma_start(
                out=sel_o[ti * 128:(ti + 1) * 128, :], in_=selb[:]), n_dma=1)

        total = seq.run(block, gs)

        @block.sync
        def _(s):
            s.wait_ge(gs, total)
    return nc


def stageE_inmaps(qi_cores, ki_batches, idx_w):
    maps = []
    for c in range(NCORES):
        b = c // 2
        maps.append(dict(
            qi_sl=np.ascontiguousarray(qi_cores[c]),
            ki_b=np.ascontiguousarray(ki_batches[b]),
            idxw=np.ascontiguousarray(idx_w.reshape(1, 4)),
        ))
    return maps




RSCALE = float((64 + 32) ** -0.5)
NG30 = -1.0e30


def build_stageF(half, ti_list=None, dbg=False):
    MMDT = BF16 if MM_BF16 else F32
    nc = bass.Bass("TRN2", target_bir_lowering=False)
    sel_in = nc.dram_tensor("sel", [TPC, 32], F32, kind="ExternalInput").ap()
    tab_in = nc.dram_tensor("tab_b", [N, 1088], BF16, kind="ExternalInput").ap()
    qc_in = nc.dram_tensor("qc_tm", [TPC, 512], F32, kind="ExternalInput").ap()
    qr_in = nc.dram_tensor("qr_tm", [TPC, 256], F32, kind="ExternalInput").ap()
    x2_in = nc.dram_tensor("x2_fm", [D, TPC], F32, kind="ExternalInput").ap()
    wout_d = nc.dram_tensor("woutL1", [512, D], F32, kind="ExternalInput").ap()
    cos2_d = nc.dram_tensor("cos2", [TPC, RHALF], F32, kind="ExternalInput").ap()
    sin2_d = nc.dram_tensor("sin2", [TPC, RHALF], F32, kind="ExternalInput").ap()
    cost1_d = nc.dram_tensor("cost1", [TPC, RHALF], F32, kind="ExternalInput").ap()
    sint1_d = nc.dram_tensor("sint1", [TPC, RHALF], F32, kind="ExternalInput").ap()
    cosr_d = nc.dram_tensor("cosr", [1, 512], F32, kind="ExternalInput").ap()
    sinr_d = nc.dram_tensor("sinr", [1, 512], F32, kind="ExternalInput").ap()
    cosjj_d = nc.dram_tensor("cosjj", [128, RHALF], F32, kind="ExternalInput").ap()
    sinjj_d = nc.dram_tensor("sinjj", [128, RHALF], F32, kind="ExternalInput").ap()
    rdel = nc.dram_tensor("rdel", [1, RHALF], F32, kind="ExternalInput").ap()
    ident_d = nc.dram_tensor("ident_d", [128, 128], F32, kind="ExternalInput").ap()
    x3_o = nc.dram_tensor("x3_fm", [D, TPC], F32, kind="ExternalOutput").ap()

    NT = TPC // 128
    tis = list(range(NT)) if ti_list is None else list(ti_list)
    if dbg:
        dbg_sc = nc.dram_tensor("dbg_sc", [128, 8 * 288], F32, kind="ExternalOutput").ap()
        dbg_out = nc.dram_tensor("dbg_out", [128, 512], F32, kind="ExternalOutput").ap()
        dbg_vout = nc.dram_tensor("dbg_vout", [128, 512], F32, kind="ExternalOutput").ap()
        dbg_prob = nc.dram_tensor("dbg_prob", [128, 8 * 288], F32, kind="ExternalOutput").ap()
        dbg_qcfm = nc.dram_tensor("dbg_qcfm", [64, 8 * 128], F32, kind="ExternalOutput").ap()
        dbg_kwfm = nc.dram_tensor("dbg_kwfm", [64, 8 * 256], F32, kind="ExternalOutput").ap()
        dbg_qhfm = nc.dram_tensor("dbg_qhfm", [32, 8 * 128], F32, kind="ExternalOutput").ap()
        dbg_khfm = nc.dram_tensor("dbg_khfm", [32, 256], F32, kind="ExternalOutput").ap()
    from contextlib import ExitStack
    with ExitStack() as ctx:
        E = ctx.enter_context
        wout_s = E(nc.sbuf_tensor("wout_s", [128, 4 * 1024], MMDT))
        identf = E(nc.sbuf_tensor("identf", [128, 128], F32))
        identm = E(nc.sbuf_tensor("identm", [128, 128], MMDT))
        ones1 = E(nc.sbuf_tensor("ones1", [1, 128], F32))
        onescol = E(nc.sbuf_tensor("onescol", [128, 1], F32))
        crb = E(nc.sbuf_tensor("crb", [128, 512], F32))
        srb = E(nc.sbuf_tensor("srb", [128, 512], F32))
        rowr = E(nc.sbuf_tensor("rowr", [1, 512], F32))
        cosjj_s = E(nc.sbuf_tensor("cosjj_s", [128, RHALF], F32))
        sinjj_s = E(nc.sbuf_tensor("sinjj_s", [128, RHALF], F32))
        iotab = E(nc.sbuf_tensor("iotab", [128, 256], F32))
        tcol = E(nc.sbuf_tensor("tcol", [128, 1], F32))
        ng30 = E(nc.sbuf_tensor("ng30", [128, 288], F32))
        wslab = E(nc.sbuf_tensor("wslab", [128, 2 * 1088], BF16))
        slab96 = E(nc.sbuf_tensor("slab96", [128, 1088], BF16))
        ktab = E(nc.sbuf_tensor("ktab", [128, 32 * 1088], BF16))
        seli = E(nc.sbuf_tensor("seli", [128, 32], I32))
        self_ = E(nc.sbuf_tensor("self_", [128, 32], F32))
        qct = E(nc.sbuf_tensor("qct", [128, 512], F32))
        qcm = E(nc.sbuf_tensor("qcm", [128, 512], MMDT))
        qrt = E(nc.sbuf_tensor("qrt", [128, 256], F32))
        mq = E(nc.sbuf_tensor("mq", [128, 256], F32))
        qhat = E(nc.sbuf_tensor("qhat", [128, 256], F32))
        qhatm = E(nc.sbuf_tensor("qhatm", [128, 256], MMDT))
        cos2_s = E(nc.sbuf_tensor("cos2_s", [128, RHALF], F32))
        sin2_s = E(nc.sbuf_tensor("sin2_s", [128, RHALF], F32))
        cost1_s = E(nc.sbuf_tensor("cost1_s", [128, RHALF], F32))
        sint1_s = E(nc.sbuf_tensor("sint1_s", [128, RHALF], F32))
        drow = E(nc.sbuf_tensor("drow", [1, 4 * RHALF], F32))
        cdb = E(nc.sbuf_tensor("cdb", [128, RHALF], F32))
        sdb = E(nc.sbuf_tensor("sdb", [128, RHALF], F32))
        rr1 = E(nc.sbuf_tensor("rr1", [128, RHALF], F32))
        rr2 = E(nc.sbuf_tensor("rr2", [128, RHALF], F32))
        qcfm = E(nc.sbuf_tensor("qcfm", [64, 8 * 128], MMDT))
        qhfm = E(nc.sbuf_tensor("qhfm", [32, 8 * 128], MMDT))
        kwfm = E(nc.sbuf_tensor("kwfm", [64, 8 * 256], MMDT))
        khfm = E(nc.sbuf_tensor("khfm", [32, 256], MMDT))
        ptw = E(nc.sbuf_tensor("ptw", [128, 256], MMDT))
        cphi = E(nc.sbuf_tensor("cphi", [128, 512], F32))
        sphi = E(nc.sbuf_tensor("sphi", [128, 512], F32))
        krsel = E(nc.sbuf_tensor("krsel", [128, 1024], F32))
        kkre = E(nc.sbuf_tensor("kkre", [128, 512], F32))
        kkim = E(nc.sbuf_tensor("kkim", [128, 512], F32))
        t5a = E(nc.sbuf_tensor("t5a", [128, 2048], F32))
        t5b = E(nc.sbuf_tensor("t5b", [128, 512], F32))
        t5c = E(nc.sbuf_tensor("t5c", [128, 512], F32))
        scoresb = E(nc.sbuf_tensor("scoresb", [128, 8 * 288], F32))
        probm = E(nc.sbuf_tensor("probm", [128, 8 * 288], MMDT))
        rowm = E(nc.sbuf_tensor("rowm", [128, 8], F32))
        rows_ = E(nc.sbuf_tensor("rows_", [128, 8], F32))
        winmU = E(nc.sbuf_tensor("winmU", [128, 256], U32))
        jmU = E(nc.sbuf_tensor("jmU", [128, 128], U32))
        rmask = E(nc.sbuf_tensor("rmask", [128, 1], U32))
        dsc = E(nc.sbuf_tensor("dsc", [128, 256], F32))
        vout = E(nc.sbuf_tensor("vout", [128, 512], F32))
        outtm = E(nc.sbuf_tensor("outtm", [128, 512], F32))
        outm = E(nc.sbuf_tensor("outm", [128, 512], MMDT))
        outfm = E(nc.sbuf_tensor("outfm", [128, 512], MMDT))
        x2f = E(nc.sbuf_tensor("x2f", [128, 8 * 128], F32))
        z1 = E(nc.sbuf_tensor("z1", [128, 8 * 128], F32))
        sqt = E(nc.sbuf_tensor("sqt", [128, 128], F32))
        rowt = E(nc.sbuf_tensor("rowt", [1, 6 * 128], F32))
        mub = E(nc.sbuf_tensor("mub", [128, 128], F32))
        rsb = E(nc.sbuf_tensor("rsb", [128, 128], F32))
        kr96 = E(nc.sbuf_tensor("kr96", [128, 32], F32))
        k2rot = E(nc.sbuf_tensor("k2rot", [128, 32], F32))
        k2m = E(nc.sbuf_tensor("k2m", [128, 32], MMDT))
        k2fm = E(nc.sbuf_tensor("k2fm", [32, 128], MMDT))
        k2cfm = E(nc.sbuf_tensor("k2cfm", [64, 8 * 128], MMDT))
        q2hat = E(nc.sbuf_tensor("q2hat", [128, 256], F32))
        q2m = E(nc.sbuf_tensor("q2m", [128, 256], MMDT))
        q2fm = E(nc.sbuf_tensor("q2fm", [32, 8 * 128], MMDT))
        sc2 = E(nc.sbuf_tensor("sc2", [128, 8 * 128], F32))
        pr2m = E(nc.sbuf_tensor("pr2m", [128, 8 * 128], MMDT))
        vout2 = E(nc.sbuf_tensor("vout2", [128, 512], F32))
        rowm2 = E(nc.sbuf_tensor("rowm2", [128, 8], F32))
        psW0 = E(nc.psum_tensor("psW0", [128, 512], F32))
        psW1 = E(nc.psum_tensor("psW1", [128, 512], F32))
        psW2 = E(nc.psum_tensor("psW2", [128, 512], F32))
        psW3 = E(nc.psum_tensor("psW3", [128, 512], F32))
        pstm = E(nc.psum_tensor("pstm", [128, 512], MMDT))
        out_ps = E(nc.psum_tensor("out_ps", [128, 512], F32))
        psb = E(nc.psum_tensor("psb", [128, 256], F32))
        gs = E(nc.semaphore("gs"))
        block = E(nc.Block())

        psW = (psW0, psW1, psW2, psW3)
        seq = Seq(nc)
        seq.step("vector", lambda v: v.memset(ones1[:], 1.0))
        seq.step("vector", lambda v: v.memset(onescol[:], 1.0))
        seq.step("vector", lambda v: v.memset(ng30[:], NG30))
        seq.step("gpsimd", lambda g: g.iota(
            iotab[:], pattern=[[1, 256]], base=0, channel_multiplier=0,
            allow_small_or_imprecise_dtypes=True))
        seq.step("sync", lambda s: [
            s.dma_start(out=identf[:], in_=ident_d),
            s.dma_start(out=rowr[:], in_=cosr_d),
            s.dma_start(out=cosjj_s[:], in_=cosjj_d),
            s.dma_start(out=sinjj_s[:], in_=sinjj_d)], n_dma=4)
        seq.step("vector", lambda v: v.tensor_copy(out=identm[:], in_=identf[:]))
        seq.step("tensor", lambda t: t.matmul(psW0[:], ones1[:], rowr[:],
                                              start=True, stop=True))
        seq.step("vector", lambda v: v.tensor_copy(out=crb[:], in_=psW0[:]))
        seq.step("sync", lambda s: s.dma_start(out=rowr[:], in_=sinr_d), n_dma=1)
        seq.step("tensor", lambda t: t.matmul(psW0[:], ones1[:], rowr[:],
                                              start=True, stop=True))
        seq.step("vector", lambda v: v.tensor_copy(out=srb[:], in_=psW0[:]))
        seq.step("gpsimd", lambda g: [
            g.dma_start(out=wout_s[:, c * 1024:(c + 1) * 1024],
                        in_=wout_d[c * 128:(c + 1) * 128, :]) for c in range(4)],
            n_dma=4)

        seq.step("sync", lambda s: s.dma_start(out=drow[0:1, 0:RHALF], in_=rdel),
                 n_dma=1)
        seq.step("scalar", lambda s: s.activation(out=drow[0:1, 0:RHALF],
                                                  in_=drow[0:1, 0:RHALF],
                                                  func=AF.Sigmoid))
        seq.step("vector", lambda v: v.tensor_scalar(
            out=drow[0:1, 0:RHALF], in0=drow[0:1, 0:RHALF],
            scalar1=-2.0 * PI, scalar2=None, op0=ALU.mult))
        seq.step("vector", lambda v: v.tensor_scalar(
            out=drow[0:1, RHALF:2 * RHALF], in0=drow[0:1, 0:RHALF],
            scalar1=-PI, scalar2=None, op0=ALU.is_lt))
        seq.step("vector", lambda v: v.scalar_tensor_tensor(
            out=drow[0:1, 0:RHALF], in0=drow[0:1, RHALF:2 * RHALF],
            scalar=2.0 * PI, in1=drow[0:1, 0:RHALF], op0=ALU.mult, op1=ALU.add))
        seq.step("scalar", lambda s: s.activation(
            out=drow[0:1, RHALF:2 * RHALF], in_=drow[0:1, 0:RHALF], func=AF.Sin))
        seq.step("scalar", lambda s: s.activation(
            out=drow[0:1, 2 * RHALF:3 * RHALF], in_=drow[0:1, 0:RHALF],
            func=AF.Abs))
        seq.step("vector", lambda v: v.tensor_scalar(
            out=drow[0:1, 2 * RHALF:3 * RHALF], in0=drow[0:1, 2 * RHALF:3 * RHALF],
            scalar1=-1.0, scalar2=PI / 2, op0=ALU.mult, op1=ALU.add))
        seq.step("scalar", lambda s: s.activation(
            out=drow[0:1, 2 * RHALF:3 * RHALF], in_=drow[0:1, 2 * RHALF:3 * RHALF],
            func=AF.Sin))
        seq.step("tensor", lambda t: t.matmul(
            psW1[:, 0:RHALF], ones1[:], drow[0:1, 2 * RHALF:3 * RHALF],
            start=True, stop=True))
        seq.step("vector", lambda v: v.tensor_copy(out=cdb[:], in_=psW1[:, 0:RHALF]))
        seq.step("tensor", lambda t: t.matmul(
            psW1[:, 0:RHALF], ones1[:], drow[0:1, RHALF:2 * RHALF],
            start=True, stop=True))
        seq.step("vector", lambda v: v.tensor_copy(out=sdb[:], in_=psW1[:, 0:RHALF]))
        crv0 = crb[:].rearrange("p (k x) -> p k x", x=RHALF)
        srv0 = srb[:].rearrange("p (k x) -> p k x", x=RHALF)
        cdv = cdb[:].rearrange("p (o x) -> p o x", o=1).to_broadcast([128, 32, RHALF])
        sdv = sdb[:].rearrange("p (o x) -> p o x", o=1).to_broadcast([128, 32, RHALF])
        t5av0 = t5a[:, 0:512].rearrange("p (k x) -> p k x", x=RHALF)
        t5bv0 = t5b[:, 0:512].rearrange("p (k x) -> p k x", x=RHALF)
        seq.step("vector", lambda v: v.tensor_tensor(out=t5av0, in0=crv0, in1=cdv,
                                                     op=ALU.mult))
        seq.step("vector", lambda v: v.tensor_tensor(out=t5bv0, in0=srv0, in1=sdv,
                                                     op=ALU.mult))
        seq.step("vector", lambda v: v.tensor_sub(out=t5a[:, 0:512],
                                                  in0=t5a[:, 0:512],
                                                  in1=t5b[:, 0:512]))
        seq.step("vector", lambda v: v.tensor_tensor(out=t5bv0, in0=srv0, in1=cdv,
                                                     op=ALU.mult))
        seq.step("vector", lambda v: v.tensor_tensor(out=srv0, in0=crv0, in1=sdv,
                                                     op=ALU.mult))
        seq.step("vector", lambda v: v.tensor_add(out=srb[:], in0=srb[:],
                                                  in1=t5b[:, 0:512]))
        seq.step("vector", lambda v: v.tensor_copy(out=crb[:], in_=t5a[:, 0:512]))
        seq.step("vector", lambda v: v.tensor_tensor(out=rr1[:], in0=cosjj_s[:],
                                                     in1=cdb[:], op=ALU.mult))
        seq.step("vector", lambda v: v.tensor_tensor(out=rr2[:], in0=sinjj_s[:],
                                                     in1=sdb[:], op=ALU.mult))
        seq.step("vector", lambda v: v.tensor_sub(out=rr1[:], in0=rr1[:],
                                                  in1=rr2[:]))
        seq.step("vector", lambda v: v.tensor_tensor(out=rr2[:], in0=sinjj_s[:],
                                                     in1=cdb[:], op=ALU.mult))
        seq.step("vector", lambda v: v.tensor_tensor(out=sinjj_s[:], in0=cosjj_s[:],
                                                     in1=sdb[:], op=ALU.mult))
        seq.step("vector", lambda v: v.tensor_add(out=sinjj_s[:], in0=sinjj_s[:],
                                                  in1=rr2[:]))
        seq.step("vector", lambda v: v.tensor_copy(out=cosjj_s[:], in_=rr1[:]))

        def softplus_steps(dst, src, tmp):
            seq.step("scalar", lambda s: s.activation(out=tmp, in_=src,
                                                      func=AF.Abs))
            seq.step("scalar", lambda s: s.activation(out=tmp, in_=tmp,
                                                      func=AF.Exp, scale=-1.0))
            seq.step("vector", lambda v: v.tensor_scalar(
                out=tmp, in0=tmp, scalar1=1.0, scalar2=None, op0=ALU.add))
            seq.step("scalar", lambda s: s.activation(out=tmp, in_=tmp,
                                                      func=AF.Ln))
            seq.step("vector", lambda v: v.tensor_scalar(
                out=dst, in0=src, scalar1=0.0, scalar2=None, op0=ALU.max))
            seq.step("vector", lambda v: v.tensor_add(out=dst, in0=dst, in1=tmp))

        def cplx_rot(dst, m1s, m2s, cs, ss, scale, conj_rot):
            pass

        for ti in tis:
            T = half * TPC + ti * 128
            W0 = T - 64
            seq.step("sync", lambda s, ti=ti: s.dma_start(
                out=self_[:], in_=sel_in[ti * 128:(ti + 1) * 128, :]), n_dma=1)
            seq.step("vector", lambda v: v.tensor_copy(out=seli[:], in_=self_[:]))
            seq.step("sync", lambda s, ti=ti: [
                s.dma_start(out=qct[:], in_=qc_in[ti * 128:(ti + 1) * 128, :]),
                s.dma_start(out=qrt[:], in_=qr_in[ti * 128:(ti + 1) * 128, :]),
                s.dma_start(out=cos2_s[:], in_=cos2_d[ti * 128:(ti + 1) * 128, :]),
                s.dma_start(out=sin2_s[:], in_=sin2_d[ti * 128:(ti + 1) * 128, :]),
                s.dma_start(out=cost1_s[:], in_=cost1_d[ti * 128:(ti + 1) * 128, :]),
                s.dma_start(out=sint1_s[:], in_=sint1_d[ti * 128:(ti + 1) * 128, :]),
                ], n_dma=6)
            for (ca, sa) in ((cos2_s, sin2_s), (cost1_s, sint1_s)):
                seq.step("vector", lambda v, ca=ca: v.tensor_tensor(
                    out=rr1[:], in0=ca[:], in1=cdb[:], op=ALU.mult))
                seq.step("vector", lambda v, sa=sa: v.tensor_tensor(
                    out=rr2[:], in0=sa[:], in1=sdb[:], op=ALU.mult))
                seq.step("vector", lambda v: v.tensor_sub(out=rr1[:], in0=rr1[:],
                                                          in1=rr2[:]))
                seq.step("vector", lambda v, sa=sa: v.tensor_tensor(
                    out=rr2[:], in0=sa[:], in1=cdb[:], op=ALU.mult))
                seq.step("vector", lambda v, ca=ca, sa=sa: v.tensor_tensor(
                    out=sa[:], in0=ca[:], in1=sdb[:], op=ALU.mult))
                seq.step("vector", lambda v, sa=sa: v.tensor_add(
                    out=sa[:], in0=sa[:], in1=rr2[:]))
                seq.step("vector", lambda v, ca=ca: v.tensor_copy(out=ca[:],
                                                                  in_=rr1[:]))
            lo0 = max(0, W0)
            sh0 = lo0 - W0
            n0 = 128 - sh0
            if sh0 > 0:
                seq.step("vector", lambda v: v.memset(wslab[:, 0:1088], 0.0))
            seq.step("sync", lambda s, lo0=lo0, n0=n0, sh0=sh0: s.dma_start(
                out=wslab[sh0:sh0 + n0, 0:1088], in_=tab_in[lo0:lo0 + n0, :]),
                n_dma=1)
            hi0 = W0 + 128
            n1 = min(128, N - hi0)
            if n1 < 128:
                seq.step("vector", lambda v: v.memset(wslab[:, 1088:2176], 0.0))
            seq.step("sync", lambda s, hi0=hi0, n1=n1: s.dma_start(
                out=wslab[0:n1, 1088:2176], in_=tab_in[hi0:hi0 + n1, :]), n_dma=1)
            seq.step("gpsimd", lambda g: [
                g.indirect_dma_start(
                    out=ktab[:, gg * 1088:(gg + 1) * 1088],
                    out_offset=None,
                    in_=tab_in,
                    in_offset=IndirectOffsetOnAxis(seli[:, gg:gg + 1], axis=0))
                for gg in range(32)], n_dma=32)
            seq.step("gpsimd", lambda g, T=T: g.iota(
                tcol[:], pattern=[[0, 1]], base=T, channel_multiplier=1,
                allow_small_or_imprecise_dtypes=True))
            seq.step("vector", lambda v: v.tensor_scalar(
                out=qcm[:], in0=qct[:], scalar1=RSCALE, scalar2=None,
                op0=ALU.mult))
            softplus_steps(mq[:], qrt[:], t5a[:, 0:256])
            for h in range(8):
                m1s = mq[:, h * 32: h * 32 + 16]
                m2s = mq[:, h * 32 + 16: h * 32 + 32]
                seq.step("vector", lambda v, m1s=m1s: v.tensor_tensor(
                    out=t5b[:, 0:16], in0=m1s, in1=cos2_s[:], op=ALU.mult))
                seq.step("vector", lambda v, m2s=m2s: v.tensor_tensor(
                    out=t5c[:, 0:16], in0=m2s, in1=sin2_s[:], op=ALU.mult))
                seq.step("vector", lambda v: v.tensor_sub(
                    out=t5b[:, 0:16], in0=t5b[:, 0:16], in1=t5c[:, 0:16]))
                seq.step("vector", lambda v, h=h: v.tensor_scalar(
                    out=qhat[:, h * 32:h * 32 + 16], in0=t5b[:, 0:16],
                    scalar1=RSCALE, scalar2=None, op0=ALU.mult))
                seq.step("vector", lambda v, m1s=m1s: v.tensor_tensor(
                    out=t5b[:, 0:16], in0=m1s, in1=sin2_s[:], op=ALU.mult))
                seq.step("vector", lambda v, m2s=m2s: v.tensor_tensor(
                    out=t5c[:, 0:16], in0=m2s, in1=cos2_s[:], op=ALU.mult))
                seq.step("vector", lambda v: v.tensor_add(
                    out=t5b[:, 0:16], in0=t5b[:, 0:16], in1=t5c[:, 0:16]))
                seq.step("vector", lambda v, h=h: v.tensor_scalar(
                    out=qhat[:, h * 32 + 16:h * 32 + 32], in0=t5b[:, 0:16],
                    scalar1=RSCALE, scalar2=None, op0=ALU.mult))
            seq.step("vector", lambda v: v.tensor_copy(out=qhatm[:], in_=qhat[:]))
            for h in range(8):
                seq.step("tensor", lambda t, h=h: t.transpose(
                    pstm[0:64, 0:128], qcm[:, h * 64:(h + 1) * 64], identm[:]))
                seq.step("vector", lambda v, h=h: v.tensor_copy(
                    out=qcfm[0:64, h * 128:(h + 1) * 128], in_=pstm[0:64, 0:128]))
                seq.step("tensor", lambda t, h=h: t.transpose(
                    pstm[0:32, 0:128], qhatm[:, h * 32:(h + 1) * 32], identm[:]))
                seq.step("vector", lambda v, h=h: v.tensor_copy(
                    out=qhfm[0:32, h * 128:(h + 1) * 128], in_=pstm[0:32, 0:128]))
            for ss2 in range(2):
                for h in range(8):
                    seq.step("tensor", lambda t, ss2=ss2, h=h: t.transpose(
                        pstm[0:64, 0:128],
                        wslab[:, ss2 * 1088 + h * 64: ss2 * 1088 + (h + 1) * 64],
                        identm[:]))
                    seq.step("vector", lambda v, ss2=ss2, h=h: v.tensor_copy(
                        out=kwfm[0:64, h * 256 + ss2 * 128: h * 256 + ss2 * 128 + 128],
                        in_=pstm[0:64, 0:128]))
                seq.step("tensor", lambda t, ss2=ss2: t.transpose(
                    pstm[0:32, 0:128], wslab[:, ss2 * 1088 + 1056: ss2 * 1088 + 1088],
                    identm[:]))
                seq.step("vector", lambda v, ss2=ss2: v.tensor_copy(
                    out=khfm[0:32, ss2 * 128:(ss2 + 1) * 128], in_=pstm[0:32, 0:128]))
            if dbg:
                seq.step("gpsimd", lambda g: [
                    g.dma_start(out=dbg_qcfm, in_=qcfm[:]),
                    g.dma_start(out=dbg_kwfm, in_=kwfm[:]),
                    g.dma_start(out=dbg_qhfm, in_=qhfm[:]),
                    g.dma_start(out=dbg_khfm, in_=khfm[:])], n_dma=4)
            for h in range(8):
                bank = psW[h // 2]
                off = (h % 2) * 256
                def mmw(t, h=h, bank=bank, off=off):
                    t.matmul(bank[:, off:off + 256],
                             qcfm[0:64, h * 128:(h + 1) * 128],
                             kwfm[0:64, h * 256:(h + 1) * 256],
                             start=True, stop=False)
                    return t.matmul(bank[:, off:off + 256],
                                    qhfm[0:32, h * 128:(h + 1) * 128],
                                    khfm[0:32, 0:256], start=False, stop=True)
                seq.step("tensor", mmw)
            seq.step("vector", lambda v, W0=W0: v.tensor_scalar(
                out=dsc[:], in0=iotab[:], scalar1=tcol[:, 0:1],
                scalar2=float(W0), op0=ALU.subtract, op1=ALU.add))
            seq.step("vector", lambda v: v.tensor_scalar(
                out=winmU[:], in0=dsc[:], scalar1=-63.0, scalar2=None,
                op0=ALU.is_lt))
            seq.step("vector", lambda v: v.tensor_scalar(
                out=dsc[:], in0=dsc[:], scalar1=0.0, scalar2=None, op0=ALU.is_gt))
            seq.step("vector", lambda v: v.tensor_tensor(
                out=winmU[:], in0=winmU[:],
                in1=dsc[:], op=ALU.add))
            for h in range(8):
                bank = psW[h // 2]
                off = (h % 2) * 256
                seq.step("vector", lambda v, h=h, bank=bank, off=off: v.select(
                    out=scoresb[:, h * 288:h * 288 + 256], mask=winmU[:],
                    on_true=ng30[:, 0:256], on_false=bank[:, off:off + 256]))
            ctv = cost1_s[:].rearrange("p (o x) -> p o x", o=1).to_broadcast(
                [128, 32, RHALF])
            stv = sint1_s[:].rearrange("p (o x) -> p o x", o=1).to_broadcast(
                [128, 32, RHALF])
            crv = crb[:].rearrange("p (k x) -> p k x", x=RHALF)
            srv = srb[:].rearrange("p (k x) -> p k x", x=RHALF)
            cpv = cphi[:].rearrange("p (k x) -> p k x", x=RHALF)
            spv = sphi[:].rearrange("p (k x) -> p k x", x=RHALF)
            t5av = t5a[:, 0:512].rearrange("p (k x) -> p k x", x=RHALF)
            seq.step("vector", lambda v: v.tensor_tensor(out=cpv, in0=ctv, in1=crv,
                                                         op=ALU.mult))
            seq.step("vector", lambda v: v.tensor_tensor(out=t5av, in0=stv, in1=srv,
                                                         op=ALU.mult))
            seq.step("vector", lambda v: v.tensor_add(out=cphi[:], in0=cphi[:],
                                                      in1=t5a[:, 0:512]))
            seq.step("vector", lambda v: v.tensor_tensor(out=spv, in0=stv, in1=crv,
                                                         op=ALU.mult))
            seq.step("vector", lambda v: v.tensor_tensor(out=t5av, in0=ctv, in1=srv,
                                                         op=ALU.mult))
            seq.step("vector", lambda v: v.tensor_sub(out=sphi[:], in0=sphi[:],
                                                      in1=t5a[:, 0:512]))
            seq.step("vector", lambda v: v.tensor_copy(
                out=krsel[:].rearrange("p (k x) -> p k x", x=32),
                in_=ktab[:].rearrange("p (k c) -> p k c", c=1088)[:, :, 1024:1056]))
            softplus_steps(krsel[:], krsel[:], t5a[:, 0:1024])
            k1v = krsel[:].rearrange("p (k x) -> p k x", x=32)[:, :, 0:16]
            k2v = krsel[:].rearrange("p (k x) -> p k x", x=32)[:, :, 16:32]
            kkrev = kkre[:].rearrange("p (k x) -> p k x", x=RHALF)
            kkimv = kkim[:].rearrange("p (k x) -> p k x", x=RHALF)
            seq.step("vector", lambda v: v.tensor_tensor(out=kkrev, in0=k1v,
                                                         in1=cpv, op=ALU.mult))
            seq.step("vector", lambda v: v.tensor_tensor(out=t5av, in0=k2v,
                                                         in1=spv, op=ALU.mult))
            seq.step("vector", lambda v: v.tensor_add(out=kkre[:], in0=kkre[:],
                                                      in1=t5a[:, 0:512]))
            seq.step("vector", lambda v: v.tensor_tensor(out=kkimv, in0=k1v,
                                                         in1=spv, op=ALU.mult))
            seq.step("vector", lambda v: v.tensor_tensor(out=t5av, in0=k2v,
                                                         in1=cpv, op=ALU.mult))
            seq.step("vector", lambda v: v.tensor_sub(out=kkim[:], in0=kkim[:],
                                                      in1=t5a[:, 0:512]))
            for h in range(8):
                qv = qcm[:, h * 64:(h + 1) * 64].rearrange(
                    "p (o x) -> p o x", o=1).to_broadcast([128, 32, 64])
                kv = ktab[:].rearrange("p (k c) -> p k c",
                                       c=1088)[:, :, h * 64:(h + 1) * 64]
                t5a2k = t5a[:].rearrange("p (k x) -> p k x", x=64)
                seq.step("vector", lambda v, qv=qv, kv=kv, t5a2k=t5a2k:
                         v.tensor_tensor(out=t5a2k, in0=qv, in1=kv, op=ALU.mult))
                seq.step("vector", lambda v, t5a2k=t5a2k: v.tensor_reduce(
                    out=t5b[:, 0:32], in_=t5a2k, axis=AX.X, op=ALU.add))
                m1v = mq[:, h * 32:h * 32 + 16].rearrange(
                    "p (o x) -> p o x", o=1).to_broadcast([128, 32, RHALF])
                m2v = mq[:, h * 32 + 16:h * 32 + 32].rearrange(
                    "p (o x) -> p o x", o=1).to_broadcast([128, 32, RHALF])
                t5ck = t5c[:].rearrange("p (k x) -> p k x", x=RHALF)
                seq.step("vector", lambda v, m1v=m1v, t5ck=t5ck: v.tensor_tensor(
                    out=t5ck, in0=m1v, in1=kkrev, op=ALU.mult))
                seq.step("vector", lambda v, t5ck=t5ck: v.tensor_reduce(
                    out=t5b[:, 32:64], in_=t5ck, axis=AX.X, op=ALU.add))
                seq.step("vector", lambda v, m2v=m2v, t5ck=t5ck: v.tensor_tensor(
                    out=t5ck, in0=m2v, in1=kkimv, op=ALU.mult))
                seq.step("vector", lambda v, t5ck=t5ck: v.tensor_reduce(
                    out=t5b[:, 64:96], in_=t5ck, axis=AX.X, op=ALU.add))
                seq.step("vector", lambda v: v.tensor_sub(
                    out=t5b[:, 32:64], in0=t5b[:, 32:64], in1=t5b[:, 64:96]))
                seq.step("vector", lambda v: v.tensor_scalar(
                    out=t5b[:, 32:64], in0=t5b[:, 32:64], scalar1=RSCALE,
                    scalar2=None, op0=ALU.mult))
                seq.step("vector", lambda v, h=h: v.tensor_add(
                    out=scoresb[:, h * 288 + 256:h * 288 + 288],
                    in0=t5b[:, 0:32], in1=t5b[:, 32:64]))
            if dbg:
                seq.step("sync", lambda s: s.dma_start(out=dbg_sc, in_=scoresb[:]),
                         n_dma=1)
            sv = scoresb[:].rearrange("p (h x) -> p h x", x=288)
            seq.step("vector", lambda v: v.tensor_reduce(
                out=rowm[:], in_=sv, axis=AX.X, op=ALU.max))
            seq.step("vector", lambda v: v.tensor_tensor(
                out=sv, in0=sv,
                in1=rowm[:].rearrange("p (h o) -> p h o", o=1).to_broadcast(
                    [128, 8, 288]), op=ALU.subtract))
            seq.step("scalar", lambda s: s.activation(out=scoresb[:],
                                                      in_=scoresb[:], func=AF.Exp))
            seq.step("vector", lambda v: v.tensor_reduce(
                out=rows_[:], in_=sv, axis=AX.X, op=ALU.add))
            seq.step("vector", lambda v: v.reciprocal(out=rows_[:], in_=rows_[:]))
            seq.step("vector", lambda v: v.tensor_tensor(
                out=probm[:].rearrange("p (h x) -> p h x", x=288), in0=sv,
                in1=rows_[:].rearrange("p (h o) -> p h o", o=1).to_broadcast(
                    [128, 8, 288]), op=ALU.mult))
            for h in range(8):
                def trp(t, h=h):
                    t.transpose(pstm[:, 0:128], probm[:, h * 288:h * 288 + 128],
                                identm[:])
                    return t.transpose(pstm[:, 128:256],
                                       probm[:, h * 288 + 128:h * 288 + 256],
                                       identm[:])
                seq.step("tensor", trp)
                seq.step("vector", lambda v: v.tensor_copy(out=ptw[:],
                                                           in_=pstm[:, 0:256]))
                def mmv(t, h=h):
                    t.matmul(out_ps[:, h * 64:(h + 1) * 64], ptw[:, 0:128],
                             wslab[:, 512 + h * 64:512 + (h + 1) * 64],
                             start=True, stop=False)
                    return t.matmul(out_ps[:, h * 64:(h + 1) * 64], ptw[:, 128:256],
                                    wslab[:, 1088 + 512 + h * 64:1088 + 512 + (h + 1) * 64],
                                    start=False, stop=True)
                seq.step("tensor", mmv)
                pv = probm[:, h * 288 + 256:h * 288 + 288].rearrange(
                    "p (k o) -> p o k", o=1).to_broadcast([128, 64, 32])
                vv = ktab[:].rearrange("p (k c) -> p c k", c=1088)[
                    :, 512 + h * 64:512 + (h + 1) * 64, :]
                t5adk = t5a[:].rearrange("p (d k) -> p d k", k=32)
                seq.step("vector", lambda v, pv=pv, vv=vv, t5adk=t5adk:
                         v.tensor_tensor(out=t5adk, in0=pv, in1=vv, op=ALU.mult))
                seq.step("vector", lambda v, h=h, t5adk=t5adk: v.tensor_reduce(
                    out=vout[:, h * 64:(h + 1) * 64], in_=t5adk, axis=AX.X,
                    op=ALU.add))
            seq.step("vector", lambda v: v.tensor_tensor(
                out=outtm[:], in0=out_ps[:], in1=vout[:], op=ALU.add))
            if dbg:
                seq.step("vector", lambda v: v.tensor_copy(out=t5a[:, 0:512],
                                                           in_=vout[:]))
                seq.step("sync", lambda s: [
                    s.dma_start(out=dbg_out, in_=outtm[:]),
                    s.dma_start(out=dbg_vout, in_=t5a[:, 0:512])], n_dma=2)
                seq.step("gpsimd", lambda g: g.dma_start(out=dbg_prob, in_=probm[:]),
                         n_dma=1)

            if half == 0 and ti == 0:
                seq.step("sync", lambda s: s.dma_start(
                    out=slab96[:], in_=tab_in[0:128, :]), n_dma=1)
                for h in range(8):
                    m1s = mq[:, h * 32: h * 32 + 16]
                    m2s = mq[:, h * 32 + 16: h * 32 + 32]
                    seq.step("vector", lambda v, m1s=m1s: v.tensor_tensor(
                        out=t5b[:, 0:16], in0=m1s, in1=cost1_s[:], op=ALU.mult))
                    seq.step("vector", lambda v, m2s=m2s: v.tensor_tensor(
                        out=t5c[:, 0:16], in0=m2s, in1=sint1_s[:], op=ALU.mult))
                    seq.step("vector", lambda v: v.tensor_sub(
                        out=t5b[:, 0:16], in0=t5b[:, 0:16], in1=t5c[:, 0:16]))
                    seq.step("vector", lambda v, h=h: v.tensor_scalar(
                        out=q2hat[:, h * 32:h * 32 + 16], in0=t5b[:, 0:16],
                        scalar1=RSCALE, scalar2=None, op0=ALU.mult))
                    seq.step("vector", lambda v, m1s=m1s: v.tensor_tensor(
                        out=t5b[:, 0:16], in0=m1s, in1=sint1_s[:], op=ALU.mult))
                    seq.step("vector", lambda v, m2s=m2s: v.tensor_tensor(
                        out=t5c[:, 0:16], in0=m2s, in1=cost1_s[:], op=ALU.mult))
                    seq.step("vector", lambda v: v.tensor_add(
                        out=t5b[:, 0:16], in0=t5b[:, 0:16], in1=t5c[:, 0:16]))
                    seq.step("vector", lambda v, h=h: v.tensor_scalar(
                        out=q2hat[:, h * 32 + 16:h * 32 + 32], in0=t5b[:, 0:16],
                        scalar1=RSCALE, scalar2=None, op0=ALU.mult))
                seq.step("vector", lambda v: v.tensor_copy(out=q2m[:], in_=q2hat[:]))
                for h in range(8):
                    seq.step("tensor", lambda t, h=h: t.transpose(
                        pstm[0:32, 0:128], q2m[:, h * 32:(h + 1) * 32], identm[:]))
                    seq.step("vector", lambda v, h=h: v.tensor_copy(
                        out=q2fm[0:32, h * 128:(h + 1) * 128], in_=pstm[0:32, 0:128]))
                    seq.step("tensor", lambda t, h=h: t.transpose(
                        pstm[0:64, 0:128], slab96[:, h * 64:(h + 1) * 64], identm[:]))
                    seq.step("vector", lambda v, h=h: v.tensor_copy(
                        out=k2cfm[0:64, h * 128:(h + 1) * 128], in_=pstm[0:64, 0:128]))
                seq.step("vector", lambda v: v.tensor_copy(
                    out=kr96[:], in_=slab96[:, 1024:1056]))
                softplus_steps(kr96[:], kr96[:], t5a[:, 0:32])
                seq.step("vector", lambda v: v.tensor_tensor(
                    out=k2rot[:, 0:16], in0=kr96[:, 0:16], in1=cosjj_s[:],
                    op=ALU.mult))
                seq.step("vector", lambda v: v.tensor_tensor(
                    out=t5c[:, 0:16], in0=kr96[:, 16:32], in1=sinjj_s[:],
                    op=ALU.mult))
                seq.step("vector", lambda v: v.tensor_sub(
                    out=k2rot[:, 0:16], in0=k2rot[:, 0:16], in1=t5c[:, 0:16]))
                seq.step("vector", lambda v: v.tensor_tensor(
                    out=k2rot[:, 16:32], in0=kr96[:, 0:16], in1=sinjj_s[:],
                    op=ALU.mult))
                seq.step("vector", lambda v: v.tensor_tensor(
                    out=t5c[:, 0:16], in0=kr96[:, 16:32], in1=cosjj_s[:],
                    op=ALU.mult))
                seq.step("vector", lambda v: v.tensor_add(
                    out=k2rot[:, 16:32], in0=k2rot[:, 16:32], in1=t5c[:, 0:16]))
                seq.step("vector", lambda v: v.tensor_copy(out=k2m[:], in_=k2rot[:]))
                seq.step("tensor", lambda t: t.transpose(
                    pstm[0:32, 0:128], k2m[:], identm[:]))
                seq.step("vector", lambda v: v.tensor_copy(
                    out=k2fm[0:32, 0:128], in_=pstm[0:32, 0:128]))
                for h in range(8):
                    bank = psW[h // 2]
                    off = (h % 2) * 128
                    def mm2(t, h=h, bank=bank, off=off):
                        t.matmul(bank[:, off:off + 128],
                                 qcfm[0:64, h * 128:(h + 1) * 128],
                                 k2cfm[0:64, h * 128:(h + 1) * 128],
                                 start=True, stop=False)
                        return t.matmul(bank[:, off:off + 128],
                                        q2fm[0:32, h * 128:(h + 1) * 128],
                                        k2fm[0:32, 0:128], start=False, stop=True)
                    seq.step("tensor", mm2)
                seq.step("vector", lambda v: v.tensor_scalar(
                    out=jmU[:], in0=iotab[:, 0:128], scalar1=95.5, scalar2=None,
                    op0=ALU.is_gt))
                for h in range(8):
                    bank = psW[h // 2]
                    off = (h % 2) * 128
                    seq.step("vector", lambda v, h=h, bank=bank, off=off: v.select(
                        out=sc2[:, h * 128:(h + 1) * 128], mask=jmU[:],
                        on_true=ng30[:, 0:128], on_false=bank[:, off:off + 128]))
                sv2 = sc2[:].rearrange("p (h x) -> p h x", x=128)
                seq.step("vector", lambda v: v.tensor_reduce(
                    out=rowm2[:], in_=sv2, axis=AX.X, op=ALU.max))
                seq.step("vector", lambda v: v.tensor_tensor(
                    out=sv2, in0=sv2,
                    in1=rowm2[:].rearrange("p (h o) -> p h o", o=1).to_broadcast(
                        [128, 8, 128]), op=ALU.subtract))
                seq.step("scalar", lambda s: s.activation(out=sc2[:], in_=sc2[:],
                                                          func=AF.Exp))
                seq.step("vector", lambda v: v.tensor_reduce(
                    out=rowm2[:], in_=sv2, axis=AX.X, op=ALU.add))
                seq.step("vector", lambda v: v.reciprocal(out=rowm2[:],
                                                          in_=rowm2[:]))
                seq.step("vector", lambda v: v.tensor_tensor(
                    out=pr2m[:].rearrange("p (h x) -> p h x", x=128), in0=sv2,
                    in1=rowm2[:].rearrange("p (h o) -> p h o", o=1).to_broadcast(
                        [128, 8, 128]), op=ALU.mult))
                for h in range(8):
                    seq.step("tensor", lambda t, h=h: t.transpose(
                        pstm[:, 0:128], pr2m[:, h * 128:(h + 1) * 128], identm[:]))
                    seq.step("vector", lambda v: v.tensor_copy(
                        out=ptw[:, 0:128], in_=pstm[:, 0:128]))
                    seq.step("tensor", lambda t, h=h: t.matmul(
                        out_ps[:, h * 64:(h + 1) * 64], ptw[:, 0:128],
                        slab96[:, 512 + h * 64:512 + (h + 1) * 64],
                        start=True, stop=True))
                seq.step("vector", lambda v: v.tensor_copy(out=vout2[:],
                                                           in_=out_ps[:]))
                seq.step("vector", lambda v: v.tensor_scalar(
                    out=rmask[:], in0=tcol[:], scalar1=62.5, scalar2=None,
                    op0=ALU.is_lt))
                seq.step("vector", lambda v: v.select(
                    out=outtm[:], mask=rmask[:].to_broadcast([128, 512]),
                    on_true=vout2[:], on_false=outtm[:]))

            seq.step("vector", lambda v: v.tensor_copy(out=outm[:], in_=outtm[:]))
            for ec in range(4):
                seq.step("tensor", lambda t, ec=ec: t.transpose(
                    pstm[:, 0:128], outm[:, ec * 128:(ec + 1) * 128], identm[:]))
                seq.step("vector", lambda v, ec=ec: v.tensor_copy(
                    out=outfm[:, ec * 128:(ec + 1) * 128], in_=pstm[:, 0:128]))
            seq.step("sync", lambda s, ti=ti: [
                s.dma_start(out=x2f[:, c * 128:(c + 1) * 128],
                            in_=x2_in[c * 128:(c + 1) * 128,
                                      ti * 128:(ti + 1) * 128])
                for c in range(8)], n_dma=8)
            for dc in range(8):
                def mmo(t, dc=dc):
                    last = None
                    for ec in range(4):
                        last = t.matmul(
                            out_ps[:, 0:128],
                            wout_s[:, ec * 1024 + dc * 128: ec * 1024 + dc * 128 + 128],
                            outfm[:, ec * 128:(ec + 1) * 128],
                            start=(ec == 0), stop=(ec == 3))
                    return last
                seq.step("tensor", mmo)
                seq.step("vector", lambda v, dc=dc: v.tensor_tensor(
                    out=z1[:, dc * 128:(dc + 1) * 128], in0=out_ps[:, 0:128],
                    in1=x2f[:, dc * 128:(dc + 1) * 128], op=ALU.add))
            G = 128
            for c in range(8):
                seq.step("tensor", lambda t, c=c: t.matmul(
                    psb[0:1, 0:G] if False else psb[:, 128:129],
                    onescol[:], z1[:, c * G:(c + 1) * G],
                    start=(c == 0), stop=(c == 7)) if False else t.matmul(
                    psb[0:1, 128:256][0:1, 0:128] if False else psb[0:1, 128:256],
                    onescol[:], z1[:, c * G:(c + 1) * G],
                    start=(c == 0), stop=(c == 7)))
            seq.step("vector", lambda v: v.tensor_scalar(
                out=rowt[0:1, 0:G], in0=psb[0:1, 128:256], scalar1=1.0 / D,
                scalar2=None, op0=ALU.mult))
            for c in range(8):
                seq.step("scalar", lambda s, c=c: s.activation(
                    out=sqt[:], in_=z1[:, c * G:(c + 1) * G], func=AF.Square))
                seq.step("tensor", lambda t, c=c: t.matmul(
                    psb[0:1, 128:256], onescol[:], sqt[:],
                    start=(c == 0), stop=(c == 7)))
            seq.step("vector", lambda v: v.tensor_scalar(
                out=rowt[0:1, G:2 * G], in0=psb[0:1, 128:256], scalar1=1.0 / D,
                scalar2=None, op0=ALU.mult))
            seq.step("scalar", lambda s: s.activation(
                out=rowt[0:1, 2 * G:3 * G], in_=rowt[0:1, 0:G], func=AF.Square))
            seq.step("vector", lambda v: v.tensor_sub(
                out=rowt[0:1, G:2 * G], in0=rowt[0:1, G:2 * G],
                in1=rowt[0:1, 2 * G:3 * G]))
            seq.step("vector", lambda v: v.tensor_scalar(
                out=rowt[0:1, G:2 * G], in0=rowt[0:1, G:2 * G], scalar1=LNEPS,
                scalar2=None, op0=ALU.add))
            seq.step("scalar", lambda s: s.activation(
                out=rowt[0:1, 3 * G:4 * G], in_=rowt[0:1, G:2 * G], func=AF.Sqrt))
            seq.step("vector", lambda v: v.reciprocal(
                out=rowt[0:1, 4 * G:5 * G], in_=rowt[0:1, 3 * G:4 * G]))
            seq.step("tensor", lambda t: t.matmul(
                psb[:, 0:128], ones1[:], rowt[0:1, 0:G], start=True, stop=True))
            seq.step("vector", lambda v: v.tensor_copy(out=mub[:],
                                                       in_=psb[:, 0:128]))
            seq.step("tensor", lambda t: t.matmul(
                psb[:, 0:128], ones1[:], rowt[0:1, 4 * G:5 * G],
                start=True, stop=True))
            seq.step("vector", lambda v: v.tensor_copy(out=rsb[:],
                                                       in_=psb[:, 0:128]))
            for c in range(8):
                seq.step("vector", lambda v, c=c: v.tensor_sub(
                    out=z1[:, c * G:(c + 1) * G], in0=z1[:, c * G:(c + 1) * G],
                    in1=mub[:]))
                seq.step("vector", lambda v, c=c: v.tensor_tensor(
                    out=z1[:, c * G:(c + 1) * G], in0=z1[:, c * G:(c + 1) * G],
                    in1=rsb[:], op=ALU.mult))
            seq.step("sync", lambda s, ti=ti: [
                s.dma_start(out=x3_o[c * 128:(c + 1) * 128,
                                     ti * 128:(ti + 1) * 128],
                            in_=z1[:, c * 128:(c + 1) * 128])
                for c in range(8)], n_dma=8)

        total = seq.run(block, gs)

        @block.sync
        def _(s):
            s.wait_ge(gs, total)
    return nc


def stageF_inmaps(sel_cores, tab_batches, qc_cores, qr_cores, x2_cores, W_outL1,
                  raw_delta):
    t = np.arange(TPC, dtype=np.float64)
    maps = []
    for c in range(NCORES):
        b, half = c // 2, c % 2
        tt = t + half * TPC
        maps.append(dict(
            sel=np.ascontiguousarray(sel_cores[c]),
            tab_b=np.ascontiguousarray(tab_batches[b]),
            qc_tm=np.ascontiguousarray(qc_cores[c]),
            qr_tm=np.ascontiguousarray(qr_cores[c]),
            x2_fm=np.ascontiguousarray(x2_cores[c]),
            woutL1=np.ascontiguousarray(W_outL1),
            cos2=np.cos(2 * tt[:, None] * THETA[None, :]).astype(np.float32),
            sin2=np.sin(2 * tt[:, None] * THETA[None, :]).astype(np.float32),
            cost1=np.cos(tt[:, None] * THETA[None, :]).astype(np.float32),
            sint1=np.sin(tt[:, None] * THETA[None, :]).astype(np.float32),
            cosr=np.cos((64 + np.arange(32, dtype=np.float64))[:, None]
                        * THETA[None, :]).astype(np.float32).reshape(1, 512),
            sinr=np.sin((64 + np.arange(32, dtype=np.float64))[:, None]
                        * THETA[None, :]).astype(np.float32).reshape(1, 512),
            cosjj=np.cos(np.arange(128, dtype=np.float64)[:, None]
                         * THETA[None, :]).astype(np.float32),
            sinjj=np.sin(np.arange(128, dtype=np.float64)[:, None]
                         * THETA[None, :]).astype(np.float32),
            rdel=np.ascontiguousarray(raw_delta.reshape(1, RHALF)),
            ident_d=np.eye(128, dtype=np.float32),
        ))
    return maps




def build_stageG():
    MMDT = BF16 if MM_BF16 else F32
    nc = bass.Bass("TRN2", target_bir_lowering=False)
    x3_in = nc.dram_tensor("x3_fm", [D, TPC], F32, kind="ExternalInput").ap()
    gwt_d = nc.dram_tensor("gwt", [D, 8], F32, kind="ExternalInput").ap()
    gb_d = nc.dram_tensor("gb", [1, 8], F32, kind="ExternalInput").ap()
    w1t_d = nc.dram_tensor("w1t", [8, D, 2048], F32, kind="ExternalInput").ap()
    w3t_d = nc.dram_tensor("w3t", [8, D, 2048], F32, kind="ExternalInput").ap()
    w2t_d = nc.dram_tensor("w2t", [8, 2048, D], F32, kind="ExternalInput").ap()
    sw1t_d = nc.dram_tensor("sw1t", [D, 2048], F32, kind="ExternalInput").ap()
    sw3t_d = nc.dram_tensor("sw3t", [D, 2048], F32, kind="ExternalInput").ap()
    sw2t_d = nc.dram_tensor("sw2t", [2048, D], F32, kind="ExternalInput").ap()
    ident_d = nc.dram_tensor("ident_d", [128, 128], F32, kind="ExternalInput").ap()
    y_o = nc.dram_tensor("yout", [TPC, D], F32, kind="ExternalOutput").ap()
    ydram = nc.dram_tensor("ydram", [D, TPC], F32, kind="Internal").ap()
    combdram = nc.dram_tensor("combdram", [8, TPC], F32, kind="Internal").ap()

    G = 512
    NG = TPC // G
    FC = 16

    from contextlib import ExitStack
    with ExitStack() as ctx:
        E = ctx.enter_context
        w1s = E(nc.sbuf_tensor("w1s", [128, 8 * 2048], MMDT))
        w3s = E(nc.sbuf_tensor("w3s", [128, 8 * 2048], MMDT))
        w2s = E(nc.sbuf_tensor("w2s", [128, 16 * 1024], MMDT))
        gwt_s = E(nc.sbuf_tensor("gwt_s", [128, 8 * 8], MMDT))
        gbrow = E(nc.sbuf_tensor("gbrow", [1, 8], F32))
        gb_b = E(nc.sbuf_tensor("gb_b", [128, 8], F32))
        iota8 = E(nc.sbuf_tensor("iota8", [128, 8], F32))
        ident = E(nc.sbuf_tensor("ident", [128, 128], F32))
        identm = E(nc.sbuf_tensor("identm", [128, 128], MMDT))
        ones1 = E(nc.sbuf_tensor("ones1", [1, 128], F32))
        onescol = E(nc.sbuf_tensor("onescol", [128, 1], F32))
        x3mg = E(nc.sbuf_tensor("x3mg", [128, 8 * G], MMDT))
        hm = E(nc.sbuf_tensor("hm", [128, FC * G], MMDT))
        hsil = E(nc.sbuf_tensor("hsil", [128, G], F32))
        ydf = E(nc.sbuf_tensor("ydf", [128, 8 * G], F32))
        lgt = E(nc.sbuf_tensor("lgt", [128, 8], F32))
        sct = E(nc.sbuf_tensor("sct", [128, 8], F32))
        bia = E(nc.sbuf_tensor("bia", [128, 8], F32))
        m8 = E(nc.sbuf_tensor("m8", [128, 8], F32))
        m8i = E(nc.sbuf_tensor("m8i", [128, 8], U32))
        idxf = E(nc.sbuf_tensor("idxf", [128, 2], F32))
        eqt = E(nc.sbuf_tensor("eqt", [128, 8], F32))
        wr_ = E(nc.sbuf_tensor("wr_", [128, 1], F32))
        comb = E(nc.sbuf_tensor("comb", [128, 8], F32))
        combm = E(nc.sbuf_tensor("combm", [128, 8], MMDT))
        combT = E(nc.sbuf_tensor("combT", [8, TPC], F32))
        combrow = E(nc.sbuf_tensor("combrow", [1, G], F32))
        comb_eb = E(nc.sbuf_tensor("comb_eb", [128, G], F32))
        rowsum = E(nc.sbuf_tensor("rowsum", [128, 1], F32))
        x3f = E(nc.sbuf_tensor("x3f", [128, G], F32))
        z1 = E(nc.sbuf_tensor("z1", [128, 8 * G], F32))
        sqt = E(nc.sbuf_tensor("sqt", [128, G], F32))
        rowt = E(nc.sbuf_tensor("rowt", [1, 6 * G], F32))
        mub = E(nc.sbuf_tensor("mub", [128, G], F32))
        rsb = E(nc.sbuf_tensor("rsb", [128, G], F32))
        otm = E(nc.sbuf_tensor("otm", [128, D], F32))
        psh = E(nc.psum_tensor("psh", [128, G], F32))
        psy = E(nc.psum_tensor("psy", [128, G], F32))
        psg = E(nc.psum_tensor("psg", [128, 8], F32))
        pst = E(nc.psum_tensor("pst", [128, G], F32))
        psb = E(nc.psum_tensor("psb", [128, G], F32))
        psrow = E(nc.psum_tensor("psrow", [1, G], F32))
        pstm = E(nc.psum_tensor("pstm", [128, 128], MMDT))
        gs = E(nc.semaphore("gs"))
        block = E(nc.Block())

        seq = Seq(nc)
        seq.step("vector", lambda v: v.memset(ones1[:], 1.0))
        seq.step("vector", lambda v: v.memset(onescol[:], 1.0))
        seq.step("gpsimd", lambda g: g.iota(
            iota8[:], pattern=[[1, 8]], base=0, channel_multiplier=0,
            allow_small_or_imprecise_dtypes=True))
        seq.step("sync", lambda s: [s.dma_start(out=ident[:], in_=ident_d),
                                    s.dma_start(out=gbrow[:], in_=gb_d)], n_dma=2)
        seq.step("vector", lambda v: v.tensor_copy(out=identm[:], in_=ident[:]))
        seq.step("gpsimd", lambda g: [
            g.dma_start(out=gwt_s[:, c * 8:(c + 1) * 8],
                        in_=gwt_d[c * 128:(c + 1) * 128, :]) for c in range(8)],
            n_dma=8)
        seq.step("tensor", lambda t: t.matmul(psg[:], ones1[:], gbrow[:],
                                              start=True, stop=True))
        seq.step("vector", lambda v: v.tensor_copy(out=gb_b[:], in_=psg[:]))

        for sub in range(TPC // 128):
            seq.step("gpsimd", lambda g, sub=sub: [
                g.dma_start(out=x3mg[:, c * 128:(c + 1) * 128] if False else
                            x3mg[:, c * G:c * G + 128],
                            in_=x3_in[c * 128:(c + 1) * 128,
                                      sub * 128:(sub + 1) * 128])
                for c in range(8)], n_dma=8)
            def mmg(t, sub=sub):
                last = None
                for c in range(8):
                    last = t.matmul(psg[:], x3mg[:, c * G:c * G + 128],
                                    gwt_s[:, c * 8:(c + 1) * 8],
                                    start=(c == 0), stop=(c == 7))
                return last
            seq.step("tensor", mmg)
            seq.step("vector", lambda v: v.tensor_copy(out=lgt[:], in_=psg[:]))
            seq.step("vector", lambda v: v.tensor_reduce(
                out=rowsum[:], in_=lgt[:], axis=AX.X, op=ALU.max))
            seq.step("vector", lambda v: v.tensor_scalar(
                out=lgt[:], in0=lgt[:], scalar1=rowsum[:, 0:1], scalar2=None,
                op0=ALU.subtract))
            seq.step("scalar", lambda s: s.activation(out=sct[:], in_=lgt[:],
                                                      func=AF.Exp))
            seq.step("vector", lambda v: v.tensor_reduce(
                out=rowsum[:], in_=sct[:], axis=AX.X, op=ALU.add))
            seq.step("vector", lambda v: v.reciprocal(out=rowsum[:],
                                                      in_=rowsum[:]))
            seq.step("vector", lambda v: v.tensor_scalar(
                out=sct[:], in0=sct[:], scalar1=rowsum[:, 0:1], scalar2=None,
                op0=ALU.mult))
            seq.step("vector", lambda v: v.tensor_add(out=bia[:], in0=sct[:],
                                                      in1=gb_b[:]))
            seq.step("vector", lambda v: v.max(out=m8[:], in_=bia[:]))
            seq.step("vector", lambda v: v.max_index(out=m8i[:], in_max=m8[:],
                                                     in_values=bia[:]))
            seq.step("vector", lambda v: v.tensor_copy(out=idxf[:],
                                                       in_=m8i[:, 0:2]))
            seq.step("vector", lambda v: v.memset(comb[:], 0.0))
            for rr in range(2):
                seq.step("vector", lambda v, rr=rr: v.tensor_scalar(
                    out=eqt[:], in0=iota8[:], scalar1=idxf[:, rr:rr + 1],
                    scalar2=None, op0=ALU.is_equal))
                seq.step("vector", lambda v: v.tensor_tensor(
                    out=eqt[:], in0=eqt[:], in1=sct[:], op=ALU.mult))
                seq.step("vector", lambda v: v.tensor_add(
                    out=comb[:], in0=comb[:], in1=eqt[:]))
            seq.step("vector", lambda v: v.tensor_copy(out=combm[:], in_=comb[:]))
            seq.step("tensor", lambda t: t.transpose(
                pstm[0:8, 0:128], combm[:], identm[:]))
            seq.step("vector", lambda v, sub=sub: v.tensor_copy(
                out=combT[0:8, sub * 128:(sub + 1) * 128], in_=pstm[0:8, 0:128]))
        seq.step("sync", lambda s: s.dma_start(out=combdram, in_=combT[0:8, :]),
                 n_dma=1)

        def expert_pass(widx, scaled):
            if widx == "shared":
                srcs = (sw1t_d, sw3t_d, sw2t_d)
                first = False
            else:
                srcs = (w1t_d[widx], w3t_d[widx], w2t_d[widx])
                first = (widx == 0)
            seq.step("gpsimd", lambda g, srcs=srcs: [
                g.dma_start(out=w1s[:, c * 2048:(c + 1) * 2048],
                            in_=srcs[0][c * 128:(c + 1) * 128, :])
                for c in range(8)], n_dma=8)
            seq.step("gpsimd", lambda g, srcs=srcs: [
                g.dma_start(out=w3s[:, c * 2048:(c + 1) * 2048],
                            in_=srcs[1][c * 128:(c + 1) * 128, :])
                for c in range(8)], n_dma=8)
            seq.step("gpsimd", lambda g, srcs=srcs: [
                g.dma_start(out=w2s[:, c * 1024:(c + 1) * 1024],
                            in_=srcs[2][c * 128:(c + 1) * 128, :])
                for c in range(16)], n_dma=16)
            for gi in range(NG):
                seq.step("gpsimd", lambda g, gi=gi: [
                    g.dma_start(out=x3mg[:, c * G:(c + 1) * G],
                                in_=x3_in[c * 128:(c + 1) * 128,
                                          gi * G:(gi + 1) * G])
                    for c in range(8)], n_dma=8)
                if scaled:
                    seq.step("sync", lambda s, gi=gi, widx=widx: s.dma_start(
                        out=combrow[:], in_=combdram[widx:widx + 1,
                                                     gi * G:(gi + 1) * G]),
                        n_dma=1)
                    seq.step("tensor", lambda t: t.matmul(
                        psb[:], ones1[:], combrow[:], start=True, stop=True))
                    seq.step("vector", lambda v: v.tensor_copy(out=comb_eb[:],
                                                               in_=psb[:]))
                for fc in range(FC):
                    def mmh1(t, fc=fc):
                        last = None
                        for c in range(8):
                            last = t.matmul(
                                psh[:],
                                w1s[:, c * 2048 + fc * 128: c * 2048 + fc * 128 + 128],
                                x3mg[:, c * G:(c + 1) * G],
                                start=(c == 0), stop=(c == 7))
                        return last
                    seq.step("tensor", mmh1)
                    seq.step("scalar", lambda s: s.activation(
                        out=hsil[:], in_=psh[:], func=AF.Silu))
                    def mmh3(t, fc=fc):
                        last = None
                        for c in range(8):
                            last = t.matmul(
                                psh[:],
                                w3s[:, c * 2048 + fc * 128: c * 2048 + fc * 128 + 128],
                                x3mg[:, c * G:(c + 1) * G],
                                start=(c == 0), stop=(c == 7))
                        return last
                    seq.step("tensor", mmh3)
                    seq.step("vector", lambda v, fc=fc: v.tensor_tensor(
                        out=hm[:, fc * G:(fc + 1) * G], in0=hsil[:], in1=psh[:],
                        op=ALU.mult))
                for dc in range(8):
                    def mmy(t, dc=dc):
                        last = None
                        for fc in range(FC):
                            last = t.matmul(
                                psy[:],
                                w2s[:, fc * 1024 + dc * 128: fc * 1024 + dc * 128 + 128],
                                hm[:, fc * G:(fc + 1) * G],
                                start=(fc == 0), stop=(fc == FC - 1))
                        return last
                    seq.step("tensor", mmy)
                    if scaled:
                        seq.step("vector", lambda v, dc=dc: v.tensor_tensor(
                            out=ydf[:, dc * G:(dc + 1) * G], in0=psy[:],
                            in1=comb_eb[:], op=ALU.mult))
                    else:
                        seq.step("vector", lambda v, dc=dc: v.tensor_copy(
                            out=ydf[:, dc * G:(dc + 1) * G], in_=psy[:]))
                seq.step("gpsimd", lambda g, gi=gi, first=first: [
                    g.dma_start(out=ydram[dc * 128:(dc + 1) * 128,
                                          gi * G:(gi + 1) * G],
                                in_=ydf[:, dc * G:(dc + 1) * G],
                                accum_op=(ALU.bypass if first else ALU.add))
                    for dc in range(8)], n_dma=8)

        for e in range(8):
            expert_pass(e, True)
        expert_pass("shared", False)

        for gi in range(NG):
            seq.step("sync", lambda s, gi=gi: [
                s.dma_start(out=z1[:, c * G:(c + 1) * G],
                            in_=ydram[c * 128:(c + 1) * 128, gi * G:(gi + 1) * G])
                for c in range(8)], n_dma=8)
            for c in range(8):
                seq.step("sync", lambda s, c=c, gi=gi: s.dma_start(
                    out=x3f[:], in_=x3_in[c * 128:(c + 1) * 128,
                                          gi * G:(gi + 1) * G]), n_dma=1)
                seq.step("vector", lambda v, c=c: v.tensor_add(
                    out=z1[:, c * G:(c + 1) * G], in0=z1[:, c * G:(c + 1) * G],
                    in1=x3f[:]))
            for c in range(8):
                seq.step("tensor", lambda t, c=c: t.matmul(
                    psrow[:], onescol[:], z1[:, c * G:(c + 1) * G],
                    start=(c == 0), stop=(c == 7)))
            seq.step("vector", lambda v: v.tensor_scalar(
                out=rowt[0:1, 0:G], in0=psrow[:], scalar1=1.0 / D, scalar2=None,
                op0=ALU.mult))
            for c in range(8):
                seq.step("scalar", lambda s, c=c: s.activation(
                    out=sqt[:], in_=z1[:, c * G:(c + 1) * G], func=AF.Square))
                seq.step("tensor", lambda t, c=c: t.matmul(
                    psrow[:], onescol[:], sqt[:], start=(c == 0), stop=(c == 7)))
            seq.step("vector", lambda v: v.tensor_scalar(
                out=rowt[0:1, G:2 * G], in0=psrow[:], scalar1=1.0 / D,
                scalar2=None, op0=ALU.mult))
            seq.step("scalar", lambda s: s.activation(
                out=rowt[0:1, 2 * G:3 * G], in_=rowt[0:1, 0:G], func=AF.Square))
            seq.step("vector", lambda v: v.tensor_sub(
                out=rowt[0:1, G:2 * G], in0=rowt[0:1, G:2 * G],
                in1=rowt[0:1, 2 * G:3 * G]))
            seq.step("vector", lambda v: v.tensor_scalar(
                out=rowt[0:1, G:2 * G], in0=rowt[0:1, G:2 * G], scalar1=LNEPS,
                scalar2=None, op0=ALU.add))
            seq.step("scalar", lambda s: s.activation(
                out=rowt[0:1, 3 * G:4 * G], in_=rowt[0:1, G:2 * G], func=AF.Sqrt))
            seq.step("vector", lambda v: v.reciprocal(
                out=rowt[0:1, 4 * G:5 * G], in_=rowt[0:1, 3 * G:4 * G]))
            seq.step("tensor", lambda t: t.matmul(
                psb[:], ones1[:], rowt[0:1, 0:G], start=True, stop=True))
            seq.step("vector", lambda v: v.tensor_copy(out=mub[:], in_=psb[:]))
            seq.step("tensor", lambda t: t.matmul(
                psb[:], ones1[:], rowt[0:1, 4 * G:5 * G], start=True, stop=True))
            seq.step("vector", lambda v: v.tensor_copy(out=rsb[:], in_=psb[:]))
            for c in range(8):
                seq.step("vector", lambda v, c=c: v.tensor_sub(
                    out=z1[:, c * G:(c + 1) * G], in0=z1[:, c * G:(c + 1) * G],
                    in1=mub[:]))
                seq.step("vector", lambda v, c=c: v.tensor_tensor(
                    out=z1[:, c * G:(c + 1) * G], in0=z1[:, c * G:(c + 1) * G],
                    in1=rsb[:], op=ALU.mult))
            for sub in range(4):
                for c in range(8):
                    seq.step("tensor", lambda t, sub=sub, c=c: t.transpose(
                        pst[:, 0:128], z1[:, c * G + sub * 128: c * G + (sub + 1) * 128],
                        ident[:]))
                    seq.step("vector", lambda v, c=c: v.tensor_copy(
                        out=otm[:, c * 128:(c + 1) * 128], in_=pst[:, 0:128]))
                seq.step("sync", lambda s, gi=gi, sub=sub: s.dma_start(
                    out=y_o[gi * G + sub * 128: gi * G + (sub + 1) * 128, :],
                    in_=otm[:]), n_dma=1)

        total = seq.run(block, gs)

        @block.sync
        def _(s):
            s.wait_ge(gs, total)
    return nc


def stageG_inmaps(x3_cores, gate_w, gate_bias, w1, w2, w3, sw1, sw2, sw3):
    com = dict(
        gwt=np.ascontiguousarray(gate_w.T),
        gb=np.ascontiguousarray(gate_bias.reshape(1, 8)),
        w1t=np.ascontiguousarray(w1.transpose(0, 2, 1)),
        w3t=np.ascontiguousarray(w3.transpose(0, 2, 1)),
        w2t=np.ascontiguousarray(w2.transpose(0, 2, 1)),
        sw1t=np.ascontiguousarray(sw1.T),
        sw3t=np.ascontiguousarray(sw3.T),
        sw2t=np.ascontiguousarray(sw2.T),
        ident_d=np.eye(128, dtype=np.float32),
    )
    maps = []
    for c in range(NCORES):
        m = dict(com)
        m["x3_fm"] = np.ascontiguousarray(x3_cores[c])
        maps.append(m)
    return maps




def build_fused():
    MMDT = BF16 if MM_BF16 else F32
    nc = bass.Bass("TRN2", target_bir_lowering=False)
    I = lambda nm, shp, dt=F32: nc.dram_tensor(nm, shp, dt, kind="ExternalInput").ap()
    S = lambda nm, shp, dt=F32: nc.dram_tensor(nm, shp, dt, kind="Internal").ap()

    xb = I("xb", [N, D])
    wq_d, wk_d, wv_d = I("wq", [D, DI]), I("wk", [D, DI]), I("wv", [D, DI])
    ware_d, waim_d = I("ware", [D, DI]), I("waim", [D, DI])
    bq_d, bk_d = I("bq", [128, 4]), I("bk", [128, 4])
    bare_d, baim_d = I("bare", [128, 4]), I("baim", [128, 4])
    bv_d, gam_d = I("bv", [1, DI]), I("gam", [128, 8])
    ident_d, masku_d = I("ident_d", [128, 128]), I("masku_d", [128, 128])
    wout0_d, bout0_d = I("wout0", [DI, D]), I("bout0", [128, 8])
    w1t_d, b1_d = I("f_w1t", [D, 4096]), I("f_b1", [128, 32])
    w2t_d, b2_d = I("f_w2t", [4096, D]), I("f_b2", [128, 8])
    wdkv_d, wuk_d, wuv_d = I("wdkv", [D, 256]), I("wuk", [256, 512]), I("wuv", [256, 512])
    wdq_d, wuq_d, wqr_d = I("wdq", [D, 512]), I("wuq", [512, 512]), I("wqr", [512, 256])
    wkr_d, wiq_d, wik_d = I("wkr", [D, 32]), I("wiq", [D, 256]), I("wik", [D, 64])
    idxw_d = I("idxw", [1, 4])
    cosj_d, sinj_d = I("cosj", [N, RHALF]), I("sinj", [N, RHALF])
    rdel_d = I("rdel", [1, RHALF])
    wout1_d = I("wout1", [512, D])
    tcol_d = I("tcolin", [TPC, 1])
    cos2_d, sin2_d = I("cos2", [TPC, RHALF]), I("sin2", [TPC, RHALF])
    cost1_d, sint1_d = I("cost1", [TPC, RHALF]), I("sint1", [TPC, RHALF])
    cosr_d, sinr_d = I("cosr", [1, 512]), I("sinr", [1, 512])
    cosjj_d, sinjj_d = I("cosjj", [128, RHALF]), I("sinjj", [128, RHALF])
    gwt_d, gb_d = I("gwt", [D, 8]), I("gb", [1, 8])
    mw1t_d, mw3t_d = I("mw1t", [8, D, 2048]), I("mw3t", [8, D, 2048])
    mw2t_d = I("mw2t", [8, 2048, D])
    sw1t_d, sw3t_d = I("sw1t", [D, 2048]), I("sw3t", [D, 2048])
    sw2t_d = I("sw2t", [2048, D])
    y_o = nc.dram_tensor("yout", [TPC, D], F32, kind="ExternalOutput").ap()

    qs_s_d, k_s_d = S("qs_scr", [DI, N]), S("k_scr", [DI, N])
    ls_s_d, th_s_d = S("ls_scr", [DI, N]), S("th_scr", [DI, N])
    v_s_d = S("v_scr", [N, DI])
    attn_s_d = S("attn_scr", [N, DI])
    x2f_s_d, x2t_s_d = S("x2f_scr", [D, N]), S("x2t_scr", [N, D])
    tab_s_d = S("tab_scr", [N, 1088], BF16)
    ki_s_d = S("ki_scr", [64, N])
    qit_s_d = S("qit_scr", [N, 256])
    qc_s_d, qr_s_d = S("qc_scr", [N, 512]), S("qr_scr", [N, 256])
    sel_s_d = S("sel_scr", [TPC, 32])
    idram = S("idram_scr", [128, N])
    x3f_s_d = S("x3f_scr", [D, TPC])
    ydram = S("ydram_scr", [D, TPC])
    combdram = S("combdram_scr", [8, TPC])

    from contextlib import ExitStack
    sems = []
    entry = None

    def segment():
        return Seq(nc)

    with ExitStack() as octx:
        block = octx.enter_context(nc.Block())

        def finish(seq, ctx_name, aux_name=None):
            nonlocal entry
            sem = octx.enter_context(nc.semaphore(ctx_name))
            aux_sem = (octx.enter_context(nc.semaphore(aux_name))
                       if aux_name else None)
            total = seq.run(block, sem, entry=entry, aux_sem=aux_sem)
            entry = (sem, total)

        with ExitStack() as ctx:
            E = ctx.enter_context
            G = 512
            NG = N // G
            nsub = 4
            wf = E(nc.sbuf_tensor("A_wf", [128, 8 * DI], F32))
            ws_q = E(nc.sbuf_tensor("A_wsq", [128, 8 * DI], MMDT))
            ws_k = E(nc.sbuf_tensor("A_wsk", [128, 8 * DI], MMDT))
            ws_v = E(nc.sbuf_tensor("A_wsv", [128, 8 * DI], MMDT))
            ws_re = E(nc.sbuf_tensor("A_wsre", [128, 8 * DI], F32))
            ws_im = E(nc.sbuf_tensor("A_wsim", [128, 8 * DI], F32))
            bq_s = E(nc.sbuf_tensor("A_bq", [128, 4], F32))
            bk_s = E(nc.sbuf_tensor("A_bk", [128, 4], F32))
            bre_s = E(nc.sbuf_tensor("A_bre", [128, 4], F32))
            bim_s = E(nc.sbuf_tensor("A_bim", [128, 4], F32))
            gam_s = E(nc.sbuf_tensor("A_gam", [128, 8], F32))
            bvrow = E(nc.sbuf_tensor("A_bvr", [1, DI], F32))
            bv_b = E(nc.sbuf_tensor("A_bvb", [128, DI], F32))
            ones1 = E(nc.sbuf_tensor("A_o1", [1, 128], F32))
            onescol = E(nc.sbuf_tensor("A_oc", [128, 1], F32))
            ident = E(nc.sbuf_tensor("A_id", [128, 128], F32))
            xtm = E(nc.sbuf_tensor("A_xtm", [128, nsub * D], F32))
            xfm = E(nc.sbuf_tensor("A_xfm", [128, 8 * G], F32))
            xnm = E(nc.sbuf_tensor("A_xnm", [128, 8 * G], MMDT))
            xnf = E(nc.sbuf_tensor("A_xnf", [128, 8 * G], F32))
            x2t_ = E(nc.sbuf_tensor("A_x2t", [128, G], F32))
            srow = E(nc.sbuf_tensor("A_srow", [1, 3 * G], F32))
            oq = E(nc.sbuf_tensor("A_oq", [128, G], F32))
            ok_ = E(nc.sbuf_tensor("A_ok", [128, G], F32))
            ov = E(nc.sbuf_tensor("A_ov", [128, DI], F32))
            re_c = E(nc.sbuf_tensor("A_re", [128, G], F32))
            im_c = E(nc.sbuf_tensor("A_im", [128, G], F32))
            t1 = E(nc.sbuf_tensor("A_t1", [128, G], F32))
            t2 = E(nc.sbuf_tensor("A_t2", [128, G], F32))
            t3 = E(nc.sbuf_tensor("A_t3", [128, G], F32))
            lsg = E(nc.sbuf_tensor("A_lsg", [128, G], F32))
            tht = E(nc.sbuf_tensor("A_tht", [128, G], F32))
            pst = E(nc.psum_tensor("A_pst", [128, G], F32))
            ps1 = E(nc.psum_tensor("A_ps1", [128, G], F32))
            ps2 = E(nc.psum_tensor("A_ps2", [128, G], F32))
            psv = E(nc.psum_tensor("A_psv", [128, DI], F32))
            psrow = E(nc.psum_tensor("A_psr", [1, G], F32))
            psb = E(nc.psum_tensor("A_psb", [128, G], F32))
            seq = segment()

            seq.step("vector", lambda v: v.memset(ones1[:], 1.0))
            seq.step("vector", lambda v: v.memset(onescol[:], 1.0))
            seq.step("sync", lambda s: [
                s.dma_start(out=ident[:], in_=ident_d),
                s.dma_start(out=bq_s[:], in_=bq_d),
                s.dma_start(out=bk_s[:], in_=bk_d),
                s.dma_start(out=bre_s[:], in_=bare_d),
                s.dma_start(out=bim_s[:], in_=baim_d),
                s.dma_start(out=gam_s[:], in_=gam_d),
                s.dma_start(out=bvrow[:], in_=bv_d)], n_dma=7)
            seq.step("tensor", lambda t: t.matmul(psv[:, 0:DI], ones1[:], bvrow[:],
                                                  start=True, stop=True))
            seq.step("vector", lambda v: v.tensor_copy(out=bv_b[:],
                                                       in_=psv[:, 0:DI]))
            for src, ws in ((wq_d, ws_q), (wk_d, ws_k), (wv_d, ws_v),
                            (ware_d, ws_re), (waim_d, ws_im)):
                seq.step("sync", lambda s, src=src: [
                    s.dma_start(out=wf[:, c * DI:(c + 1) * DI],
                                in_=src[c * 128:(c + 1) * 128, :])
                    for c in range(8)], n_dma=8)
                for c in range(8):
                    seq.step("vector", lambda v, c=c, ws=ws: v.tensor_scalar(
                        out=ws[:, c * DI:(c + 1) * DI],
                        in0=wf[:, c * DI:(c + 1) * DI],
                        scalar1=gam_s[:, c:c + 1], scalar2=None, op0=ALU.mult))
            for gi in range(NG):
                seq.step("sync", lambda s, gi=gi: [
                    s.dma_start(out=xtm[:, sub * D:(sub + 1) * D],
                                in_=xb[gi * G + sub * 128: gi * G + (sub + 1) * 128, :])
                    for sub in range(nsub)], n_dma=nsub)
                for c in range(8):
                    def tr(t, c=c):
                        last = None
                        for sub in range(nsub):
                            last = t.transpose(
                                pst[:, sub * 128:(sub + 1) * 128],
                                xtm[:, sub * D + c * 128: sub * D + (c + 1) * 128],
                                ident[:])
                        return last
                    seq.step("tensor", tr)
                    seq.step("vector", lambda v, c=c: v.tensor_copy(
                        out=xfm[:, c * G:(c + 1) * G], in_=pst[:]))
                    seq.step("scalar", lambda s, c=c: s.activation(
                        out=x2t_[:], in_=pst[:], func=AF.Square))
                    seq.step("tensor", lambda t, c=c: t.matmul(
                        psrow[:], onescol[:], x2t_[:], start=(c == 0), stop=(c == 7)))
                seq.step("scalar", lambda s: s.activation(
                    out=srow[0:1, 0:G], in_=psrow[:], func=AF.Sqrt))
                seq.step("vector", lambda v: v.tensor_scalar(
                    out=srow[0:1, G:2 * G], in0=srow[0:1, 0:G],
                    scalar1=1e-12, scalar2=None, op0=ALU.max))
                seq.step("vector", lambda v: v.reciprocal(
                    out=srow[0:1, 2 * G:3 * G], in_=srow[0:1, G:2 * G]))
                seq.step("vector", lambda v: v.tensor_scalar(
                    out=srow[0:1, 2 * G:3 * G], in0=srow[0:1, 2 * G:3 * G],
                    scalar1=float(np.sqrt(D)), scalar2=None, op0=ALU.mult))
                seq.step("tensor", lambda t: t.matmul(
                    psb[:], ones1[:], srow[0:1, 2 * G:3 * G], start=True, stop=True))
                for c in range(8):
                    seq.step("vector", lambda v, c=c: v.tensor_tensor(
                        out=xnf[:, c * G:(c + 1) * G], in0=xfm[:, c * G:(c + 1) * G],
                        in1=psb[:], op=ALU.mult))
                for c in range(8):
                    seq.step("vector", lambda v, c=c: v.tensor_copy(
                        out=xnm[:, c * G:(c + 1) * G], in_=xnf[:, c * G:(c + 1) * G]))
                for rc in range(4):
                    def mmq(t, rc=rc):
                        last = None
                        for c in range(8):
                            last = t.matmul(
                                ps1[:],
                                ws_q[:, c * DI + rc * 128: c * DI + rc * 128 + 128],
                                xnm[:, c * G:(c + 1) * G],
                                start=(c == 0), stop=(c == 7))
                        return last
                    seq.step("tensor", mmq)
                    seq.step("vector", lambda v, rc=rc: v.tensor_scalar(
                        out=oq[:], in0=ps1[:], scalar1=bq_s[:, rc:rc + 1],
                        scalar2=SCALE, op0=ALU.add, op1=ALU.mult))
                    seq.step("sync", lambda s, rc=rc, gi=gi: s.dma_start(
                        out=qs_s_d[rc * 128:(rc + 1) * 128, gi * G:(gi + 1) * G],
                        in_=oq[:]), n_dma=1)

                    def mmk(t, rc=rc):
                        last = None
                        for c in range(8):
                            last = t.matmul(
                                ps2[:],
                                ws_k[:, c * DI + rc * 128: c * DI + rc * 128 + 128],
                                xnm[:, c * G:(c + 1) * G],
                                start=(c == 0), stop=(c == 7))
                        return last
                    seq.step("tensor", mmk)
                    seq.step("vector", lambda v, rc=rc: v.tensor_scalar(
                        out=ok_[:], in0=ps2[:], scalar1=bk_s[:, rc:rc + 1],
                        scalar2=None, op0=ALU.add))
                    seq.step("sync", lambda s, rc=rc, gi=gi: s.dma_start(
                        out=k_s_d[rc * 128:(rc + 1) * 128, gi * G:(gi + 1) * G],
                        in_=ok_[:]), n_dma=1)
                for sub in range(nsub):
                    def mmv(t, sub=sub):
                        last = None
                        for c in range(8):
                            last = t.matmul(
                                psv[:],
                                xnm[:, c * G + sub * 128: c * G + (sub + 1) * 128],
                                ws_v[:, c * DI:(c + 1) * DI],
                                start=(c == 0), stop=(c == 7))
                        return last
                    seq.step("tensor", mmv)
                    seq.step("vector", lambda v: v.tensor_tensor(
                        out=ov[:], in0=psv[:], in1=bv_b[:], op=ALU.add))
                    seq.step("sync", lambda s, sub=sub, gi=gi: s.dma_start(
                        out=v_s_d[gi * G + sub * 128: gi * G + (sub + 1) * 128, :],
                        in_=ov[:]), n_dma=1)
                for rc in range(4):
                    def mmre(t, rc=rc):
                        last = None
                        for c in range(8):
                            last = t.matmul(
                                ps1[:],
                                ws_re[:, c * DI + rc * 128: c * DI + rc * 128 + 128],
                                xnf[:, c * G:(c + 1) * G],
                                start=(c == 0), stop=(c == 7))
                        return last
                    seq.step("tensor", mmre)
                    seq.step("vector", lambda v, rc=rc: v.tensor_scalar(
                        out=re_c[:], in0=ps1[:], scalar1=bre_s[:, rc:rc + 1],
                        scalar2=None, op0=ALU.add))

                    def mmim(t, rc=rc):
                        last = None
                        for c in range(8):
                            last = t.matmul(
                                ps2[:],
                                ws_im[:, c * DI + rc * 128: c * DI + rc * 128 + 128],
                                xnf[:, c * G:(c + 1) * G],
                                start=(c == 0), stop=(c == 7))
                        return last
                    seq.step("tensor", mmim)
                    seq.step("vector", lambda v, rc=rc: v.tensor_scalar(
                        out=im_c[:], in0=ps2[:], scalar1=bim_s[:, rc:rc + 1],
                        scalar2=None, op0=ALU.add))
                    seq.step("scalar", lambda s: s.activation(out=t1[:], in_=re_c[:],
                                                              func=AF.Square))
                    seq.step("scalar", lambda s: s.activation(out=t2[:], in_=im_c[:],
                                                              func=AF.Square))
                    seq.step("vector", lambda v: v.tensor_add(out=t1[:], in0=t1[:],
                                                              in1=t2[:]))
                    seq.step("scalar", lambda s: s.activation(out=t2[:], in_=t1[:],
                                                              func=AF.Sqrt))
                    seq.step("scalar", lambda s: s.activation(out=t3[:], in_=t2[:],
                                                              func=AF.Sigmoid))
                    seq.step("scalar", lambda s: s.activation(out=lsg[:], in_=t3[:],
                                                              func=AF.Ln))
                    seq.step("sync", lambda s, rc=rc, gi=gi: s.dma_start(
                        out=ls_s_d[rc * 128:(rc + 1) * 128, gi * G:(gi + 1) * G],
                        in_=lsg[:]), n_dma=1)
                    seq.step("vector", lambda v: v.reciprocal(out=t1[:],
                                                              in_=re_c[:]))
                    seq.step("vector", lambda v: v.tensor_tensor(
                        out=t1[:], in0=t1[:], in1=im_c[:], op=ALU.mult))
                    seq.step("scalar", lambda s: s.activation(out=t2[:], in_=t1[:],
                                                              func=AF.Arctan))
                    seq.step("vector", lambda v: v.tensor_scalar(
                        out=t1[:], in0=re_c[:], scalar1=0.0, scalar2=None,
                        op0=ALU.is_lt))
                    seq.step("vector", lambda v: v.tensor_scalar(
                        out=t3[:], in0=im_c[:], scalar1=0.0, scalar2=None,
                        op0=ALU.is_lt))
                    seq.step("vector", lambda v: v.tensor_scalar(
                        out=t3[:], in0=t3[:], scalar1=-2.0, scalar2=1.0,
                        op0=ALU.mult, op1=ALU.add))
                    seq.step("vector", lambda v: v.tensor_tensor(
                        out=t3[:], in0=t3[:], in1=t1[:], op=ALU.mult))
                    seq.step("vector", lambda v: v.scalar_tensor_tensor(
                        out=tht[:], in0=t3[:], scalar=PI, in1=t2[:],
                        op0=ALU.mult, op1=ALU.add))
                    seq.step("sync", lambda s, rc=rc, gi=gi: s.dma_start(
                        out=th_s_d[rc * 128:(rc + 1) * 128, gi * G:(gi + 1) * G],
                        in_=tht[:]), n_dma=1)
            finish(seq, "semA")

        with ExitStack() as ctx:
            E = ctx.enter_context
            NCH = N // 128
            B1 = E(nc.sbuf_tensor("B_B1", [128, N], F32))
            B2 = E(nc.sbuf_tensor("B_B2", [128, N], F32))
            B3 = E(nc.sbuf_tensor("B_B3", [128, N], F32))
            B4 = E(nc.sbuf_tensor("B_B4", [128, N], F32))
            qtm = E(nc.sbuf_tensor("B_qtm", [128, N], MMDT))
            ktm = E(nc.sbuf_tensor("B_ktm", [128, N], MMDT))
            vm = E(nc.sbuf_tensor("B_vm", [128, NCH * 64], MMDT))
            zrow = E(nc.sbuf_tensor("B_zr", [128, 1], F32))
            masku = E(nc.sbuf_tensor("B_mu", [128, 128], F32))
            identf = E(nc.sbuf_tensor("B_idf", [128, 128], F32))
            identm = E(nc.sbuf_tensor("B_idm", [128, 128], MMDT))
            mscore = E(nc.sbuf_tensor("B_ms", [128, 128], MMDT))
            ktc = E(nc.sbuf_tensor("B_ktc", [128, 64], MMDT))
            S_sb = E(nc.sbuf_tensor("B_Ssb", [128, 64], MMDT))
            outsb = E(nc.sbuf_tensor("B_out", [128, 64], F32))
            ps_s = E(nc.psum_tensor("B_pss", [128, 128], F32))
            out_ps = E(nc.psum_tensor("B_pso", [128, 64], F32))
            S_ps = E(nc.psum_tensor("B_psS", [64, 64], F32))
            kt_ps = E(nc.psum_tensor("B_psk", [128, 64], MMDT))
            seq = segment()
            seq.step("vector", lambda v: v.memset(zrow[:], 0.0))
            seq.step("sync", lambda s: [s.dma_start(out=masku[:], in_=masku_d),
                                        s.dma_start(out=identf[:], in_=ident_d)],
                     n_dma=2)
            seq.step("vector", lambda v: v.tensor_copy(out=identm[:], in_=identf[:]))
            for pair in range(4):
                u0, u1 = 2 * pair, 2 * pair + 1

                def ld2(s, src, dst, u0=u0, u1=u1):
                    return [s.dma_start(out=dst[0:64, :],
                                        in_=src[u0 * 64:(u0 + 1) * 64, :]),
                            s.dma_start(out=dst[64:128, :],
                                        in_=src[u1 * 64:(u1 + 1) * 64, :])]
                seq.step("sync", lambda s, f=ls_s_d, ld2=ld2: ld2(s, f, B1), n_dma=2)
                seq.step("vector", lambda v: v.tensor_tensor_scan(
                    out=B2[:], data0=B1[:], data1=zrow[:].to_broadcast([128, N]),
                    initial=0.0, op0=ALU.add, op1=ALU.add))
                seq.step("scalar", lambda s: s.activation(out=B1[:], in_=B2[:],
                                                          func=AF.Exp))
                seq.step("sync", lambda s, f=th_s_d, ld2=ld2: ld2(s, f, B2), n_dma=2)
                seq.step("vector", lambda v: v.tensor_tensor_scan(
                    out=B3[:], data0=B2[:], data1=zrow[:].to_broadcast([128, N]),
                    initial=0.0, op0=ALU.add, op1=ALU.add))
                seq.step("vector", lambda v: v.tensor_scalar(
                    out=B2[:], in0=B3[:], scalar1=PI / 2, scalar2=None, op0=ALU.add))
                seq.step("vector", lambda v: v.tensor_scalar(
                    out=B4[:], in0=B2[:], scalar1=INV2PI, scalar2=MAGIC,
                    op0=ALU.mult, op1=ALU.add))
                seq.step("vector", lambda v: v.tensor_scalar(
                    out=B4[:], in0=B4[:], scalar1=MAGIC, scalar2=None,
                    op0=ALU.subtract))
                for cc in (C1RR, C2RR, C3RR):
                    seq.step("vector", lambda v, cc=cc: v.scalar_tensor_tensor(
                        out=B2[:], in0=B4[:], scalar=-cc, in1=B2[:],
                        op0=ALU.mult, op1=ALU.add))
                seq.step("scalar", lambda s: s.activation(out=B2[:], in_=B2[:],
                                                          func=AF.Sin))
                seq.step("vector", lambda v: v.tensor_tensor(
                    out=B2[:], in0=B1[:], in1=B2[:], op=ALU.mult))
                seq.step("vector", lambda v: v.tensor_scalar(
                    out=B4[:], in0=B2[:], scalar1=1e-10, scalar2=None, op0=ALU.max))
                seq.step("vector", lambda v: v.reciprocal(out=B4[:], in_=B4[:]))
                seq.step("sync", lambda s, f=qs_s_d, ld2=ld2: ld2(s, f, B1), n_dma=2)
                seq.step("sync", lambda s, f=k_s_d, ld2=ld2: ld2(s, f, B3), n_dma=2)
                seq.step("vector", lambda v: v.tensor_tensor(
                    out=qtm[:], in0=B1[:], in1=B2[:], op=ALU.mult))
                seq.step("vector", lambda v: v.tensor_tensor(
                    out=ktm[:], in0=B3[:], in1=B4[:], op=ALU.mult))
                for (u, pb) in ((u0, 0), (u1, 64)):
                    def ldv_all(g, u=u):
                        return [g.dma_start(
                            out=vm[:, c * 64:(c + 1) * 64],
                            in_=v_s_d[c * 128:(c + 1) * 128, u * 64:(u + 1) * 64])
                            for c in range(NCH)]
                    seq.step("gpsimd", ldv_all, n_dma=NCH)
                    for c in range(NCH):
                        qc = qtm[pb:pb + 64, c * 128:(c + 1) * 128]
                        kc = ktm[pb:pb + 64, c * 128:(c + 1) * 128]
                        seq.step("tensor", lambda t, kc=kc, qc=qc: t.matmul(
                            ps_s[:], kc, qc, start=True, stop=True))
                        seq.step("vector", lambda v: v.tensor_tensor(
                            out=mscore[:], in0=ps_s[:], in1=masku[:], op=ALU.mult))

                        def mmout(t, c=c, qc=qc, pb=pb):
                            i1 = t.matmul(out_ps[:], mscore[:],
                                          vm[:, c * 64:(c + 1) * 64],
                                          start=True, stop=(c == 0))
                            if c > 0:
                                return t.matmul(out_ps[:], qc, S_sb[pb:pb + 64, :],
                                                start=False, stop=True)
                            return i1
                        seq.step("tensor", mmout)
                        seq.step("vector", lambda v: v.tensor_copy(
                            out=outsb[:], in_=out_ps[:]))
                        seq.step("sync", lambda s, u=u, c=c: s.dma_start(
                            out=attn_s_d[c * 128:(c + 1) * 128,
                                         u * 64:(u + 1) * 64],
                            in_=outsb[:]), n_dma=1)
                        seq.step("tensor", lambda t, kc=kc, pb=pb: t.transpose(
                            kt_ps[:, 0:64], kc, identm[pb:pb + 64, pb:pb + 64]))
                        seq.step("vector", lambda v: v.tensor_copy(
                            out=ktc[:], in_=kt_ps[:, 0:64]))
                        seq.step("tensor", lambda t, c=c: t.matmul(
                            S_ps[:], ktc[:], vm[:, c * 64:(c + 1) * 64],
                            start=(c == 0), stop=(c == NCH - 1)))
                        seq.step("vector", lambda v, pb=pb: v.tensor_copy(
                            out=S_sb[pb:pb + 64, :], in_=S_ps[:]))
            finish(seq, "semB")

        with ExitStack() as ctx:
            E = ctx.enter_context
            G = 256
            NSUB = G // 128
            NG = N // G
            FC = 32
            w1q = E(nc.sbuf_tensor("C_w1q", [128, 8 * 4096], MMDT))
            w2q = E(nc.sbuf_tensor("C_w2q", [128, 32 * 1024], MMDT))
            wout_s = E(nc.sbuf_tensor("C_wo", [128, 4 * 1024], MMDT))
            bout_s = E(nc.sbuf_tensor("C_bo", [128, 8], F32))
            b1_s = E(nc.sbuf_tensor("C_b1", [128, 32], F32))
            b2_s = E(nc.sbuf_tensor("C_b2", [128, 8], F32))
            ident = E(nc.sbuf_tensor("C_id", [128, 128], F32))
            ones1 = E(nc.sbuf_tensor("C_o1", [1, 128], F32))
            onescol = E(nc.sbuf_tensor("C_oc", [128, 1], F32))
            wabs = E(nc.sbuf_tensor("C_wab", [128, 72], F32))
            thr_b = E(nc.sbuf_tensor("C_thr", [128, 1], F32))
            qctx = ExitStack()
            wf = qctx.enter_context(nc.sbuf_tensor("C_wf", [128, 1024], F32))
            qtmp = qctx.enter_context(nc.sbuf_tensor("C_qt", [128, 1024], F32))
            qsgn = qctx.enter_context(nc.sbuf_tensor("C_qs", [128, 1024], F32))
            seq = segment()
            seq.step("vector", lambda v: v.memset(ones1[:], 1.0))
            seq.step("vector", lambda v: v.memset(onescol[:], 1.0))
            seq.step("sync", lambda s: [
                s.dma_start(out=ident[:], in_=ident_d),
                s.dma_start(out=bout_s[:], in_=bout0_d),
                s.dma_start(out=b1_s[:], in_=b1_d),
                s.dma_start(out=b2_s[:], in_=b2_d)], n_dma=4)
            seq.step("gpsimd", lambda g: [
                g.dma_start(out=wout_s[:, c * 1024:(c + 1) * 1024],
                            in_=wout0_d[c * 128:(c + 1) * 128, :])
                for c in range(4)], n_dma=4)

            def w1_chunk(i):
                c, q = divmod(i, 4)
                return w1t_d[c * 128:(c + 1) * 128, q * 1024:(q + 1) * 1024]

            def w2_chunk(i):
                return w2t_d[i * 128:(i + 1) * 128, :]

            for i in range(32):
                seq.step("sync", lambda s, i=i: s.dma_start(out=wf[:],
                                                            in_=w1_chunk(i)),
                         n_dma=1)
                seq.step("vector", lambda v, i=i: v.tensor_reduce(
                    out=wabs[:, i:i + 1], in_=wf[:], axis=AX.X, op=ALU.add,
                    apply_absolute_value=True))
            for i in range(32):
                seq.step("sync", lambda s, i=i: s.dma_start(out=wf[:],
                                                            in_=w2_chunk(i)),
                         n_dma=1)
                seq.step("vector", lambda v, i=i: v.tensor_reduce(
                    out=wabs[:, 32 + i:33 + i], in_=wf[:], axis=AX.X, op=ALU.add,
                    apply_absolute_value=True))
            seq.step("vector", lambda v: v.tensor_reduce(
                out=wabs[:, 64:65], in_=wabs[:, 0:32], axis=AX.X, op=ALU.add))
            seq.step("vector", lambda v: v.tensor_reduce(
                out=wabs[:, 65:66], in_=wabs[:, 32:64], axis=AX.X, op=ALU.add))
            for (src_col, dst_col) in ((64, 66), (65, 67)):
                seq.step("tensor", lambda t, src_col=src_col: t.matmul(
                    psq[:], wabs[:, src_col:src_col + 1], onescol[:],
                    start=True, stop=True))
                seq.step("vector", lambda v, dst_col=dst_col: v.tensor_scalar(
                    out=wabs[0:1, dst_col:dst_col + 1], in0=psq[:],
                    scalar1=1.0 / (4096.0 * 1024.0), scalar2=QEPS,
                    op0=ALU.mult, op1=ALU.add))
                seq.step("vector", lambda v, dst_col=dst_col: v.tensor_scalar(
                    out=wabs[0:1, dst_col:dst_col + 1],
                    in0=wabs[0:1, dst_col:dst_col + 1], scalar1=0.5,
                    scalar2=None, op0=ALU.mult))

            def quant_pass(chunk_fn, thr_col, dst):
                seq.step("tensor", lambda t, thr_col=thr_col: t.matmul(
                    psb[:, 0:1], ones1[:], wabs[0:1, thr_col:thr_col + 1],
                    start=True, stop=True))
                seq.step("vector", lambda v: v.tensor_copy(out=thr_b[:],
                                                           in_=psb[:, 0:1]))
                for i in range(32):
                    seq.step("sync", lambda s, i=i, chunk_fn=chunk_fn: s.dma_start(
                        out=wf[:], in_=chunk_fn(i)), n_dma=1)
                    seq.step("scalar", lambda sc: sc.sign(out=qsgn[:], in_=wf[:]))
                    seq.step("scalar", lambda sc: sc.activation(
                        out=qtmp[:], in_=wf[:], func=AF.Abs))
                    seq.step("vector", lambda v: v.tensor_scalar(
                        out=qtmp[:], in0=qtmp[:], scalar1=thr_b[:, 0:1],
                        scalar2=None, op0=ALU.is_gt))
                    seq.step("vector", lambda v, i=i, dst=dst: v.tensor_tensor(
                        out=dst[:, i * 1024:(i + 1) * 1024],
                        in0=qsgn[:], in1=qtmp[:], op=ALU.mult))

            quant_pass(w1_chunk, 66, w1q)
            quant_pass(w2_chunk, 67, w2q)
            qctx.close()

            xtm = E(nc.sbuf_tensor("C_xtm", [128, NSUB * D], F32))
            attm = E(nc.sbuf_tensor("C_atm", [128, NSUB * DI], F32))
            atfm = E(nc.sbuf_tensor("C_atf", [128, 4 * G], MMDT))
            xfm = E(nc.sbuf_tensor("C_xfm", [128, 8 * G], F32))
            z1 = E(nc.sbuf_tensor("C_z1", [128, 8 * G], F32))
            x1m = E(nc.sbuf_tensor("C_x1m", [128, 8 * G], MMDT))
            hm = E(nc.sbuf_tensor("C_hm", [128, FC * G], MMDT))
            ytmp = E(nc.sbuf_tensor("C_ytp", [128, G], F32))
            sqt = E(nc.sbuf_tensor("C_sqt", [128, G], F32))
            rowt = E(nc.sbuf_tensor("C_rt", [1, 6 * G], F32))
            mub = E(nc.sbuf_tensor("C_mub", [128, G], F32))
            rsb = E(nc.sbuf_tensor("C_rsb", [128, G], F32))
            otmC = E(nc.sbuf_tensor("C_otm", [128, D], F32))
            pst = E(nc.psum_tensor("C_pst", [128, 128], F32))
            pso = E(nc.psum_tensor("C_pso", [128, G], F32))
            psh = E(nc.psum_tensor("C_psh", [128, G], F32))
            psrow = E(nc.psum_tensor("C_psr", [1, G], F32))
            psb = E(nc.psum_tensor("C_psb", [128, G], F32))
            psq = E(nc.psum_tensor("C_psq", [1, 1], F32))

            def ln_blockC(src_sb, dst_f32, dst_m):
                for c in range(8):
                    seq.step("tensor", lambda t, c=c, src_sb=src_sb: t.matmul(
                        psrow[:], onescol[:], src_sb[:, c * G:(c + 1) * G],
                        start=(c == 0), stop=(c == 7)))
                seq.step("vector", lambda v: v.tensor_scalar(
                    out=rowt[0:1, 0:G], in0=psrow[:], scalar1=1.0 / D,
                    scalar2=None, op0=ALU.mult))
                for c in range(8):
                    seq.step("scalar", lambda s, c=c, src_sb=src_sb: s.activation(
                        out=sqt[:], in_=src_sb[:, c * G:(c + 1) * G],
                        func=AF.Square))
                    seq.step("tensor", lambda t, c=c: t.matmul(
                        psrow[:], onescol[:], sqt[:], start=(c == 0), stop=(c == 7)))
                seq.step("vector", lambda v: v.tensor_scalar(
                    out=rowt[0:1, G:2 * G], in0=psrow[:], scalar1=1.0 / D,
                    scalar2=None, op0=ALU.mult))
                seq.step("scalar", lambda s: s.activation(
                    out=rowt[0:1, 2 * G:3 * G], in_=rowt[0:1, 0:G], func=AF.Square))
                seq.step("vector", lambda v: v.tensor_sub(
                    out=rowt[0:1, G:2 * G], in0=rowt[0:1, G:2 * G],
                    in1=rowt[0:1, 2 * G:3 * G]))
                seq.step("vector", lambda v: v.tensor_scalar(
                    out=rowt[0:1, G:2 * G], in0=rowt[0:1, G:2 * G], scalar1=LNEPS,
                    scalar2=None, op0=ALU.add))
                seq.step("scalar", lambda s: s.activation(
                    out=rowt[0:1, 3 * G:4 * G], in_=rowt[0:1, G:2 * G],
                    func=AF.Sqrt))
                seq.step("vector", lambda v: v.reciprocal(
                    out=rowt[0:1, 4 * G:5 * G], in_=rowt[0:1, 3 * G:4 * G]))
                seq.step("tensor", lambda t: t.matmul(
                    psb[:], ones1[:], rowt[0:1, 0:G], start=True, stop=True))
                seq.step("vector", lambda v: v.tensor_copy(out=mub[:], in_=psb[:]))
                seq.step("tensor", lambda t: t.matmul(
                    psb[:], ones1[:], rowt[0:1, 4 * G:5 * G],
                    start=True, stop=True))
                seq.step("vector", lambda v: v.tensor_copy(out=rsb[:], in_=psb[:]))
                for c in range(8):
                    seq.step("vector",
                             lambda v, c=c, src_sb=src_sb, dst_f32=dst_f32:
                             v.tensor_sub(
                                 out=dst_f32[:, c * G:(c + 1) * G],
                                 in0=src_sb[:, c * G:(c + 1) * G], in1=mub[:]))
                    seq.step("vector", lambda v, c=c, dst_f32=dst_f32:
                             v.tensor_tensor(
                                 out=dst_f32[:, c * G:(c + 1) * G],
                                 in0=dst_f32[:, c * G:(c + 1) * G], in1=rsb[:],
                                 op=ALU.mult))
                    if dst_m is not None:
                        seq.step("vector",
                                 lambda v, c=c, dst_f32=dst_f32, dst_m=dst_m:
                                 v.tensor_copy(
                                     out=dst_m[:, c * G:(c + 1) * G],
                                     in_=dst_f32[:, c * G:(c + 1) * G]))

            for gi in range(NG):
                seq.step("sync", lambda s, gi=gi: [
                    s.dma_start(out=xtm[:, sub * D:(sub + 1) * D],
                                in_=xb[gi * G + sub * 128:
                                       gi * G + (sub + 1) * 128, :])
                    for sub in range(NSUB)] + [
                    s.dma_start(out=attm[:, sub * DI:(sub + 1) * DI],
                                in_=attn_s_d[gi * G + sub * 128:
                                             gi * G + (sub + 1) * 128, :])
                    for sub in range(NSUB)],
                    n_dma=2 * NSUB)
                for sub in range(NSUB):
                    for c in range(4):
                        seq.step("tensor", lambda t, c=c, sub=sub: t.transpose(
                            pst[:], attm[:, sub * DI + c * 128:
                                         sub * DI + (c + 1) * 128], ident[:]))
                        seq.step("vector", lambda v, c=c, sub=sub: v.tensor_copy(
                            out=atfm[:, c * G + sub * 128:c * G + sub * 128 + 128],
                            in_=pst[:]))
                    for c in range(8):
                        seq.step("tensor", lambda t, c=c, sub=sub: t.transpose(
                            pst[:], xtm[:, sub * D + c * 128:
                                        sub * D + (c + 1) * 128], ident[:]))
                        seq.step("vector", lambda v, c=c, sub=sub: v.tensor_copy(
                            out=xfm[:, c * G + sub * 128:c * G + sub * 128 + 128],
                            in_=pst[:]))
                for dc in range(8):
                    def mmo(t, dc=dc):
                        last = None
                        for ec in range(4):
                            last = t.matmul(
                                pso[:],
                                wout_s[:, ec * 1024 + dc * 128: ec * 1024 + dc * 128 + 128],
                                atfm[:, ec * G:(ec + 1) * G],
                                start=(ec == 0), stop=(ec == 3))
                        return last
                    seq.step("tensor", mmo)
                    seq.step("vector", lambda v, dc=dc: v.tensor_scalar(
                        out=z1[:, dc * G:(dc + 1) * G], in0=pso[:],
                        scalar1=bout_s[:, dc:dc + 1], scalar2=None, op0=ALU.add))
                    seq.step("vector", lambda v, dc=dc: v.tensor_tensor(
                        out=z1[:, dc * G:(dc + 1) * G],
                        in0=z1[:, dc * G:(dc + 1) * G],
                        in1=xfm[:, dc * G:(dc + 1) * G], op=ALU.add))
                ln_blockC(z1, xfm, x1m)
                for fc in range(FC):
                    def mmh(t, fc=fc):
                        last = None
                        for c in range(8):
                            last = t.matmul(
                                psh[:],
                                w1q[:, c * 4096 + fc * 128: c * 4096 + fc * 128 + 128],
                                x1m[:, c * G:(c + 1) * G],
                                start=(c == 0), stop=(c == 7))
                        return last
                    seq.step("tensor", mmh)
                    seq.step("scalar", lambda s, fc=fc: s.activation(
                        out=hm[:, fc * G:(fc + 1) * G], in_=psh[:], func=AF.Relu,
                        bias=b1_s[:, fc:fc + 1]))
                for dc in range(8):
                    def mmy(t, dc=dc):
                        last = None
                        for fc in range(FC):
                            last = t.matmul(
                                pso[:],
                                w2q[:, fc * 1024 + dc * 128: fc * 1024 + dc * 128 + 128],
                                hm[:, fc * G:(fc + 1) * G],
                                start=(fc == 0), stop=(fc == FC - 1))
                        return last
                    seq.step("tensor", mmy)
                    seq.step("vector", lambda v, dc=dc: v.tensor_scalar(
                        out=ytmp[:], in0=pso[:],
                        scalar1=b2_s[:, dc:dc + 1], scalar2=None, op0=ALU.add))
                    seq.step("vector", lambda v, dc=dc: v.tensor_tensor(
                        out=z1[:, dc * G:(dc + 1) * G],
                        in0=xfm[:, dc * G:(dc + 1) * G],
                        in1=ytmp[:], op=ALU.add))
                ln_blockC(z1, z1, None)
                seq.step("sync", lambda s, gi=gi: [
                    s.dma_start(out=x2f_s_d[c * 128:(c + 1) * 128,
                                            gi * G:(gi + 1) * G],
                                in_=z1[:, c * G:(c + 1) * G])
                    for c in range(8)], n_dma=8)
                for sub in range(NSUB):
                    for c in range(8):
                        seq.step("tensor", lambda t, c=c, sub=sub: t.transpose(
                            pst[:], z1[:, c * G + sub * 128:
                                       c * G + sub * 128 + 128], ident[:]))
                        seq.step("vector", lambda v, c=c: v.tensor_copy(
                            out=otmC[:, c * 128:(c + 1) * 128], in_=pst[:]))
                    seq.step("sync", lambda s, gi=gi, sub=sub: s.dma_start(
                        out=x2t_s_d[gi * G + sub * 128:gi * G + (sub + 1) * 128, :],
                        in_=otmC[:]), n_dma=1)
            finish(seq, "semC")

        with ExitStack() as ctx:
            E = ctx.enter_context
            G = 128
            NG = N // G
            x2f = E(nc.sbuf_tensor("D_x2f", [128, 8 * G], F32))
            x2m = E(nc.sbuf_tensor("D_x2m", [128, 8 * G], MMDT))
            wdkv_s = E(nc.sbuf_tensor("D_wdkv", [128, 8 * 256], MMDT))
            wuk_s = E(nc.sbuf_tensor("D_wuk", [128, 2 * 512], MMDT))
            wuv_s = E(nc.sbuf_tensor("D_wuv", [128, 2 * 512], MMDT))
            wdq_s = E(nc.sbuf_tensor("D_wdq", [128, 8 * 512], MMDT))
            wuq_s = E(nc.sbuf_tensor("D_wuq", [128, 4 * 512], MMDT))
            wqr_s = E(nc.sbuf_tensor("D_wqr", [128, 4 * 256], MMDT))
            wkr_s = E(nc.sbuf_tensor("D_wkr", [128, 8 * 32], MMDT))
            wiq_s = E(nc.sbuf_tensor("D_wiq", [128, 8 * 256], F32))
            wik_s = E(nc.sbuf_tensor("D_wik", [128, 8 * 64], F32))
            ckvf = E(nc.sbuf_tensor("D_ckv", [128, 2 * G], MMDT))
            cqf = E(nc.sbuf_tensor("D_cqf", [128, 4 * G], MMDT))
            tabm = E(nc.sbuf_tensor("D_tab", [128, 1088], BF16))
            krf = E(nc.sbuf_tensor("D_krf", [128, 32], F32))
            cosj_s = E(nc.sbuf_tensor("D_cjs", [128, RHALF], F32))
            sinj_s = E(nc.sbuf_tensor("D_sjs", [128, RHALF], F32))
            cj2 = E(nc.sbuf_tensor("D_cj2", [128, RHALF], F32))
            sj2 = E(nc.sbuf_tensor("D_sj2", [128, RHALF], F32))
            drow = E(nc.sbuf_tensor("D_dr", [1, 4 * RHALF], F32))
            cdb = E(nc.sbuf_tensor("D_cdb", [128, RHALF], F32))
            sdb = E(nc.sbuf_tensor("D_sdb", [128, RHALF], F32))
            ones1D = E(nc.sbuf_tensor("D_o1", [1, 128], F32))
            sp1 = E(nc.sbuf_tensor("D_sp1", [128, 32], F32))
            sp2 = E(nc.sbuf_tensor("D_sp2", [128, 32], F32))
            sp3 = E(nc.sbuf_tensor("D_sp3", [128, 32], F32))
            outf = E(nc.sbuf_tensor("D_of", [128, 512], F32))
            pK = E(nc.psum_tensor("D_pK", [128, 512], F32))
            pV = E(nc.psum_tensor("D_pV", [128, 512], F32))
            pA = E(nc.psum_tensor("D_pA", [128, 512], F32))
            pB_ = E(nc.psum_tensor("D_pB", [128, 512], F32))
            seq = segment()

            def ldw(dst, src, chunks, cols):
                seq.step("gpsimd",
                         lambda g, dst=dst, src=src, chunks=chunks, cols=cols: [
                             g.dma_start(out=dst[:, c * cols:(c + 1) * cols],
                                         in_=src[c * 128:(c + 1) * 128, :])
                             for c in range(chunks)], n_dma=chunks)

            ldw(wdkv_s, wdkv_d, 8, 256)
            ldw(wuk_s, wuk_d, 2, 512)
            ldw(wuv_s, wuv_d, 2, 512)
            ldw(wdq_s, wdq_d, 8, 512)
            ldw(wuq_s, wuq_d, 4, 512)
            ldw(wqr_s, wqr_d, 4, 256)
            ldw(wkr_s, wkr_d, 8, 32)
            seq.step("sync", lambda s: [
                s.dma_start(out=wiq_s[:, c * 256:(c + 1) * 256],
                            in_=wiq_d[c * 128:(c + 1) * 128, :])
                for c in range(8)], n_dma=8)
            seq.step("sync", lambda s: [
                s.dma_start(out=wik_s[:, c * 64:(c + 1) * 64],
                            in_=wik_d[c * 128:(c + 1) * 128, :])
                for c in range(8)], n_dma=8)
            seq.step("vector", lambda v: v.memset(ones1D[:], 1.0))
            seq.step("sync", lambda s: s.dma_start(out=drow[0:1, 0:RHALF],
                                                   in_=rdel_d), n_dma=1)
            seq.step("scalar", lambda s: s.activation(
                out=drow[0:1, 0:RHALF], in_=drow[0:1, 0:RHALF], func=AF.Sigmoid))
            seq.step("vector", lambda v: v.tensor_scalar(
                out=drow[0:1, 0:RHALF], in0=drow[0:1, 0:RHALF],
                scalar1=-2.0 * PI, scalar2=None, op0=ALU.mult))
            seq.step("vector", lambda v: v.tensor_scalar(
                out=drow[0:1, RHALF:2 * RHALF], in0=drow[0:1, 0:RHALF],
                scalar1=-PI, scalar2=None, op0=ALU.is_lt))
            seq.step("vector", lambda v: v.scalar_tensor_tensor(
                out=drow[0:1, 0:RHALF], in0=drow[0:1, RHALF:2 * RHALF],
                scalar=2.0 * PI, in1=drow[0:1, 0:RHALF],
                op0=ALU.mult, op1=ALU.add))
            seq.step("scalar", lambda s: s.activation(
                out=drow[0:1, RHALF:2 * RHALF], in_=drow[0:1, 0:RHALF],
                func=AF.Sin))
            seq.step("scalar", lambda s: s.activation(
                out=drow[0:1, 2 * RHALF:3 * RHALF], in_=drow[0:1, 0:RHALF],
                func=AF.Abs))
            seq.step("vector", lambda v: v.tensor_scalar(
                out=drow[0:1, 2 * RHALF:3 * RHALF],
                in0=drow[0:1, 2 * RHALF:3 * RHALF],
                scalar1=-1.0, scalar2=PI / 2, op0=ALU.mult, op1=ALU.add))
            seq.step("scalar", lambda s: s.activation(
                out=drow[0:1, 2 * RHALF:3 * RHALF],
                in_=drow[0:1, 2 * RHALF:3 * RHALF], func=AF.Sin))
            seq.step("tensor", lambda t: t.matmul(
                pA[:, 0:RHALF], ones1D[:], drow[0:1, 2 * RHALF:3 * RHALF],
                start=True, stop=True))
            seq.step("vector", lambda v: v.tensor_copy(out=cdb[:],
                                                       in_=pA[:, 0:RHALF]))
            seq.step("tensor", lambda t: t.matmul(
                pA[:, 0:RHALF], ones1D[:], drow[0:1, RHALF:2 * RHALF],
                start=True, stop=True))
            seq.step("vector", lambda v: v.tensor_copy(out=sdb[:],
                                                       in_=pA[:, 0:RHALF]))

            for gi in range(NG):
                seq.step("sync", lambda s, gi=gi: [
                    s.dma_start(out=x2f[:, c * G:(c + 1) * G],
                                in_=x2f_s_d[c * 128:(c + 1) * 128,
                                            gi * G:(gi + 1) * G])
                    for c in range(8)], n_dma=8)
                for c in range(8):
                    seq.step("vector", lambda v, c=c: v.tensor_copy(
                        out=x2m[:, c * G:(c + 1) * G],
                        in_=x2f[:, c * G:(c + 1) * G]))
                seq.step("sync", lambda s, gi=gi: [
                    s.dma_start(out=cj2[:], in_=cosj_d[gi * G:(gi + 1) * G, :]),
                    s.dma_start(out=sj2[:], in_=sinj_d[gi * G:(gi + 1) * G, :])],
                    n_dma=2)
                seq.step("vector", lambda v: v.tensor_tensor(
                    out=cosj_s[:], in0=cj2[:], in1=cdb[:], op=ALU.mult))
                seq.step("vector", lambda v: v.tensor_tensor(
                    out=sp3[:, 0:RHALF], in0=sj2[:], in1=sdb[:], op=ALU.mult))
                seq.step("vector", lambda v: v.tensor_sub(
                    out=cosj_s[:], in0=cosj_s[:], in1=sp3[:, 0:RHALF]))
                seq.step("vector", lambda v: v.tensor_tensor(
                    out=sinj_s[:], in0=sj2[:], in1=cdb[:], op=ALU.mult))
                seq.step("vector", lambda v: v.tensor_tensor(
                    out=sp3[:, 0:RHALF], in0=cj2[:], in1=sdb[:], op=ALU.mult))
                seq.step("vector", lambda v: v.tensor_add(
                    out=sinj_s[:], in0=sinj_s[:], in1=sp3[:, 0:RHALF]))
                for rc in range(2):
                    def mm1(t, rc=rc):
                        last = None
                        for c in range(8):
                            last = t.matmul(
                                pA[:, 0:G],
                                wdkv_s[:, c * 256 + rc * 128: c * 256 + rc * 128 + 128],
                                x2m[:, c * G:(c + 1) * G],
                                start=(c == 0), stop=(c == 7))
                        return last
                    seq.step("tensor", mm1)
                    seq.step("vector", lambda v, rc=rc: v.tensor_copy(
                        out=ckvf[:, rc * G:(rc + 1) * G], in_=pA[:, 0:G]))
                for rc in range(4):
                    def mm2(t, rc=rc):
                        last = None
                        for c in range(8):
                            last = t.matmul(
                                pA[:, 0:G],
                                wdq_s[:, c * 512 + rc * 128: c * 512 + rc * 128 + 128],
                                x2m[:, c * G:(c + 1) * G],
                                start=(c == 0), stop=(c == 7))
                        return last
                    seq.step("tensor", mm2)
                    seq.step("vector", lambda v, rc=rc: v.tensor_copy(
                        out=cqf[:, rc * G:(rc + 1) * G], in_=pA[:, 0:G]))

                def mmK(t):
                    last = None
                    for j in range(2):
                        last = t.matmul(pK[:], ckvf[:, j * G:(j + 1) * G],
                                        wuk_s[:, j * 512:(j + 1) * 512],
                                        start=(j == 0), stop=(j == 1))
                    return last
                seq.step("tensor", mmK)
                seq.step("vector", lambda v: v.tensor_copy(out=tabm[:, 0:512],
                                                           in_=pK[:]))

                def mmV(t):
                    last = None
                    for j in range(2):
                        last = t.matmul(pV[:], ckvf[:, j * G:(j + 1) * G],
                                        wuv_s[:, j * 512:(j + 1) * 512],
                                        start=(j == 0), stop=(j == 1))
                    return last
                seq.step("tensor", mmV)
                seq.step("vector", lambda v: v.tensor_copy(out=tabm[:, 512:1024],
                                                           in_=pV[:]))

                def mmR(t):
                    last = None
                    for c in range(8):
                        last = t.matmul(pA[:, 0:32], x2m[:, c * G:(c + 1) * G],
                                        wkr_s[:, c * 32:(c + 1) * 32],
                                        start=(c == 0), stop=(c == 7))
                    return last
                seq.step("tensor", mmR)
                seq.step("vector", lambda v: v.tensor_copy(out=krf[:],
                                                           in_=pA[:, 0:32]))
                seq.step("vector", lambda v: v.tensor_copy(out=tabm[:, 1024:1056],
                                                           in_=krf[:]))
                seq.step("scalar", lambda s: s.activation(out=sp1[:], in_=krf[:],
                                                          func=AF.Abs))
                seq.step("scalar", lambda s: s.activation(out=sp1[:], in_=sp1[:],
                                                          func=AF.Exp, scale=-1.0))
                seq.step("vector", lambda v: v.tensor_scalar(
                    out=sp1[:], in0=sp1[:], scalar1=1.0, scalar2=None, op0=ALU.add))
                seq.step("scalar", lambda s: s.activation(out=sp1[:], in_=sp1[:],
                                                          func=AF.Ln))
                seq.step("vector", lambda v: v.tensor_scalar(
                    out=sp2[:], in0=krf[:], scalar1=0.0, scalar2=None, op0=ALU.max))
                seq.step("vector", lambda v: v.tensor_add(out=sp1[:], in0=sp1[:],
                                                          in1=sp2[:]))
                seq.step("vector", lambda v: v.tensor_tensor(
                    out=sp2[:, 0:RHALF], in0=sp1[:, 0:RHALF], in1=cosj_s[:],
                    op=ALU.mult))
                seq.step("vector", lambda v: v.tensor_tensor(
                    out=sp3[:, 0:RHALF], in0=sp1[:, RHALF:32], in1=sinj_s[:],
                    op=ALU.mult))
                seq.step("vector", lambda v: v.tensor_sub(
                    out=sp2[:, 0:RHALF], in0=sp2[:, 0:RHALF], in1=sp3[:, 0:RHALF]))
                seq.step("vector", lambda v: v.tensor_copy(
                    out=tabm[:, 1056:1056 + RHALF], in_=sp2[:, 0:RHALF]))
                seq.step("vector", lambda v: v.tensor_tensor(
                    out=sp2[:, 0:RHALF], in0=sp1[:, 0:RHALF], in1=sinj_s[:],
                    op=ALU.mult))
                seq.step("vector", lambda v: v.tensor_tensor(
                    out=sp3[:, 0:RHALF], in0=sp1[:, RHALF:32], in1=cosj_s[:],
                    op=ALU.mult))
                seq.step("vector", lambda v: v.tensor_add(
                    out=sp2[:, 0:RHALF], in0=sp2[:, 0:RHALF], in1=sp3[:, 0:RHALF]))
                seq.step("vector", lambda v: v.tensor_copy(
                    out=tabm[:, 1056 + RHALF:1088], in_=sp2[:, 0:RHALF]))
                seq.step("gpsimd", lambda g, gi=gi: g.dma_start(
                    out=tab_s_d[gi * G:(gi + 1) * G, :], in_=tabm[:]), n_dma=1)

                def mmQ(t):
                    last = None
                    for j in range(4):
                        last = t.matmul(pB_[:], cqf[:, j * G:(j + 1) * G],
                                        wuq_s[:, j * 512:(j + 1) * 512],
                                        start=(j == 0), stop=(j == 3))
                    return last
                seq.step("tensor", mmQ)
                seq.step("vector", lambda v: v.tensor_copy(out=outf[:], in_=pB_[:]))
                seq.step("sync", lambda s, gi=gi: s.dma_start(
                    out=qc_s_d[gi * G:(gi + 1) * G, :], in_=outf[:]), n_dma=1)

                def mmQR(t):
                    last = None
                    for j in range(4):
                        last = t.matmul(pB_[:, 0:256], cqf[:, j * G:(j + 1) * G],
                                        wqr_s[:, j * 256:(j + 1) * 256],
                                        start=(j == 0), stop=(j == 3))
                    return last
                seq.step("tensor", mmQR)
                seq.step("vector", lambda v: v.tensor_copy(out=outf[:, 0:256],
                                                           in_=pB_[:, 0:256]))
                seq.step("sync", lambda s, gi=gi: s.dma_start(
                    out=qr_s_d[gi * G:(gi + 1) * G, :], in_=outf[:, 0:256]),
                    n_dma=1)

                def mmIQ(t):
                    last = None
                    for c in range(8):
                        last = t.matmul(pA[:, 0:256], x2f[:, c * G:(c + 1) * G],
                                        wiq_s[:, c * 256:(c + 1) * 256],
                                        start=(c == 0), stop=(c == 7))
                    return last
                seq.step("tensor", mmIQ)
                seq.step("vector", lambda v: v.tensor_copy(out=outf[:, 0:256],
                                                           in_=pA[:, 0:256]))
                seq.step("sync", lambda s, gi=gi: s.dma_start(
                    out=qit_s_d[gi * G:(gi + 1) * G, :], in_=outf[:, 0:256]),
                    n_dma=1)

                def mmIK(t):
                    last = None
                    for c in range(8):
                        last = t.matmul(pA[0:64, 0:G],
                                        wik_s[:, c * 64:(c + 1) * 64],
                                        x2f[:, c * G:(c + 1) * G],
                                        start=(c == 0), stop=(c == 7))
                    return last
                seq.step("tensor", mmIK)
                seq.step("vector", lambda v: v.tensor_copy(out=outf[0:64, 0:G],
                                                           in_=pA[0:64, 0:G]))
                seq.step("sync", lambda s, gi=gi: s.dma_start(
                    out=ki_s_d[:, gi * G:(gi + 1) * G], in_=outf[0:64, 0:G]),
                    n_dma=1)
            finish(seq, "semD")

        with ExitStack() as ctx:
            E = ctx.enter_context
            NT = TPC // 128
            ki_s = E(nc.sbuf_tensor("E_ki", [64, N], F32))
            qirows = E(nc.sbuf_tensor("E_qir", [128, 256], F32))
            qi_s = E(nc.sbuf_tensor("E_qi", [64, 512], F32))
            itile = E(nc.sbuf_tensor("E_it", [128, N], F32))
            tmpc = E(nc.sbuf_tensor("E_tc", [128, 512], F32))
            dsc = E(nc.sbuf_tensor("E_dsc", [128, 512], F32))
            winU = E(nc.sbuf_tensor("E_wu", [128, 512], U32))
            futU = E(nc.sbuf_tensor("E_fu", [128, 512], U32))
            m1 = E(nc.sbuf_tensor("E_m1", [128, 128], F32))
            m8 = E(nc.sbuf_tensor("E_m8", [128, 8], F32))
            m8i = E(nc.sbuf_tensor("E_m8i", [128, 8], U32))
            cidu = E(nc.sbuf_tensor("E_cid", [128, 32], U32))
            cidf = E(nc.sbuf_tensor("E_cif", [128, 32], F32))
            cneg = E(nc.sbuf_tensor("E_cn", [128, 32], F32))
            sortid = E(nc.sbuf_tensor("E_srt", [128, 32], F32))
            gidx = E(nc.sbuf_tensor("E_gx", [128, 32], I32))
            gtmp = E(nc.sbuf_tensor("E_gt", [128, 32], F32))
            prow = E(nc.sbuf_tensor("E_pr", [128, 1], F32))
            cand = E(nc.sbuf_tensor("E_cnd", [128, 2048], F32))
            pos = E(nc.sbuf_tensor("E_pos", [128, 32], U32))
            posf = E(nc.sbuf_tensor("E_pf", [128, 32], F32))
            lanef = E(nc.sbuf_tensor("E_lf", [128, 32], F32))
            shfu = E(nc.sbuf_tensor("E_sh", [128, 32], U32))
            eqt = E(nc.sbuf_tensor("E_eq", [128, 1024], F32))
            selb = E(nc.sbuf_tensor("E_sel", [128, 32], F32))
            iotab = E(nc.sbuf_tensor("E_io", [128, 512], F32))
            iota32 = E(nc.sbuf_tensor("E_i32", [128, 32], F32))
            tcol = E(nc.sbuf_tensor("E_tcl", [128, 1], F32))
            tcoli = E(nc.sbuf_tensor("E_tci", [128, 1], I32))
            idxwb = E(nc.sbuf_tensor("E_iw", [128, 4], F32))
            idxwr = E(nc.sbuf_tensor("E_iwr", [1, 4], F32))
            ones1 = E(nc.sbuf_tensor("E_o1", [1, 128], F32))
            ngt = E(nc.sbuf_tensor("E_ng", [128, 512], F32))
            identf = E(nc.sbuf_tensor("E_idf", [128, 128], F32))
            psI0 = E(nc.psum_tensor("E_p0", [128, 512], F32))
            psI1 = E(nc.psum_tensor("E_p1", [128, 512], F32))
            psI2 = E(nc.psum_tensor("E_p2", [128, 512], F32))
            psI3 = E(nc.psum_tensor("E_p3", [128, 512], F32))
            psb = E(nc.psum_tensor("E_pb", [128, 512], F32))
            seq = segment()
            seq.step("vector", lambda v: v.memset(ones1[:], 1.0))
            seq.step("vector", lambda v: v.memset(ngt[:], NEG38))
            seq.step("gpsimd", lambda g: g.iota(
                iotab[:], pattern=[[1, 512]], base=0, channel_multiplier=0,
                allow_small_or_imprecise_dtypes=True))
            seq.step("gpsimd", lambda g: g.iota(
                iota32[:], pattern=[[1, 32]], base=0, channel_multiplier=0,
                allow_small_or_imprecise_dtypes=True))
            seq.step("gpsimd", lambda g: g.iota(
                prow[:], pattern=[[0, 1]], base=0, channel_multiplier=128,
                allow_small_or_imprecise_dtypes=True))
            seq.step("sync", lambda s: [
                s.dma_start(out=ki_s[:], in_=ki_s_d),
                s.dma_start(out=idxwr[:], in_=idxw_d),
                s.dma_start(out=identf[:], in_=ident_d)], n_dma=3)
            seq.step("tensor", lambda t: t.matmul(psb[:, 0:4], ones1[:], idxwr[:],
                                                  start=True, stop=True))
            seq.step("vector", lambda v: v.tensor_copy(out=idxwb[:],
                                                       in_=psb[:, 0:4]))

            def emit_gatherE():
                return seq.step("gpsimd", lambda g: [
                    g.indirect_dma_start(
                        out=cand[:, gg * 64:(gg + 1) * 64], out_offset=None,
                        in_=idram.rearrange("p (r c) -> (p r) c", c=64),
                        in_offset=IndirectOffsetOnAxis(gidx[:, gg:gg + 1],
                                                       axis=0))
                    for gg in range(32)], n_dma=32, aux=True)

            def emit_posttailE(ti):
                for r in range(4):
                    seq.step("vector", lambda v: v.max(out=m8[:], in_=cand[:]))
                    seq.step("vector", lambda v: v.max_index(
                        out=m8i[:], in_max=m8[:], in_values=cand[:]))
                    seq.step("vector", lambda v, r=r: v.tensor_copy(
                        out=pos[:, r * 8:(r + 1) * 8], in_=m8i[:]))
                    if r < 3:
                        seq.step("vector", lambda v: v.match_replace(
                            out=cand[:], in_to_replace=m8[:], in_values=cand[:],
                            imm_value=ZAPV))
                seq.step("vector", lambda v: v.tensor_scalar(
                    out=shfu[:], in0=pos[:], scalar1=6, scalar2=None,
                    op0=ALU.logical_shift_right))
                seq.step("vector", lambda v: v.tensor_scalar(
                    out=pos[:], in0=pos[:], scalar1=63, scalar2=None,
                    op0=ALU.bitwise_and))
                seq.step("vector", lambda v: v.tensor_copy(out=posf[:],
                                                           in_=shfu[:]))
                seq.step("vector", lambda v: v.tensor_copy(out=lanef[:],
                                                           in_=pos[:]))
                seq.step("vector", lambda v: v.tensor_tensor(
                    out=eqt[:].rearrange("p (j c) -> p j c", c=32),
                    in0=posf[:].rearrange("p (j o) -> p j o", o=1).to_broadcast(
                        [128, 32, 32]),
                    in1=iota32[:].rearrange("p (o c) -> p o c", o=1).to_broadcast(
                        [128, 32, 32]),
                    op=ALU.is_equal))
                seq.step("vector", lambda v: v.tensor_tensor(
                    out=eqt[:].rearrange("p (j c) -> p j c", c=32),
                    in0=eqt[:].rearrange("p (j c) -> p j c", c=32),
                    in1=sortid[:].rearrange("p (o c) -> p o c", o=1).to_broadcast(
                        [128, 32, 32]),
                    op=ALU.mult))
                seq.step("vector", lambda v: v.tensor_reduce(
                    out=selb[:], in_=eqt[:].rearrange("p (j c) -> p j c", c=32),
                    axis=AX.X, op=ALU.add))
                seq.step("vector", lambda v: v.tensor_scalar(
                    out=selb[:], in0=selb[:], scalar1=64.0, scalar2=None,
                    op0=ALU.mult))
                seq.step("vector", lambda v: v.tensor_add(
                    out=selb[:], in0=selb[:], in1=lanef[:]))
                seq.step("sync", lambda s, ti=ti: s.dma_start(
                    out=sel_s_d[ti * 128:(ti + 1) * 128, :], in_=selb[:]),
                    n_dma=1)

            prev_ti = None
            chunk_end_prev = None
            last_idram_prev = None
            for ti in range(NT):
                Lraw = ((4096 + ti * 128 + 160 + 511) // 512) * 512
                L = max(2048, min(N, Lraw))
                NSC = L // 512
                s_tcol = seq.step(
                    "sync", lambda s, ti=ti: s.dma_start(
                        out=tcol[:], in_=tcol_d[ti * 128:(ti + 1) * 128, :]),
                    n_dma=1)
                s_tcoli = seq.step("vector", lambda v: v.tensor_copy(
                    out=tcoli[:], in_=tcol[:]))
                s_qg = seq.step("gpsimd", lambda g: g.indirect_dma_start(
                    out=qirows[:], out_offset=None, in_=qit_s_d,
                    in_offset=IndirectOffsetOnAxis(tcoli[:, 0:1], axis=0)),
                    n_dma=1)
                cp_last = None
                for h in range(4):
                    s_tr = seq.step("tensor", lambda t, h=h: t.transpose(
                        psb[0:64, 0:128], qirows[:, h * 64:(h + 1) * 64],
                        identf[:]))
                    cp_last = seq.step("vector", lambda v, h=h: v.tensor_copy(
                        out=qi_s[0:64, h * 128:(h + 1) * 128],
                        in_=psb[0:64, 0:128]))
                if prev_ti is not None:
                    emit_gatherE()
                axl = [] if last_idram_prev is None else [last_idram_prev]
                mm_dep = cp_last
                last_idram = None
                chunk_end = None
                for sc in range(NSC):
                    lo = sc * 512

                    def mmI(t, sc=sc):
                        last = None
                        for h in range(4):
                            ps = (psI0, psI1, psI2, psI3)[h]
                            last = t.matmul(ps[:],
                                            qi_s[0:64, h * 128:(h + 1) * 128],
                                            ki_s[:, sc * 512:(sc + 1) * 512],
                                            start=True, stop=True)
                        return last
                    s_mm = seq.step("tensor", mmI, aux_after=axl)
                    for h in range(4):
                        ps = (psI0, psI1, psI2, psI3)[h]
                        if h == 0:
                            seq.step("vector", lambda v, ps=ps, h=h, lo=lo:
                                     v.tensor_scalar(
                                         out=itile[:, lo:lo + 512], in0=ps[:],
                                         scalar1=0.0, scalar2=idxwb[:, h:h + 1],
                                         op0=ALU.max, op1=ALU.mult),
                                     aux_after=axl)
                        else:
                            s_hr = seq.step("vector", lambda v, ps=ps, h=h:
                                            v.tensor_scalar(
                                                out=tmpc[:], in0=ps[:],
                                                scalar1=0.0,
                                                scalar2=idxwb[:, h:h + 1],
                                                op0=ALU.max, op1=ALU.mult),
                                            aux_after=axl)
                            seq.step("vector", lambda v, lo=lo: v.tensor_add(
                                out=itile[:, lo:lo + 512],
                                in0=itile[:, lo:lo + 512], in1=tmpc[:]),
                                aux_after=axl)
                    mm_dep = s_hr
                    seq.step("vector", lambda v, lo=lo: v.tensor_scalar(
                        out=dsc[:], in0=iotab[:], scalar1=tcol[:, 0:1],
                        scalar2=float(lo), op0=ALU.subtract, op1=ALU.add),
                        aux_after=axl)
                    seq.step("vector", lambda v: v.tensor_scalar(
                        out=winU[:], in0=dsc[:], scalar1=-63.0, scalar2=None,
                        op0=ALU.is_ge), aux_after=axl)
                    seq.step("vector", lambda v: v.tensor_scalar(
                        out=futU[:], in0=dsc[:], scalar1=0.0, scalar2=None,
                        op0=ALU.is_gt), aux_after=axl)
                    seq.step("vector", lambda v: v.tensor_tensor(
                        out=winU[:], in0=winU[:], in1=futU[:], op=ALU.subtract),
                        aux_after=axl)
                    seq.step("vector", lambda v, lo=lo: v.select(
                        out=itile[:, lo:lo + 512], mask=winU[:], on_true=ngt[:],
                        on_false=itile[:, lo:lo + 512]),
                        aux_after=axl)
                    seq.step("vector", lambda v, lo=lo: v.tensor_scalar(
                        out=tmpc[:], in0=iotab[:], scalar1=FILLB,
                        scalar2=FILLA + lo * FILLB, op0=ALU.mult, op1=ALU.add),
                        aux_after=axl)
                    chunk_end = seq.step("vector", lambda v, lo=lo: v.select(
                        out=itile[:, lo:lo + 512], mask=futU[:], on_true=tmpc[:],
                        on_false=itile[:, lo:lo + 512]),
                        aux_after=axl)
                    last_idram = seq.step("sync", lambda s, lo=lo: s.dma_start(
                        out=idram[:, lo:lo + 512], in_=itile[:, lo:lo + 512]),
                        n_dma=1, aux=True)
                if prev_ti is not None:
                    emit_posttailE(prev_ti)
                NCH1 = L // 64
                seq.step("vector", lambda v, L=L, NCH1=NCH1: v.tensor_reduce(
                    out=m1[:, 0:NCH1],
                    in_=itile[:, 0:L].rearrange("p (a b) -> p a b", b=64),
                    axis=AX.X, op=ALU.max))
                for r in range(4):
                    seq.step("vector", lambda v, NCH1=NCH1: v.max(
                        out=m8[:], in_=m1[:, 0:NCH1]))
                    seq.step("vector", lambda v, NCH1=NCH1: v.max_index(
                        out=m8i[:], in_max=m8[:], in_values=m1[:, 0:NCH1]))
                    seq.step("vector", lambda v, r=r: v.tensor_copy(
                        out=cidu[:, r * 8:(r + 1) * 8], in_=m8i[:]))
                    if r < 3:
                        seq.step("vector", lambda v, NCH1=NCH1: v.match_replace(
                            out=m1[:, 0:NCH1], in_to_replace=m8[:],
                            in_values=m1[:, 0:NCH1], imm_value=ZAPV))
                seq.step("vector", lambda v: v.tensor_copy(out=cidf[:],
                                                           in_=cidu[:]))
                seq.step("vector", lambda v: v.tensor_scalar(
                    out=cneg[:], in0=cidf[:], scalar1=-1.0, scalar2=None,
                    op0=ALU.mult))
                for r in range(4):
                    seq.step("vector", lambda v: v.max(out=m8[:], in_=cneg[:]))
                    seq.step("vector", lambda v, r=r: v.tensor_scalar(
                        out=sortid[:, r * 8:(r + 1) * 8], in0=m8[:], scalar1=-1.0,
                        scalar2=None, op0=ALU.mult))
                    if r < 3:
                        seq.step("vector", lambda v: v.match_replace(
                            out=cneg[:], in_to_replace=m8[:], in_values=cneg[:],
                            imm_value=ZAPV))
                seq.step("vector", lambda v: v.tensor_scalar(
                    out=gtmp[:], in0=sortid[:], scalar1=prow[:, 0:1],
                    scalar2=None, op0=ALU.add))
                seq.step("vector", lambda v: v.tensor_copy(out=gidx[:],
                                                           in_=gtmp[:]))
                prev_ti = ti
                chunk_end_prev = chunk_end
                last_idram_prev = last_idram
            emit_gatherE()
            emit_posttailE(prev_ti)
            finish(seq, "semE", aux_name="semEx")

        with ExitStack() as ctx:
            E = ctx.enter_context
            NT = TPC // 128
            wout_s = E(nc.sbuf_tensor("F_wo", [128, 4 * 1024], MMDT))
            identf = E(nc.sbuf_tensor("F_idf", [128, 128], F32))
            identm = E(nc.sbuf_tensor("F_idm", [128, 128], MMDT))
            ones1 = E(nc.sbuf_tensor("F_o1", [1, 128], F32))
            crb = E(nc.sbuf_tensor("F_crb", [128, 512], F32))
            srb = E(nc.sbuf_tensor("F_srb", [128, 512], F32))
            rowr = E(nc.sbuf_tensor("F_rr", [1, 512], F32))
            cosjj_s = E(nc.sbuf_tensor("F_cjj", [128, RHALF], F32))
            sinjj_s = E(nc.sbuf_tensor("F_sjj", [128, RHALF], F32))
            drow = E(nc.sbuf_tensor("F_dr", [1, 4 * RHALF], F32))
            cdb = E(nc.sbuf_tensor("F_cdb", [128, RHALF], F32))
            sdb = E(nc.sbuf_tensor("F_sdb", [128, RHALF], F32))
            rr1 = E(nc.sbuf_tensor("F_rr1", [128, RHALF], F32))
            rr2 = E(nc.sbuf_tensor("F_rr2", [128, RHALF], F32))
            iotab = E(nc.sbuf_tensor("F_io", [128, 256], F32))
            tcol = E(nc.sbuf_tensor("F_tcl", [128, 1], F32))
            prowf = E(nc.sbuf_tensor("F_prw", [128, 1], F32))
            gidxf = E(nc.sbuf_tensor("F_gxf", [128, 2], F32))
            gidxi = E(nc.sbuf_tensor("F_gxi", [128, 2], I32))
            ng30 = E(nc.sbuf_tensor("F_ng", [128, 288], F32))
            wslab = E(nc.sbuf_tensor("F_ws", [128, 2 * 1088], BF16))
            slab96 = E(nc.sbuf_tensor("F_s96", [128, 1088], BF16))
            ktab = E(nc.sbuf_tensor("F_kt", [128, 32 * 1088], BF16))
            seli = E(nc.sbuf_tensor("F_si", [128, 32], I32))
            self_ = E(nc.sbuf_tensor("F_sf", [128, 32], F32))
            qct = E(nc.sbuf_tensor("F_qct", [128, 512], F32))
            qcm = E(nc.sbuf_tensor("F_qcm", [128, 512], MMDT))
            qrt = E(nc.sbuf_tensor("F_qrt", [128, 256], F32))
            mq = E(nc.sbuf_tensor("F_mq", [128, 256], F32))
            qhat = E(nc.sbuf_tensor("F_qh", [128, 256], F32))
            qhatm = E(nc.sbuf_tensor("F_qhm", [128, 256], MMDT))
            cos2_s = E(nc.sbuf_tensor("F_c2", [128, RHALF], F32))
            sin2_s = E(nc.sbuf_tensor("F_s2", [128, RHALF], F32))
            cost1_s = E(nc.sbuf_tensor("F_c1", [128, RHALF], F32))
            sint1_s = E(nc.sbuf_tensor("F_s1", [128, RHALF], F32))
            qcfm = E(nc.sbuf_tensor("F_qcf", [64, 8 * 128], MMDT))
            qhfm = E(nc.sbuf_tensor("F_qhf", [32, 8 * 128], MMDT))
            kwfm = E(nc.sbuf_tensor("F_kwf", [64, 8 * 256], MMDT))
            khfm = E(nc.sbuf_tensor("F_khf", [32, 256], MMDT))
            ptw = E(nc.sbuf_tensor("F_ptw", [128, 256], MMDT))
            cphi = E(nc.sbuf_tensor("F_cp", [128, 512], F32))
            sphi = E(nc.sbuf_tensor("F_sp", [128, 512], F32))
            krsel = E(nc.sbuf_tensor("F_krs", [128, 1024], F32))
            kkre = E(nc.sbuf_tensor("F_kre", [128, 512], F32))
            kkim = E(nc.sbuf_tensor("F_kim", [128, 512], F32))
            t5a = E(nc.sbuf_tensor("F_t5a", [128, 2048], F32))
            t5b = E(nc.sbuf_tensor("F_t5b", [128, 512], F32))
            t5c = E(nc.sbuf_tensor("F_t5c", [128, 512], F32))
            scoresb = E(nc.sbuf_tensor("F_sc", [128, 8 * 288], F32))
            probm = E(nc.sbuf_tensor("F_pm", [128, 8 * 288], MMDT))
            rowm = E(nc.sbuf_tensor("F_rm", [128, 8], F32))
            rows_ = E(nc.sbuf_tensor("F_rs", [128, 8], F32))
            winmU = E(nc.sbuf_tensor("F_wmu", [128, 256], U32))
            jmU = E(nc.sbuf_tensor("F_jmu", [128, 128], U32))
            rmask = E(nc.sbuf_tensor("F_rmk", [128, 1], U32))
            dsc = E(nc.sbuf_tensor("F_dsc", [128, 256], F32))
            vout = E(nc.sbuf_tensor("F_vo", [128, 512], F32))
            outtm = E(nc.sbuf_tensor("F_ot", [128, 512], F32))
            outm = E(nc.sbuf_tensor("F_om", [128, 512], MMDT))
            outfm = E(nc.sbuf_tensor("F_ofm", [128, 512], MMDT))
            x2row = E(nc.sbuf_tensor("F_x2r", [128, D], F32))
            z1 = E(nc.sbuf_tensor("F_z1", [128, D], F32))
            sqt = E(nc.sbuf_tensor("F_sq", [128, D], F32))
            rowv = E(nc.sbuf_tensor("F_rv", [128, 4], F32))
            kr96 = E(nc.sbuf_tensor("F_k96", [128, 32], F32))
            k2rot = E(nc.sbuf_tensor("F_k2r", [128, 32], F32))
            k2m = E(nc.sbuf_tensor("F_k2m", [128, 32], MMDT))
            k2fm = E(nc.sbuf_tensor("F_k2f", [32, 128], MMDT))
            k2cfm = E(nc.sbuf_tensor("F_k2c", [64, 8 * 128], MMDT))
            q2hat = E(nc.sbuf_tensor("F_q2h", [128, 256], F32))
            q2m = E(nc.sbuf_tensor("F_q2m", [128, 256], MMDT))
            q2fm = E(nc.sbuf_tensor("F_q2f", [32, 8 * 128], MMDT))
            sc2 = E(nc.sbuf_tensor("F_sc2", [128, 8 * 128], F32))
            pr2m = E(nc.sbuf_tensor("F_pr2", [128, 8 * 128], MMDT))
            vout2 = E(nc.sbuf_tensor("F_vo2", [128, 512], F32))
            rowm2 = E(nc.sbuf_tensor("F_rm2", [128, 8], F32))
            psW0 = E(nc.psum_tensor("F_w0", [128, 512], F32))
            psW1 = E(nc.psum_tensor("F_w1", [128, 512], F32))
            psW2 = E(nc.psum_tensor("F_w2", [128, 512], F32))
            psW3 = E(nc.psum_tensor("F_w3", [128, 512], F32))
            pstm = E(nc.psum_tensor("F_ptm", [128, 512], MMDT))
            out_ps = E(nc.psum_tensor("F_ops", [128, 512], F32))
            psF = E(nc.psum_tensor("F_psF", [128, 128], F32))
            psW = (psW0, psW1, psW2, psW3)
            seq = segment()
            seq.step("vector", lambda v: v.memset(ones1[:], 1.0))
            seq.step("vector", lambda v: v.memset(ng30[:], NG30))
            seq.step("gpsimd", lambda g: g.iota(
                iotab[:], pattern=[[1, 256]], base=0, channel_multiplier=0,
                allow_small_or_imprecise_dtypes=True))
            seq.step("gpsimd", lambda g: g.iota(
                prowf[:], pattern=[[0, 1]], base=0, channel_multiplier=1,
                allow_small_or_imprecise_dtypes=True))
            seq.step("sync", lambda s: [
                s.dma_start(out=identf[:], in_=ident_d),
                s.dma_start(out=rowr[:], in_=cosr_d),
                s.dma_start(out=cosjj_s[:], in_=cosjj_d),
                s.dma_start(out=sinjj_s[:], in_=sinjj_d)], n_dma=4)
            seq.step("vector", lambda v: v.tensor_copy(out=identm[:],
                                                       in_=identf[:]))
            seq.step("tensor", lambda t: t.matmul(psW0[:], ones1[:], rowr[:],
                                                  start=True, stop=True))
            seq.step("vector", lambda v: v.tensor_copy(out=crb[:], in_=psW0[:]))
            seq.step("sync", lambda s: s.dma_start(out=rowr[:], in_=sinr_d),
                     n_dma=1)
            seq.step("tensor", lambda t: t.matmul(psW0[:], ones1[:], rowr[:],
                                                  start=True, stop=True))
            seq.step("vector", lambda v: v.tensor_copy(out=srb[:], in_=psW0[:]))
            seq.step("gpsimd", lambda g: [
                g.dma_start(out=wout_s[:, c * 1024:(c + 1) * 1024],
                            in_=wout1_d[c * 128:(c + 1) * 128, :])
                for c in range(4)], n_dma=4)
            seq.step("sync", lambda s: s.dma_start(out=drow[0:1, 0:RHALF],
                                                   in_=rdel_d), n_dma=1)
            seq.step("scalar", lambda s: s.activation(
                out=drow[0:1, 0:RHALF], in_=drow[0:1, 0:RHALF], func=AF.Sigmoid))
            seq.step("vector", lambda v: v.tensor_scalar(
                out=drow[0:1, 0:RHALF], in0=drow[0:1, 0:RHALF],
                scalar1=-2.0 * PI, scalar2=None, op0=ALU.mult))
            seq.step("vector", lambda v: v.tensor_scalar(
                out=drow[0:1, RHALF:2 * RHALF], in0=drow[0:1, 0:RHALF],
                scalar1=-PI, scalar2=None, op0=ALU.is_lt))
            seq.step("vector", lambda v: v.scalar_tensor_tensor(
                out=drow[0:1, 0:RHALF], in0=drow[0:1, RHALF:2 * RHALF],
                scalar=2.0 * PI, in1=drow[0:1, 0:RHALF],
                op0=ALU.mult, op1=ALU.add))
            seq.step("scalar", lambda s: s.activation(
                out=drow[0:1, RHALF:2 * RHALF], in_=drow[0:1, 0:RHALF],
                func=AF.Sin))
            seq.step("scalar", lambda s: s.activation(
                out=drow[0:1, 2 * RHALF:3 * RHALF], in_=drow[0:1, 0:RHALF],
                func=AF.Abs))
            seq.step("vector", lambda v: v.tensor_scalar(
                out=drow[0:1, 2 * RHALF:3 * RHALF],
                in0=drow[0:1, 2 * RHALF:3 * RHALF],
                scalar1=-1.0, scalar2=PI / 2, op0=ALU.mult, op1=ALU.add))
            seq.step("scalar", lambda s: s.activation(
                out=drow[0:1, 2 * RHALF:3 * RHALF],
                in_=drow[0:1, 2 * RHALF:3 * RHALF], func=AF.Sin))
            seq.step("tensor", lambda t: t.matmul(
                psW1[:, 0:RHALF], ones1[:], drow[0:1, 2 * RHALF:3 * RHALF],
                start=True, stop=True))
            seq.step("vector", lambda v: v.tensor_copy(out=cdb[:],
                                                       in_=psW1[:, 0:RHALF]))
            seq.step("tensor", lambda t: t.matmul(
                psW1[:, 0:RHALF], ones1[:], drow[0:1, RHALF:2 * RHALF],
                start=True, stop=True))
            seq.step("vector", lambda v: v.tensor_copy(out=sdb[:],
                                                       in_=psW1[:, 0:RHALF]))
            crv0 = crb[:].rearrange("p (k x) -> p k x", x=RHALF)
            srv0 = srb[:].rearrange("p (k x) -> p k x", x=RHALF)
            cdv = cdb[:].rearrange("p (o x) -> p o x", o=1).to_broadcast(
                [128, 32, RHALF])
            sdv = sdb[:].rearrange("p (o x) -> p o x", o=1).to_broadcast(
                [128, 32, RHALF])
            t5av0 = t5a[:, 0:512].rearrange("p (k x) -> p k x", x=RHALF)
            t5bv0 = t5b[:, 0:512].rearrange("p (k x) -> p k x", x=RHALF)
            seq.step("vector", lambda v: v.tensor_tensor(out=t5av0, in0=crv0,
                                                         in1=cdv, op=ALU.mult))
            seq.step("vector", lambda v: v.tensor_tensor(out=t5bv0, in0=srv0,
                                                         in1=sdv, op=ALU.mult))
            seq.step("vector", lambda v: v.tensor_sub(
                out=t5a[:, 0:512], in0=t5a[:, 0:512], in1=t5b[:, 0:512]))
            seq.step("vector", lambda v: v.tensor_tensor(out=t5bv0, in0=srv0,
                                                         in1=cdv, op=ALU.mult))
            seq.step("vector", lambda v: v.tensor_tensor(out=srv0, in0=crv0,
                                                         in1=sdv, op=ALU.mult))
            seq.step("vector", lambda v: v.tensor_add(out=srb[:], in0=srb[:],
                                                      in1=t5b[:, 0:512]))
            seq.step("vector", lambda v: v.tensor_copy(out=crb[:],
                                                       in_=t5a[:, 0:512]))
            seq.step("vector", lambda v: v.tensor_tensor(out=rr1[:], in0=cosjj_s[:],
                                                         in1=cdb[:], op=ALU.mult))
            seq.step("vector", lambda v: v.tensor_tensor(out=rr2[:], in0=sinjj_s[:],
                                                         in1=sdb[:], op=ALU.mult))
            seq.step("vector", lambda v: v.tensor_sub(out=rr1[:], in0=rr1[:],
                                                      in1=rr2[:]))
            seq.step("vector", lambda v: v.tensor_tensor(out=rr2[:], in0=sinjj_s[:],
                                                         in1=cdb[:], op=ALU.mult))
            seq.step("vector", lambda v: v.tensor_tensor(out=sinjj_s[:],
                                                         in0=cosjj_s[:],
                                                         in1=sdb[:], op=ALU.mult))
            seq.step("vector", lambda v: v.tensor_add(out=sinjj_s[:],
                                                      in0=sinjj_s[:], in1=rr2[:]))
            seq.step("vector", lambda v: v.tensor_copy(out=cosjj_s[:], in_=rr1[:]))

            def softplus_steps(dst, src, tmp):
                seq.step("scalar", lambda s: s.activation(out=tmp, in_=src,
                                                          func=AF.Abs))
                seq.step("scalar", lambda s: s.activation(out=tmp, in_=tmp,
                                                          func=AF.Exp, scale=-1.0))
                seq.step("vector", lambda v: v.tensor_scalar(
                    out=tmp, in0=tmp, scalar1=1.0, scalar2=None, op0=ALU.add))
                seq.step("scalar", lambda s: s.activation(out=tmp, in_=tmp,
                                                          func=AF.Ln))
                seq.step("vector", lambda v: v.tensor_scalar(
                    out=dst, in0=src, scalar1=0.0, scalar2=None, op0=ALU.max))
                seq.step("vector", lambda v: v.tensor_add(out=dst, in0=dst,
                                                          in1=tmp))

            for ti in range(NT):
                seq.step("sync", lambda s, ti=ti: [
                    s.dma_start(out=self_[:], in_=sel_s_d[ti * 128:(ti + 1) * 128, :]),
                    s.dma_start(out=tcol[:], in_=tcol_d[ti * 128:(ti + 1) * 128, :]),
                    s.dma_start(out=cos2_s[:], in_=cos2_d[ti * 128:(ti + 1) * 128, :]),
                    s.dma_start(out=sin2_s[:], in_=sin2_d[ti * 128:(ti + 1) * 128, :]),
                    s.dma_start(out=cost1_s[:], in_=cost1_d[ti * 128:(ti + 1) * 128, :]),
                    s.dma_start(out=sint1_s[:], in_=sint1_d[ti * 128:(ti + 1) * 128, :]),
                    ], n_dma=6)
                seq.step("vector", lambda v: v.tensor_copy(out=seli[:],
                                                           in_=self_[:]))
                for (ca, sa) in ((cos2_s, sin2_s), (cost1_s, sint1_s)):
                    seq.step("vector", lambda v, ca=ca: v.tensor_tensor(
                        out=rr1[:], in0=ca[:], in1=cdb[:], op=ALU.mult))
                    seq.step("vector", lambda v, sa=sa: v.tensor_tensor(
                        out=rr2[:], in0=sa[:], in1=sdb[:], op=ALU.mult))
                    seq.step("vector", lambda v: v.tensor_sub(
                        out=rr1[:], in0=rr1[:], in1=rr2[:]))
                    seq.step("vector", lambda v, sa=sa: v.tensor_tensor(
                        out=rr2[:], in0=sa[:], in1=cdb[:], op=ALU.mult))
                    seq.step("vector", lambda v, ca=ca, sa=sa: v.tensor_tensor(
                        out=sa[:], in0=ca[:], in1=sdb[:], op=ALU.mult))
                    seq.step("vector", lambda v, sa=sa: v.tensor_add(
                        out=sa[:], in0=sa[:], in1=rr2[:]))
                    seq.step("vector", lambda v, ca=ca: v.tensor_copy(
                        out=ca[:], in_=rr1[:]))
                seq.step("vector", lambda v: v.tensor_copy(
                    out=gidxi[:, 0:1], in_=tcol[:]))
                seq.step("gpsimd", lambda g: [
                    g.indirect_dma_start(
                        out=qct[:], out_offset=None, in_=qc_s_d,
                        in_offset=IndirectOffsetOnAxis(gidxi[:, 0:1], axis=0)),
                    g.indirect_dma_start(
                        out=qrt[:], out_offset=None, in_=qr_s_d,
                        in_offset=IndirectOffsetOnAxis(gidxi[:, 0:1], axis=0)),
                    g.indirect_dma_start(
                        out=x2row[:], out_offset=None, in_=x2t_s_d,
                        in_offset=IndirectOffsetOnAxis(gidxi[:, 0:1], axis=0))],
                    n_dma=3)
                seq.step("vector", lambda v: v.tensor_scalar(
                    out=gidxf[:, 0:1], in0=tcol[:], scalar1=-64.0, scalar2=0.0,
                    op0=ALU.add, op1=ALU.max))
                seq.step("vector", lambda v: v.tensor_scalar(
                    out=gidxf[:, 1:2], in0=tcol[:], scalar1=64.0, scalar2=8191.0,
                    op0=ALU.add, op1=ALU.min))
                seq.step("vector", lambda v: v.tensor_copy(out=gidxi[:],
                                                           in_=gidxf[:]))
                seq.step("gpsimd", lambda g: [
                    g.indirect_dma_start(
                        out=wslab[:, 0:1088], out_offset=None, in_=tab_s_d,
                        in_offset=IndirectOffsetOnAxis(gidxi[:, 0:1], axis=0)),
                    g.indirect_dma_start(
                        out=wslab[:, 1088:2176], out_offset=None, in_=tab_s_d,
                        in_offset=IndirectOffsetOnAxis(gidxi[:, 1:2], axis=0))],
                    n_dma=2)
                seq.step("gpsimd", lambda g: [
                    g.indirect_dma_start(
                        out=ktab[:, gg * 1088:(gg + 1) * 1088], out_offset=None,
                        in_=tab_s_d,
                        in_offset=IndirectOffsetOnAxis(seli[:, gg:gg + 1], axis=0))
                    for gg in range(32)], n_dma=32)
                seq.step("vector", lambda v: v.tensor_scalar(
                    out=qcm[:], in0=qct[:], scalar1=RSCALE, scalar2=None,
                    op0=ALU.mult))
                softplus_steps(mq[:], qrt[:], t5a[:, 0:256])
                for h in range(8):
                    m1s = mq[:, h * 32: h * 32 + 16]
                    m2s = mq[:, h * 32 + 16: h * 32 + 32]
                    seq.step("vector", lambda v, m1s=m1s: v.tensor_tensor(
                        out=t5b[:, 0:16], in0=m1s, in1=cos2_s[:], op=ALU.mult))
                    seq.step("vector", lambda v, m2s=m2s: v.tensor_tensor(
                        out=t5c[:, 0:16], in0=m2s, in1=sin2_s[:], op=ALU.mult))
                    seq.step("vector", lambda v: v.tensor_sub(
                        out=t5b[:, 0:16], in0=t5b[:, 0:16], in1=t5c[:, 0:16]))
                    seq.step("vector", lambda v, h=h: v.tensor_scalar(
                        out=qhat[:, h * 32:h * 32 + 16], in0=t5b[:, 0:16],
                        scalar1=RSCALE, scalar2=None, op0=ALU.mult))
                    seq.step("vector", lambda v, m1s=m1s: v.tensor_tensor(
                        out=t5b[:, 0:16], in0=m1s, in1=sin2_s[:], op=ALU.mult))
                    seq.step("vector", lambda v, m2s=m2s: v.tensor_tensor(
                        out=t5c[:, 0:16], in0=m2s, in1=cos2_s[:], op=ALU.mult))
                    seq.step("vector", lambda v: v.tensor_add(
                        out=t5b[:, 0:16], in0=t5b[:, 0:16], in1=t5c[:, 0:16]))
                    seq.step("vector", lambda v, h=h: v.tensor_scalar(
                        out=qhat[:, h * 32 + 16:h * 32 + 32], in0=t5b[:, 0:16],
                        scalar1=RSCALE, scalar2=None, op0=ALU.mult))
                seq.step("vector", lambda v: v.tensor_copy(out=qhatm[:],
                                                           in_=qhat[:]))
                for h in range(8):
                    seq.step("tensor", lambda t, h=h: t.transpose(
                        pstm[0:64, 0:128], qcm[:, h * 64:(h + 1) * 64], identm[:]))
                    seq.step("vector", lambda v, h=h: v.tensor_copy(
                        out=qcfm[0:64, h * 128:(h + 1) * 128],
                        in_=pstm[0:64, 0:128]))
                    seq.step("tensor", lambda t, h=h: t.transpose(
                        pstm[0:32, 0:128], qhatm[:, h * 32:(h + 1) * 32],
                        identm[:]))
                    seq.step("vector", lambda v, h=h: v.tensor_copy(
                        out=qhfm[0:32, h * 128:(h + 1) * 128],
                        in_=pstm[0:32, 0:128]))
                for ss2 in range(2):
                    for h in range(8):
                        seq.step("tensor", lambda t, ss2=ss2, h=h: t.transpose(
                            pstm[0:64, 0:128],
                            wslab[:, ss2 * 1088 + h * 64: ss2 * 1088 + (h + 1) * 64],
                            identm[:]))
                        seq.step("vector", lambda v, ss2=ss2, h=h: v.tensor_copy(
                            out=kwfm[0:64, h * 256 + ss2 * 128: h * 256 + ss2 * 128 + 128],
                            in_=pstm[0:64, 0:128]))
                    seq.step("tensor", lambda t, ss2=ss2: t.transpose(
                        pstm[0:32, 0:128],
                        wslab[:, ss2 * 1088 + 1056: ss2 * 1088 + 1088], identm[:]))
                    seq.step("vector", lambda v, ss2=ss2: v.tensor_copy(
                        out=khfm[0:32, ss2 * 128:(ss2 + 1) * 128],
                        in_=pstm[0:32, 0:128]))
                for h in range(8):
                    bank = psW[h // 2]
                    off = (h % 2) * 256

                    def mmw(t, h=h, bank=bank, off=off):
                        t.matmul(bank[:, off:off + 256],
                                 qcfm[0:64, h * 128:(h + 1) * 128],
                                 kwfm[0:64, h * 256:(h + 1) * 256],
                                 start=True, stop=False)
                        return t.matmul(bank[:, off:off + 256],
                                        qhfm[0:32, h * 128:(h + 1) * 128],
                                        khfm[0:32, 0:256], start=False, stop=True)
                    seq.step("tensor", mmw)
                seq.step("vector", lambda v: v.tensor_scalar(
                    out=dsc[:], in0=iotab[:], scalar1=prowf[:, 0:1],
                    scalar2=-64.0, op0=ALU.subtract, op1=ALU.add))
                seq.step("vector", lambda v: v.tensor_scalar(
                    out=winmU[:], in0=dsc[:], scalar1=-63.0, scalar2=None,
                    op0=ALU.is_lt))
                seq.step("vector", lambda v: v.tensor_scalar(
                    out=dsc[:], in0=dsc[:], scalar1=0.0, scalar2=None,
                    op0=ALU.is_gt))
                seq.step("vector", lambda v: v.tensor_tensor(
                    out=winmU[:], in0=winmU[:], in1=dsc[:], op=ALU.add))
                for h in range(8):
                    bank = psW[h // 2]
                    off = (h % 2) * 256
                    seq.step("vector", lambda v, h=h, bank=bank, off=off: v.select(
                        out=scoresb[:, h * 288:h * 288 + 256], mask=winmU[:],
                        on_true=ng30[:, 0:256], on_false=bank[:, off:off + 256]))
                ctv = cost1_s[:].rearrange("p (o x) -> p o x", o=1).to_broadcast(
                    [128, 32, RHALF])
                stv = sint1_s[:].rearrange("p (o x) -> p o x", o=1).to_broadcast(
                    [128, 32, RHALF])
                crv = crb[:].rearrange("p (k x) -> p k x", x=RHALF)
                srv = srb[:].rearrange("p (k x) -> p k x", x=RHALF)
                cpv = cphi[:].rearrange("p (k x) -> p k x", x=RHALF)
                spv = sphi[:].rearrange("p (k x) -> p k x", x=RHALF)
                t5av = t5a[:, 0:512].rearrange("p (k x) -> p k x", x=RHALF)
                seq.step("vector", lambda v: v.tensor_tensor(
                    out=cpv, in0=ctv, in1=crv, op=ALU.mult))
                seq.step("vector", lambda v: v.tensor_tensor(
                    out=t5av, in0=stv, in1=srv, op=ALU.mult))
                seq.step("vector", lambda v: v.tensor_add(
                    out=cphi[:], in0=cphi[:], in1=t5a[:, 0:512]))
                seq.step("vector", lambda v: v.tensor_tensor(
                    out=spv, in0=stv, in1=crv, op=ALU.mult))
                seq.step("vector", lambda v: v.tensor_tensor(
                    out=t5av, in0=ctv, in1=srv, op=ALU.mult))
                seq.step("vector", lambda v: v.tensor_sub(
                    out=sphi[:], in0=sphi[:], in1=t5a[:, 0:512]))
                seq.step("vector", lambda v: v.tensor_copy(
                    out=krsel[:].rearrange("p (k x) -> p k x", x=32),
                    in_=ktab[:].rearrange("p (k c) -> p k c", c=1088)[:, :, 1024:1056]))
                softplus_steps(krsel[:], krsel[:], t5a[:, 0:1024])
                k1v = krsel[:].rearrange("p (k x) -> p k x", x=32)[:, :, 0:16]
                k2v = krsel[:].rearrange("p (k x) -> p k x", x=32)[:, :, 16:32]
                kkrev = kkre[:].rearrange("p (k x) -> p k x", x=RHALF)
                kkimv = kkim[:].rearrange("p (k x) -> p k x", x=RHALF)
                seq.step("vector", lambda v: v.tensor_tensor(
                    out=kkrev, in0=k1v, in1=cpv, op=ALU.mult))
                seq.step("vector", lambda v: v.tensor_tensor(
                    out=t5av, in0=k2v, in1=spv, op=ALU.mult))
                seq.step("vector", lambda v: v.tensor_add(
                    out=kkre[:], in0=kkre[:], in1=t5a[:, 0:512]))
                seq.step("vector", lambda v: v.tensor_tensor(
                    out=kkimv, in0=k1v, in1=spv, op=ALU.mult))
                seq.step("vector", lambda v: v.tensor_tensor(
                    out=t5av, in0=k2v, in1=cpv, op=ALU.mult))
                seq.step("vector", lambda v: v.tensor_sub(
                    out=kkim[:], in0=kkim[:], in1=t5a[:, 0:512]))
                for h in range(8):
                    qv = qcm[:, h * 64:(h + 1) * 64].rearrange(
                        "p (o x) -> p o x", o=1).to_broadcast([128, 32, 64])
                    kv = ktab[:].rearrange("p (k c) -> p k c",
                                           c=1088)[:, :, h * 64:(h + 1) * 64]
                    t5a2k = t5a[:].rearrange("p (k x) -> p k x", x=64)
                    seq.step("vector", lambda v, qv=qv, kv=kv, t5a2k=t5a2k:
                             v.tensor_tensor(out=t5a2k, in0=qv, in1=kv,
                                             op=ALU.mult))
                    seq.step("vector", lambda v, t5a2k=t5a2k: v.tensor_reduce(
                        out=t5b[:, 0:32], in_=t5a2k, axis=AX.X, op=ALU.add))
                    m1v = mq[:, h * 32:h * 32 + 16].rearrange(
                        "p (o x) -> p o x", o=1).to_broadcast([128, 32, RHALF])
                    m2v = mq[:, h * 32 + 16:h * 32 + 32].rearrange(
                        "p (o x) -> p o x", o=1).to_broadcast([128, 32, RHALF])
                    t5ck = t5c[:].rearrange("p (k x) -> p k x", x=RHALF)
                    seq.step("vector", lambda v, m1v=m1v, t5ck=t5ck:
                             v.tensor_tensor(out=t5ck, in0=m1v, in1=kkrev,
                                             op=ALU.mult))
                    seq.step("vector", lambda v, t5ck=t5ck: v.tensor_reduce(
                        out=t5b[:, 32:64], in_=t5ck, axis=AX.X, op=ALU.add))
                    seq.step("vector", lambda v, m2v=m2v, t5ck=t5ck:
                             v.tensor_tensor(out=t5ck, in0=m2v, in1=kkimv,
                                             op=ALU.mult))
                    seq.step("vector", lambda v, t5ck=t5ck: v.tensor_reduce(
                        out=t5b[:, 64:96], in_=t5ck, axis=AX.X, op=ALU.add))
                    seq.step("vector", lambda v: v.tensor_sub(
                        out=t5b[:, 32:64], in0=t5b[:, 32:64], in1=t5b[:, 64:96]))
                    seq.step("vector", lambda v: v.tensor_scalar(
                        out=t5b[:, 32:64], in0=t5b[:, 32:64], scalar1=RSCALE,
                        scalar2=None, op0=ALU.mult))
                    seq.step("vector", lambda v, h=h: v.tensor_add(
                        out=scoresb[:, h * 288 + 256:h * 288 + 288],
                        in0=t5b[:, 0:32], in1=t5b[:, 32:64]))
                sv = scoresb[:].rearrange("p (h x) -> p h x", x=288)
                seq.step("vector", lambda v: v.tensor_reduce(
                    out=rowm[:], in_=sv, axis=AX.X, op=ALU.max))
                seq.step("vector", lambda v: v.tensor_tensor(
                    out=sv, in0=sv,
                    in1=rowm[:].rearrange("p (h o) -> p h o", o=1).to_broadcast(
                        [128, 8, 288]), op=ALU.subtract))
                seq.step("scalar", lambda s: s.activation(
                    out=scoresb[:], in_=scoresb[:], func=AF.Exp))
                seq.step("vector", lambda v: v.tensor_reduce(
                    out=rows_[:], in_=sv, axis=AX.X, op=ALU.add))
                seq.step("vector", lambda v: v.reciprocal(out=rows_[:],
                                                          in_=rows_[:]))
                seq.step("vector", lambda v: v.tensor_tensor(
                    out=probm[:].rearrange("p (h x) -> p h x", x=288), in0=sv,
                    in1=rows_[:].rearrange("p (h o) -> p h o", o=1).to_broadcast(
                        [128, 8, 288]), op=ALU.mult))
                for h in range(8):
                    def trp(t, h=h):
                        t.transpose(pstm[:, 0:128],
                                    probm[:, h * 288:h * 288 + 128], identm[:])
                        return t.transpose(pstm[:, 128:256],
                                           probm[:, h * 288 + 128:h * 288 + 256],
                                           identm[:])
                    seq.step("tensor", trp)
                    seq.step("vector", lambda v: v.tensor_copy(
                        out=ptw[:], in_=pstm[:, 0:256]))

                    def mmv(t, h=h):
                        t.matmul(out_ps[:, h * 64:(h + 1) * 64], ptw[:, 0:128],
                                 wslab[:, 512 + h * 64:512 + (h + 1) * 64],
                                 start=True, stop=False)
                        return t.matmul(out_ps[:, h * 64:(h + 1) * 64],
                                        ptw[:, 128:256],
                                        wslab[:, 1088 + 512 + h * 64:1088 + 512 + (h + 1) * 64],
                                        start=False, stop=True)
                    seq.step("tensor", mmv)
                    pv = probm[:, h * 288 + 256:h * 288 + 288].rearrange(
                        "p (k o) -> p o k", o=1).to_broadcast([128, 64, 32])
                    vv = ktab[:].rearrange("p (k c) -> p c k", c=1088)[
                        :, 512 + h * 64:512 + (h + 1) * 64, :]
                    t5adk = t5a[:].rearrange("p (d k) -> p d k", k=32)
                    seq.step("vector", lambda v, pv=pv, vv=vv, t5adk=t5adk:
                             v.tensor_tensor(out=t5adk, in0=pv, in1=vv,
                                             op=ALU.mult))
                    seq.step("vector", lambda v, h=h, t5adk=t5adk: v.tensor_reduce(
                        out=vout[:, h * 64:(h + 1) * 64], in_=t5adk, axis=AX.X,
                        op=ALU.add))
                seq.step("vector", lambda v: v.tensor_tensor(
                    out=outtm[:], in0=out_ps[:], in1=vout[:], op=ALU.add))

                if ti == 0:
                    seq.step("sync", lambda s: s.dma_start(
                        out=slab96[:], in_=tab_s_d[0:128, :]), n_dma=1)
                    for h in range(8):
                        m1s = mq[:, h * 32: h * 32 + 16]
                        m2s = mq[:, h * 32 + 16: h * 32 + 32]
                        seq.step("vector", lambda v, m1s=m1s: v.tensor_tensor(
                            out=t5b[:, 0:16], in0=m1s, in1=cost1_s[:],
                            op=ALU.mult))
                        seq.step("vector", lambda v, m2s=m2s: v.tensor_tensor(
                            out=t5c[:, 0:16], in0=m2s, in1=sint1_s[:],
                            op=ALU.mult))
                        seq.step("vector", lambda v: v.tensor_sub(
                            out=t5b[:, 0:16], in0=t5b[:, 0:16], in1=t5c[:, 0:16]))
                        seq.step("vector", lambda v, h=h: v.tensor_scalar(
                            out=q2hat[:, h * 32:h * 32 + 16], in0=t5b[:, 0:16],
                            scalar1=RSCALE, scalar2=None, op0=ALU.mult))
                        seq.step("vector", lambda v, m1s=m1s: v.tensor_tensor(
                            out=t5b[:, 0:16], in0=m1s, in1=sint1_s[:],
                            op=ALU.mult))
                        seq.step("vector", lambda v, m2s=m2s: v.tensor_tensor(
                            out=t5c[:, 0:16], in0=m2s, in1=cost1_s[:],
                            op=ALU.mult))
                        seq.step("vector", lambda v: v.tensor_add(
                            out=t5b[:, 0:16], in0=t5b[:, 0:16], in1=t5c[:, 0:16]))
                        seq.step("vector", lambda v, h=h: v.tensor_scalar(
                            out=q2hat[:, h * 32 + 16:h * 32 + 32],
                            in0=t5b[:, 0:16],
                            scalar1=RSCALE, scalar2=None, op0=ALU.mult))
                    seq.step("vector", lambda v: v.tensor_copy(out=q2m[:],
                                                               in_=q2hat[:]))
                    for h in range(8):
                        seq.step("tensor", lambda t, h=h: t.transpose(
                            pstm[0:32, 0:128], q2m[:, h * 32:(h + 1) * 32],
                            identm[:]))
                        seq.step("vector", lambda v, h=h: v.tensor_copy(
                            out=q2fm[0:32, h * 128:(h + 1) * 128],
                            in_=pstm[0:32, 0:128]))
                        seq.step("tensor", lambda t, h=h: t.transpose(
                            pstm[0:64, 0:128], slab96[:, h * 64:(h + 1) * 64],
                            identm[:]))
                        seq.step("vector", lambda v, h=h: v.tensor_copy(
                            out=k2cfm[0:64, h * 128:(h + 1) * 128],
                            in_=pstm[0:64, 0:128]))
                    seq.step("vector", lambda v: v.tensor_copy(
                        out=kr96[:], in_=slab96[:, 1024:1056]))
                    softplus_steps(kr96[:], kr96[:], t5a[:, 0:32])
                    seq.step("vector", lambda v: v.tensor_tensor(
                        out=k2rot[:, 0:16], in0=kr96[:, 0:16], in1=cosjj_s[:],
                        op=ALU.mult))
                    seq.step("vector", lambda v: v.tensor_tensor(
                        out=t5c[:, 0:16], in0=kr96[:, 16:32], in1=sinjj_s[:],
                        op=ALU.mult))
                    seq.step("vector", lambda v: v.tensor_sub(
                        out=k2rot[:, 0:16], in0=k2rot[:, 0:16],
                        in1=t5c[:, 0:16]))
                    seq.step("vector", lambda v: v.tensor_tensor(
                        out=k2rot[:, 16:32], in0=kr96[:, 0:16], in1=sinjj_s[:],
                        op=ALU.mult))
                    seq.step("vector", lambda v: v.tensor_tensor(
                        out=t5c[:, 0:16], in0=kr96[:, 16:32], in1=cosjj_s[:],
                        op=ALU.mult))
                    seq.step("vector", lambda v: v.tensor_add(
                        out=k2rot[:, 16:32], in0=k2rot[:, 16:32],
                        in1=t5c[:, 0:16]))
                    seq.step("vector", lambda v: v.tensor_copy(out=k2m[:],
                                                               in_=k2rot[:]))
                    seq.step("tensor", lambda t: t.transpose(
                        pstm[0:32, 0:128], k2m[:], identm[:]))
                    seq.step("vector", lambda v: v.tensor_copy(
                        out=k2fm[0:32, 0:128], in_=pstm[0:32, 0:128]))
                    for h in range(8):
                        bank = psW[h // 2]
                        off = (h % 2) * 128

                        def mm2(t, h=h, bank=bank, off=off):
                            t.matmul(bank[:, off:off + 128],
                                     qcfm[0:64, h * 128:(h + 1) * 128],
                                     k2cfm[0:64, h * 128:(h + 1) * 128],
                                     start=True, stop=False)
                            return t.matmul(bank[:, off:off + 128],
                                            q2fm[0:32, h * 128:(h + 1) * 128],
                                            k2fm[0:32, 0:128],
                                            start=False, stop=True)
                        seq.step("tensor", mm2)
                    seq.step("vector", lambda v: v.tensor_scalar(
                        out=jmU[:], in0=iotab[:, 0:128], scalar1=95.5,
                        scalar2=None, op0=ALU.is_gt))
                    for h in range(8):
                        bank = psW[h // 2]
                        off = (h % 2) * 128
                        seq.step("vector", lambda v, h=h, bank=bank, off=off:
                                 v.select(
                                     out=sc2[:, h * 128:(h + 1) * 128],
                                     mask=jmU[:], on_true=ng30[:, 0:128],
                                     on_false=bank[:, off:off + 128]))
                    sv2 = sc2[:].rearrange("p (h x) -> p h x", x=128)
                    seq.step("vector", lambda v: v.tensor_reduce(
                        out=rowm2[:], in_=sv2, axis=AX.X, op=ALU.max))
                    seq.step("vector", lambda v: v.tensor_tensor(
                        out=sv2, in0=sv2,
                        in1=rowm2[:].rearrange("p (h o) -> p h o",
                                               o=1).to_broadcast([128, 8, 128]),
                        op=ALU.subtract))
                    seq.step("scalar", lambda s: s.activation(
                        out=sc2[:], in_=sc2[:], func=AF.Exp))
                    seq.step("vector", lambda v: v.tensor_reduce(
                        out=rowm2[:], in_=sv2, axis=AX.X, op=ALU.add))
                    seq.step("vector", lambda v: v.reciprocal(out=rowm2[:],
                                                              in_=rowm2[:]))
                    seq.step("vector", lambda v: v.tensor_tensor(
                        out=pr2m[:].rearrange("p (h x) -> p h x", x=128),
                        in0=sv2,
                        in1=rowm2[:].rearrange("p (h o) -> p h o",
                                               o=1).to_broadcast([128, 8, 128]),
                        op=ALU.mult))
                    for h in range(8):
                        seq.step("tensor", lambda t, h=h: t.transpose(
                            pstm[:, 0:128], pr2m[:, h * 128:(h + 1) * 128],
                            identm[:]))
                        seq.step("vector", lambda v: v.tensor_copy(
                            out=ptw[:, 0:128], in_=pstm[:, 0:128]))
                        seq.step("tensor", lambda t, h=h: t.matmul(
                            out_ps[:, h * 64:(h + 1) * 64], ptw[:, 0:128],
                            slab96[:, 512 + h * 64:512 + (h + 1) * 64],
                            start=True, stop=True))
                    seq.step("vector", lambda v: v.tensor_copy(out=vout2[:],
                                                               in_=out_ps[:]))
                    seq.step("vector", lambda v: v.tensor_scalar(
                        out=rmask[:], in0=tcol[:], scalar1=62.5, scalar2=None,
                        op0=ALU.is_lt))
                    seq.step("vector", lambda v: v.select(
                        out=outtm[:], mask=rmask[:].to_broadcast([128, 512]),
                        on_true=vout2[:], on_false=outtm[:]))

                seq.step("vector", lambda v: v.tensor_copy(out=outm[:],
                                                           in_=outtm[:]))
                for ec in range(4):
                    seq.step("tensor", lambda t, ec=ec: t.transpose(
                        pstm[:, 0:128], outm[:, ec * 128:(ec + 1) * 128],
                        identm[:]))
                    seq.step("vector", lambda v, ec=ec: v.tensor_copy(
                        out=outfm[:, ec * 128:(ec + 1) * 128],
                        in_=pstm[:, 0:128]))

                for hh in range(2):
                    def mmoZ(t, hh=hh):
                        last = None
                        for ec in range(4):
                            last = t.matmul(
                                out_ps[:],
                                outfm[:, ec * 128:(ec + 1) * 128],
                                wout_s[:, ec * 1024 + hh * 512: ec * 1024 + hh * 512 + 512],
                                start=(ec == 0), stop=(ec == 3))
                        return last
                    seq.step("tensor", mmoZ)
                    seq.step("vector", lambda v, hh=hh: v.tensor_tensor(
                        out=z1[:, hh * 512:(hh + 1) * 512], in0=out_ps[:],
                        in1=x2row[:, hh * 512:(hh + 1) * 512], op=ALU.add))
                seq.step("vector", lambda v: v.tensor_reduce(
                    out=rowv[:, 0:1], in_=z1[:], axis=AX.X, op=ALU.add))
                seq.step("vector", lambda v: v.tensor_scalar(
                    out=rowv[:, 0:1], in0=rowv[:, 0:1], scalar1=1.0 / D,
                    scalar2=None, op0=ALU.mult))
                seq.step("scalar", lambda s: s.activation(out=sqt[:], in_=z1[:],
                                                          func=AF.Square))
                seq.step("vector", lambda v: v.tensor_reduce(
                    out=rowv[:, 1:2], in_=sqt[:], axis=AX.X, op=ALU.add))
                seq.step("vector", lambda v: v.tensor_scalar(
                    out=rowv[:, 1:2], in0=rowv[:, 1:2], scalar1=1.0 / D,
                    scalar2=None, op0=ALU.mult))
                seq.step("scalar", lambda s: s.activation(
                    out=rowv[:, 2:3], in_=rowv[:, 0:1], func=AF.Square))
                seq.step("vector", lambda v: v.tensor_sub(
                    out=rowv[:, 1:2], in0=rowv[:, 1:2], in1=rowv[:, 2:3]))
                seq.step("vector", lambda v: v.tensor_scalar(
                    out=rowv[:, 1:2], in0=rowv[:, 1:2], scalar1=LNEPS,
                    scalar2=None, op0=ALU.add))
                seq.step("scalar", lambda s: s.activation(
                    out=rowv[:, 1:2], in_=rowv[:, 1:2], func=AF.Sqrt))
                seq.step("vector", lambda v: v.reciprocal(out=rowv[:, 1:2],
                                                          in_=rowv[:, 1:2]))
                seq.step("vector", lambda v: v.tensor_scalar(
                    out=z1[:], in0=z1[:], scalar1=rowv[:, 0:1], scalar2=None,
                    op0=ALU.subtract))
                seq.step("vector", lambda v: v.tensor_scalar(
                    out=z1[:], in0=z1[:], scalar1=rowv[:, 1:2], scalar2=None,
                    op0=ALU.mult))
                for dc in range(8):
                    seq.step("tensor", lambda t, dc=dc: t.transpose(
                        psF[:, 0:128], z1[:, dc * 128:(dc + 1) * 128], identf[:]))
                    seq.step("vector", lambda v, dc=dc: v.tensor_copy(
                        out=sqt[:, 0:128] if False else outtm[:, 0:128],
                        in_=psF[:, 0:128]))
                    seq.step("sync", lambda s, dc=dc, ti=ti: s.dma_start(
                        out=x3f_s_d[dc * 128:(dc + 1) * 128,
                                    ti * 128:(ti + 1) * 128],
                        in_=outtm[:, 0:128]), n_dma=1)
            finish(seq, "semF")

        with ExitStack() as ctx:
            E = ctx.enter_context
            G = 512
            NG = TPC // G
            FC = 16
            w1s = E(nc.sbuf_tensor("G_w1", [128, 8 * 2048], MMDT))
            w3s = E(nc.sbuf_tensor("G_w3", [128, 8 * 2048], MMDT))
            w2s = E(nc.sbuf_tensor("G_w2", [128, 16 * 1024], MMDT))
            gwt_s = E(nc.sbuf_tensor("G_gw", [128, 8 * 8], MMDT))
            gbrow = E(nc.sbuf_tensor("G_gb", [1, 8], F32))
            gb_b = E(nc.sbuf_tensor("G_gbb", [128, 8], F32))
            iota8 = E(nc.sbuf_tensor("G_i8", [128, 8], F32))
            ident = E(nc.sbuf_tensor("G_id", [128, 128], F32))
            identm = E(nc.sbuf_tensor("G_idm", [128, 128], MMDT))
            ones1 = E(nc.sbuf_tensor("G_o1", [1, 128], F32))
            onescol = E(nc.sbuf_tensor("G_oc", [128, 1], F32))
            x3mg = E(nc.sbuf_tensor("G_x3m", [128, 8 * G], MMDT))
            hm = E(nc.sbuf_tensor("G_hm", [128, FC * G], MMDT))
            hsil = E(nc.sbuf_tensor("G_hs", [128, G], F32))
            ydf = E(nc.sbuf_tensor("G_ydf", [128, 8 * G], F32))
            lgt = E(nc.sbuf_tensor("G_lg", [128, 8], F32))
            sct = E(nc.sbuf_tensor("G_sct", [128, 8], F32))
            bia = E(nc.sbuf_tensor("G_bia", [128, 8], F32))
            m8 = E(nc.sbuf_tensor("G_m8", [128, 8], F32))
            m8i = E(nc.sbuf_tensor("G_m8i", [128, 8], U32))
            idxf = E(nc.sbuf_tensor("G_idx", [128, 2], F32))
            eqt = E(nc.sbuf_tensor("G_eq", [128, 8], F32))
            comb = E(nc.sbuf_tensor("G_cb", [128, 8], F32))
            combm = E(nc.sbuf_tensor("G_cbm", [128, 8], MMDT))
            combT = E(nc.sbuf_tensor("G_cbT", [8, TPC], F32))
            combrow = E(nc.sbuf_tensor("G_cbr", [1, G], F32))
            comb_eb = E(nc.sbuf_tensor("G_ceb", [128, G], F32))
            rowsum = E(nc.sbuf_tensor("G_rs", [128, 1], F32))
            x3f = E(nc.sbuf_tensor("G_x3f", [128, G], F32))
            z1 = E(nc.sbuf_tensor("G_z1", [128, 8 * G], F32))
            sqt = E(nc.sbuf_tensor("G_sq", [128, G], F32))
            rowt = E(nc.sbuf_tensor("G_rt", [1, 6 * G], F32))
            mub = E(nc.sbuf_tensor("G_mub", [128, G], F32))
            rsb = E(nc.sbuf_tensor("G_rsb", [128, G], F32))
            otm = E(nc.sbuf_tensor("G_otm", [128, D], F32))
            psh = E(nc.psum_tensor("G_psh", [128, G], F32))
            psy = E(nc.psum_tensor("G_psy", [128, G], F32))
            psg = E(nc.psum_tensor("G_psg", [128, 8], F32))
            pst = E(nc.psum_tensor("G_pst", [128, G], F32))
            psb = E(nc.psum_tensor("G_psb", [128, G], F32))
            psrow = E(nc.psum_tensor("G_psr", [1, G], F32))
            pstm = E(nc.psum_tensor("G_ptm", [128, 128], MMDT))
            seq = segment()
            seq.step("vector", lambda v: v.memset(ones1[:], 1.0))
            seq.step("vector", lambda v: v.memset(onescol[:], 1.0))
            seq.step("gpsimd", lambda g: g.iota(
                iota8[:], pattern=[[1, 8]], base=0, channel_multiplier=0,
                allow_small_or_imprecise_dtypes=True))
            seq.step("sync", lambda s: [s.dma_start(out=ident[:], in_=ident_d),
                                        s.dma_start(out=gbrow[:], in_=gb_d)],
                     n_dma=2)
            seq.step("vector", lambda v: v.tensor_copy(out=identm[:],
                                                       in_=ident[:]))
            seq.step("gpsimd", lambda g: [
                g.dma_start(out=gwt_s[:, c * 8:(c + 1) * 8],
                            in_=gwt_d[c * 128:(c + 1) * 128, :])
                for c in range(8)], n_dma=8)
            seq.step("tensor", lambda t: t.matmul(psg[:], ones1[:], gbrow[:],
                                                  start=True, stop=True))
            seq.step("vector", lambda v: v.tensor_copy(out=gb_b[:], in_=psg[:]))
            for sub in range(TPC // 128):
                seq.step("gpsimd", lambda g, sub=sub: [
                    g.dma_start(out=x3mg[:, c * G:c * G + 128],
                                in_=x3f_s_d[c * 128:(c + 1) * 128,
                                            sub * 128:(sub + 1) * 128])
                    for c in range(8)], n_dma=8)

                def mmg(t, sub=sub):
                    last = None
                    for c in range(8):
                        last = t.matmul(psg[:], x3mg[:, c * G:c * G + 128],
                                        gwt_s[:, c * 8:(c + 1) * 8],
                                        start=(c == 0), stop=(c == 7))
                    return last
                seq.step("tensor", mmg)
                seq.step("vector", lambda v: v.tensor_copy(out=lgt[:], in_=psg[:]))
                seq.step("vector", lambda v: v.tensor_reduce(
                    out=rowsum[:], in_=lgt[:], axis=AX.X, op=ALU.max))
                seq.step("vector", lambda v: v.tensor_scalar(
                    out=lgt[:], in0=lgt[:], scalar1=rowsum[:, 0:1], scalar2=None,
                    op0=ALU.subtract))
                seq.step("scalar", lambda s: s.activation(out=sct[:], in_=lgt[:],
                                                          func=AF.Exp))
                seq.step("vector", lambda v: v.tensor_reduce(
                    out=rowsum[:], in_=sct[:], axis=AX.X, op=ALU.add))
                seq.step("vector", lambda v: v.reciprocal(out=rowsum[:],
                                                          in_=rowsum[:]))
                seq.step("vector", lambda v: v.tensor_scalar(
                    out=sct[:], in0=sct[:], scalar1=rowsum[:, 0:1], scalar2=None,
                    op0=ALU.mult))
                seq.step("vector", lambda v: v.tensor_add(out=bia[:], in0=sct[:],
                                                          in1=gb_b[:]))
                seq.step("vector", lambda v: v.max(out=m8[:], in_=bia[:]))
                seq.step("vector", lambda v: v.max_index(out=m8i[:], in_max=m8[:],
                                                         in_values=bia[:]))
                seq.step("vector", lambda v: v.tensor_copy(out=idxf[:],
                                                           in_=m8i[:, 0:2]))
                seq.step("vector", lambda v: v.memset(comb[:], 0.0))
                for rr in range(2):
                    seq.step("vector", lambda v, rr=rr: v.tensor_scalar(
                        out=eqt[:], in0=iota8[:], scalar1=idxf[:, rr:rr + 1],
                        scalar2=None, op0=ALU.is_equal))
                    seq.step("vector", lambda v: v.tensor_tensor(
                        out=eqt[:], in0=eqt[:], in1=sct[:], op=ALU.mult))
                    seq.step("vector", lambda v: v.tensor_add(
                        out=comb[:], in0=comb[:], in1=eqt[:]))
                seq.step("vector", lambda v: v.tensor_copy(out=combm[:],
                                                           in_=comb[:]))
                seq.step("tensor", lambda t: t.transpose(
                    pstm[0:8, 0:128], combm[:], identm[:]))
                seq.step("vector", lambda v, sub=sub: v.tensor_copy(
                    out=combT[0:8, sub * 128:(sub + 1) * 128],
                    in_=pstm[0:8, 0:128]))
            seq.step("sync", lambda s: s.dma_start(out=combdram,
                                                   in_=combT[0:8, :]), n_dma=1)

            def expert_pass(widx, scaled):
                if widx == "shared":
                    srcs = (sw1t_d, sw3t_d, sw2t_d)
                    first = False
                else:
                    srcs = (mw1t_d[widx], mw3t_d[widx], mw2t_d[widx])
                    first = (widx == 0)
                seq.step("gpsimd", lambda g, srcs=srcs: [
                    g.dma_start(out=w1s[:, c * 2048:(c + 1) * 2048],
                                in_=srcs[0][c * 128:(c + 1) * 128, :])
                    for c in range(8)], n_dma=8)
                seq.step("gpsimd", lambda g, srcs=srcs: [
                    g.dma_start(out=w3s[:, c * 2048:(c + 1) * 2048],
                                in_=srcs[1][c * 128:(c + 1) * 128, :])
                    for c in range(8)], n_dma=8)
                seq.step("gpsimd", lambda g, srcs=srcs: [
                    g.dma_start(out=w2s[:, c * 1024:(c + 1) * 1024],
                                in_=srcs[2][c * 128:(c + 1) * 128, :])
                    for c in range(16)], n_dma=16)
                for gi in range(NG):
                    seq.step("gpsimd", lambda g, gi=gi: [
                        g.dma_start(out=x3mg[:, c * G:(c + 1) * G],
                                    in_=x3f_s_d[c * 128:(c + 1) * 128,
                                                gi * G:(gi + 1) * G])
                        for c in range(8)], n_dma=8)
                    if scaled:
                        seq.step("sync", lambda s, gi=gi, widx=widx: s.dma_start(
                            out=combrow[:], in_=combdram[widx:widx + 1,
                                                         gi * G:(gi + 1) * G]),
                            n_dma=1)
                        seq.step("tensor", lambda t: t.matmul(
                            psb[:], ones1[:], combrow[:], start=True, stop=True))
                        seq.step("vector", lambda v: v.tensor_copy(
                            out=comb_eb[:], in_=psb[:]))
                    for fc in range(FC):
                        def mmh1(t, fc=fc):
                            last = None
                            for c in range(8):
                                last = t.matmul(
                                    psh[:],
                                    w1s[:, c * 2048 + fc * 128: c * 2048 + fc * 128 + 128],
                                    x3mg[:, c * G:(c + 1) * G],
                                    start=(c == 0), stop=(c == 7))
                            return last
                        seq.step("tensor", mmh1)
                        seq.step("scalar", lambda s: s.activation(
                            out=hsil[:], in_=psh[:], func=AF.Silu))

                        def mmh3(t, fc=fc):
                            last = None
                            for c in range(8):
                                last = t.matmul(
                                    psh[:],
                                    w3s[:, c * 2048 + fc * 128: c * 2048 + fc * 128 + 128],
                                    x3mg[:, c * G:(c + 1) * G],
                                    start=(c == 0), stop=(c == 7))
                            return last
                        seq.step("tensor", mmh3)
                        seq.step("vector", lambda v, fc=fc: v.tensor_tensor(
                            out=hm[:, fc * G:(fc + 1) * G], in0=hsil[:],
                            in1=psh[:], op=ALU.mult))
                    for dc in range(8):
                        def mmy(t, dc=dc):
                            last = None
                            for fc in range(FC):
                                last = t.matmul(
                                    psy[:],
                                    w2s[:, fc * 1024 + dc * 128: fc * 1024 + dc * 128 + 128],
                                    hm[:, fc * G:(fc + 1) * G],
                                    start=(fc == 0), stop=(fc == FC - 1))
                            return last
                        seq.step("tensor", mmy)
                        if scaled:
                            seq.step("vector", lambda v, dc=dc: v.tensor_tensor(
                                out=ydf[:, dc * G:(dc + 1) * G], in0=psy[:],
                                in1=comb_eb[:], op=ALU.mult))
                        else:
                            seq.step("vector", lambda v, dc=dc: v.tensor_copy(
                                out=ydf[:, dc * G:(dc + 1) * G], in_=psy[:]))
                    seq.step("gpsimd", lambda g, gi=gi, first=first: [
                        g.dma_start(out=ydram[dc * 128:(dc + 1) * 128,
                                              gi * G:(gi + 1) * G],
                                    in_=ydf[:, dc * G:(dc + 1) * G],
                                    accum_op=(ALU.bypass if first else ALU.add))
                        for dc in range(8)], n_dma=8)

            for e in range(8):
                expert_pass(e, True)
            expert_pass("shared", False)

            for gi in range(NG):
                seq.step("sync", lambda s, gi=gi: [
                    s.dma_start(out=z1[:, c * G:(c + 1) * G],
                                in_=ydram[c * 128:(c + 1) * 128,
                                          gi * G:(gi + 1) * G])
                    for c in range(8)], n_dma=8)
                for c in range(8):
                    seq.step("sync", lambda s, c=c, gi=gi: s.dma_start(
                        out=x3f[:], in_=x3f_s_d[c * 128:(c + 1) * 128,
                                                gi * G:(gi + 1) * G]), n_dma=1)
                    seq.step("vector", lambda v, c=c: v.tensor_add(
                        out=z1[:, c * G:(c + 1) * G],
                        in0=z1[:, c * G:(c + 1) * G], in1=x3f[:]))
                for c in range(8):
                    seq.step("tensor", lambda t, c=c: t.matmul(
                        psrow[:], onescol[:], z1[:, c * G:(c + 1) * G],
                        start=(c == 0), stop=(c == 7)))
                seq.step("vector", lambda v: v.tensor_scalar(
                    out=rowt[0:1, 0:G], in0=psrow[:], scalar1=1.0 / D,
                    scalar2=None, op0=ALU.mult))
                for c in range(8):
                    seq.step("scalar", lambda s, c=c: s.activation(
                        out=sqt[:], in_=z1[:, c * G:(c + 1) * G], func=AF.Square))
                    seq.step("tensor", lambda t, c=c: t.matmul(
                        psrow[:], onescol[:], sqt[:], start=(c == 0),
                        stop=(c == 7)))
                seq.step("vector", lambda v: v.tensor_scalar(
                    out=rowt[0:1, G:2 * G], in0=psrow[:], scalar1=1.0 / D,
                    scalar2=None, op0=ALU.mult))
                seq.step("scalar", lambda s: s.activation(
                    out=rowt[0:1, 2 * G:3 * G], in_=rowt[0:1, 0:G],
                    func=AF.Square))
                seq.step("vector", lambda v: v.tensor_sub(
                    out=rowt[0:1, G:2 * G], in0=rowt[0:1, G:2 * G],
                    in1=rowt[0:1, 2 * G:3 * G]))
                seq.step("vector", lambda v: v.tensor_scalar(
                    out=rowt[0:1, G:2 * G], in0=rowt[0:1, G:2 * G],
                    scalar1=LNEPS, scalar2=None, op0=ALU.add))
                seq.step("scalar", lambda s: s.activation(
                    out=rowt[0:1, 3 * G:4 * G], in_=rowt[0:1, G:2 * G],
                    func=AF.Sqrt))
                seq.step("vector", lambda v: v.reciprocal(
                    out=rowt[0:1, 4 * G:5 * G], in_=rowt[0:1, 3 * G:4 * G]))
                seq.step("tensor", lambda t: t.matmul(
                    psb[:], ones1[:], rowt[0:1, 0:G], start=True, stop=True))
                seq.step("vector", lambda v: v.tensor_copy(out=mub[:],
                                                           in_=psb[:]))
                seq.step("tensor", lambda t: t.matmul(
                    psb[:], ones1[:], rowt[0:1, 4 * G:5 * G],
                    start=True, stop=True))
                seq.step("vector", lambda v: v.tensor_copy(out=rsb[:],
                                                           in_=psb[:]))
                for c in range(8):
                    seq.step("vector", lambda v, c=c: v.tensor_sub(
                        out=z1[:, c * G:(c + 1) * G],
                        in0=z1[:, c * G:(c + 1) * G], in1=mub[:]))
                    seq.step("vector", lambda v, c=c: v.tensor_tensor(
                        out=z1[:, c * G:(c + 1) * G],
                        in0=z1[:, c * G:(c + 1) * G], in1=rsb[:], op=ALU.mult))
                for sub in range(4):
                    for c in range(8):
                        seq.step("tensor", lambda t, sub=sub, c=c: t.transpose(
                            pst[:, 0:128],
                            z1[:, c * G + sub * 128: c * G + (sub + 1) * 128],
                            ident[:]))
                        seq.step("vector", lambda v, c=c: v.tensor_copy(
                            out=otm[:, c * 128:(c + 1) * 128], in_=pst[:, 0:128]))
                    seq.step("sync", lambda s, gi=gi, sub=sub: s.dma_start(
                        out=y_o[gi * G + sub * 128: gi * G + (sub + 1) * 128, :],
                        in_=otm[:]), n_dma=1)
            finish(seq, "semG")

    return nc


def fused_inmaps(ins):
    x = np.asarray(ins["L0_attn_x"], np.float32)
    t_local = np.arange(TPC, dtype=np.float64)
    jj = np.arange(N, dtype=np.float64)
    com = dict(
        wq=np.ascontiguousarray(ins["L0_attn_W_qkv"][:, 0:DI]),
        wk=np.ascontiguousarray(ins["L0_attn_W_qkv"][:, DI:2 * DI]),
        wv=np.ascontiguousarray(ins["L0_attn_W_qkv"][:, 2 * DI:3 * DI]),
        ware=np.ascontiguousarray(ins["L0_attn_W_a"][:, 0::2]),
        waim=np.ascontiguousarray(ins["L0_attn_W_a"][:, 1::2]),
        bq=np.ascontiguousarray(ins["L0_attn_b_qkv"][0:DI].reshape(4, 128).T),
        bk=np.ascontiguousarray(ins["L0_attn_b_qkv"][DI:2 * DI].reshape(4, 128).T),
        bv=np.ascontiguousarray(ins["L0_attn_b_qkv"][2 * DI:3 * DI].reshape(1, DI)),
        bare=np.ascontiguousarray(ins["L0_attn_b_a"][0::2].reshape(4, 128).T),
        baim=np.ascontiguousarray(ins["L0_attn_b_a"][1::2].reshape(4, 128).T),
        gam=np.ascontiguousarray(ins["L0_attn_gamma"].reshape(8, 128).T),
        ident_d=np.eye(128, dtype=np.float32),
        masku_d=(np.arange(128)[:, None] <= np.arange(128)[None, :]).astype(
            np.float32),
        wout0=np.ascontiguousarray(ins["L0_attn_W_out"]),
        bout0=np.ascontiguousarray(ins["L0_attn_b_out"].reshape(8, 128).T),
        f_w1t=np.ascontiguousarray(ins["L0_ffn_W1"].T),
        f_b1=np.ascontiguousarray(ins["L0_ffn_b1"].reshape(32, 128).T),
        f_w2t=np.ascontiguousarray(ins["L0_ffn_W2"].T),
        f_b2=np.ascontiguousarray(ins["L0_ffn_b2"].reshape(8, 128).T),
        wdkv=np.ascontiguousarray(ins["L1_attn_W_dkv"]),
        wuk=np.ascontiguousarray(ins["L1_attn_W_uk"]),
        wuv=np.ascontiguousarray(ins["L1_attn_W_uv"]),
        wdq=np.ascontiguousarray(ins["L1_attn_W_dq"]),
        wuq=np.ascontiguousarray(ins["L1_attn_W_uq"]),
        wqr=np.ascontiguousarray(ins["L1_attn_W_qr"]),
        wkr=np.ascontiguousarray(ins["L1_attn_W_kr"]),
        wiq=np.ascontiguousarray(ins["L1_attn_idx_wq"]),
        wik=np.ascontiguousarray(ins["L1_attn_idx_wk"]),
        idxw=np.ascontiguousarray(ins["L1_attn_idx_w"].reshape(1, 4)),
        rdel=np.ascontiguousarray(ins["L1_attn_raw_delta"].reshape(1, RHALF)),
        cosj=np.cos((jj[:, None] + 63.0) * THETA[None, :]).astype(np.float32),
        sinj=np.sin((jj[:, None] + 63.0) * THETA[None, :]).astype(np.float32),
        wout1=np.ascontiguousarray(ins["L1_attn_W_out"]),
        cosr=np.cos((64 + np.arange(32, dtype=np.float64))[:, None]
                    * THETA[None, :]).astype(np.float32).reshape(1, 512),
        sinr=np.sin((64 + np.arange(32, dtype=np.float64))[:, None]
                    * THETA[None, :]).astype(np.float32).reshape(1, 512),
        cosjj=np.cos(np.arange(128, dtype=np.float64)[:, None]
                     * THETA[None, :]).astype(np.float32),
        sinjj=np.sin(np.arange(128, dtype=np.float64)[:, None]
                     * THETA[None, :]).astype(np.float32),
        gwt=np.ascontiguousarray(ins["L1_ffn_gate_w"].T),
        gb=np.ascontiguousarray(ins["L1_ffn_gate_bias"].reshape(1, 8)),
        mw1t=np.ascontiguousarray(ins["L1_ffn_w1"].transpose(0, 2, 1)),
        mw3t=np.ascontiguousarray(ins["L1_ffn_w3"].transpose(0, 2, 1)),
        mw2t=np.ascontiguousarray(ins["L1_ffn_w2"].transpose(0, 2, 1)),
        sw1t=np.ascontiguousarray(ins["L1_ffn_sw1"].T),
        sw3t=np.ascontiguousarray(ins["L1_ffn_sw3"].T),
        sw2t=np.ascontiguousarray(ins["L1_ffn_sw2"].T),
    )
    maps = []
    for c in range(NCORES):
        b, half = c // 2, c % 2
        tt = t_local + half * TPC
        m = dict(com)
        m["xb"] = np.ascontiguousarray(x[b].reshape(N, D))
        m["tcolin"] = tt.reshape(TPC, 1).astype(np.float32)
        m["cos2"] = np.cos(2 * tt[:, None] * THETA[None, :]).astype(np.float32)
        m["sin2"] = np.sin(2 * tt[:, None] * THETA[None, :]).astype(np.float32)
        m["cost1"] = np.cos(tt[:, None] * THETA[None, :]).astype(np.float32)
        m["sint1"] = np.sin(tt[:, None] * THETA[None, :]).astype(np.float32)
        maps.append(m)
    return maps


def kernel(**inputs):
    ins = inputs
    maps = fused_inmaps(ins)
    res = run_bass_kernel_spmd(build_fused(), maps,
                               core_ids=list(range(NCORES))).results
    y = np.vstack([res[c]["yout"] for c in range(NCORES)])
    return np.ascontiguousarray(y.reshape(B, N, D).astype(np.float32))


def kernel_unfused(**inputs):
    ins = inputs
    x = np.asarray(ins["L0_attn_x"], np.float32)

    mapsA = stageA_inmaps(x, ins["L0_attn_W_qkv"], ins["L0_attn_b_qkv"],
                          ins["L0_attn_W_a"], ins["L0_attn_b_a"],
                          ins["L0_attn_gamma"])
    resA = run_bass_kernel_spmd(build_stageA(), mapsA,
                                core_ids=list(range(NCORES))).results
    qs = np.stack([np.hstack([resA[2 * b]["qs_fm"], resA[2 * b + 1]["qs_fm"]])
                   for b in range(B)])
    kf = np.stack([np.hstack([resA[2 * b]["k_fm"], resA[2 * b + 1]["k_fm"]])
                   for b in range(B)])
    ls = np.stack([np.hstack([resA[2 * b]["lsig_fm"], resA[2 * b + 1]["lsig_fm"]])
                   for b in range(B)])
    th = np.stack([np.hstack([resA[2 * b]["th_fm"], resA[2 * b + 1]["th_fm"]])
                   for b in range(B)])
    vt = np.stack([np.vstack([resA[2 * b]["v_tm"], resA[2 * b + 1]["v_tm"]])
                   for b in range(B)])

    mapsB = stageB_inmaps(qs, kf, ls, th, vt)
    resB = run_bass_kernel_spmd(build_stageB(), mapsB,
                                core_ids=list(range(NCORES))).results
    units = [(b, h) for b in range(B) for h in range(HEADS)]
    attn = np.zeros((B, N, DI), np.float32)
    for c in range(NCORES):
        au = resB[c]["attn_u"]
        for ui, (b, h) in enumerate(units[c * 4:(c + 1) * 4]):
            attn[b, :, h * 64:(h + 1) * 64] = au[ui]
    attn_tok = np.ascontiguousarray(attn.reshape(B * N, DI))

    mapsC = stageC_inmaps(x, attn_tok, ins["L0_attn_W_out"], ins["L0_attn_b_out"],
                          ins["L0_ffn_W1"], ins["L0_ffn_b1"],
                          ins["L0_ffn_W2"], ins["L0_ffn_b2"])
    resC = run_bass_kernel_spmd(build_stageC(), mapsC,
                                core_ids=list(range(NCORES))).results
    x2c = [resC[c]["x2_fm"] for c in range(NCORES)]

    mapsD = stageD_inmaps(x2c, ins["L1_attn_W_dkv"], ins["L1_attn_W_uk"],
                          ins["L1_attn_W_uv"], ins["L1_attn_W_dq"],
                          ins["L1_attn_W_uq"], ins["L1_attn_W_qr"],
                          ins["L1_attn_W_kr"], ins["L1_attn_idx_wq"],
                          ins["L1_attn_idx_wk"], ins["L1_attn_raw_delta"])
    resD = run_bass_kernel_spmd(build_stageD(0), mapsD,
                                core_ids=list(range(NCORES))).results
    tab_b = [np.vstack([resD[2 * b]["tab"], resD[2 * b + 1]["tab"]])
             for b in range(B)]
    ki_b = [np.hstack([resD[2 * b]["ki_fm"], resD[2 * b + 1]["ki_fm"]])
            for b in range(B)]
    qi_c = [resD[c]["qi_fm"] for c in range(NCORES)]
    qc_c = [resD[c]["qc_tm"] for c in range(NCORES)]
    qr_c = [resD[c]["qr_tm"] for c in range(NCORES)]

    mapsE = stageE_inmaps(qi_c, ki_b, ins["L1_attn_idx_w"])
    sel_c = [None] * NCORES
    for half in (0, 1):
        cores = [c for c in range(NCORES) if c % 2 == half]
        r = run_bass_kernel_spmd(build_stageE(half), [mapsE[c] for c in cores],
                                 core_ids=cores).results
        for i, c in enumerate(cores):
            sel_c[c] = r[i]["sel"]

    mapsF = stageF_inmaps(sel_c, tab_b, qc_c, qr_c, x2c, ins["L1_attn_W_out"],
                          ins["L1_attn_raw_delta"])
    x3c = [None] * NCORES
    for half in (0, 1):
        cores = [c for c in range(NCORES) if c % 2 == half]
        r = run_bass_kernel_spmd(build_stageF(half), [mapsF[c] for c in cores],
                                 core_ids=cores).results
        for i, c in enumerate(cores):
            x3c[c] = r[i]["x3_fm"]

    mapsG = stageG_inmaps(x3c, ins["L1_ffn_gate_w"], ins["L1_ffn_gate_bias"],
                          ins["L1_ffn_w1"], ins["L1_ffn_w2"], ins["L1_ffn_w3"],
                          ins["L1_ffn_sw1"], ins["L1_ffn_sw2"], ins["L1_ffn_sw3"])
    resG = run_bass_kernel_spmd(build_stageG(), mapsG,
                                core_ids=list(range(NCORES))).results
    y = np.vstack([resG[c]["yout"] for c in range(NCORES)])
    return np.ascontiguousarray(y.reshape(B, N, D).astype(np.float32))
